# version 08
# baseline (speedup 1.0000x reference)
import numpy as np
from contextlib import ExitStack

import concourse.bass as bass
import concourse.mybir as mybir
from concourse.bass_utils import run_bass_kernel_spmd

F32 = mybir.dt.float32
AF = mybir.ActivationFunctionType
ALU = mybir.AluOpType

T = 2048
D = 1024
NCH = 8
NTT = 16


class Sem:
    __slots__ = ("h", "n", "name")

    def __init__(self, h, name):
        self.h, self.n, self.name = h, 0, name

    def bump(self, k=1):
        self.n += k
        return self.n


class Prog:
    ENGS = ("tensor", "vector", "scalar", "gpsimd", "sync")

    def __init__(self, nc, es):
        self.nc = nc
        self.es = es
        self.stream = {e: [] for e in self.ENGS}
        self.waited = {e: {} for e in self.ENGS}
        self._semidx = 0
        self.sems = []
        self.paranoid = False

    def sem(self, name=None):
        name = name or f"s{self._semidx}"
        self._semidx += 1
        h = self.es.enter_context(self.nc.semaphore(name))
        s = Sem(h, name)
        self.sems.append(s)
        return s

    def emit(self, eng, fn, waits=(), inc=None, inc_k=1):
        if self.paranoid:
            if not hasattr(self, "tick"):
                self.tick = {e: self.sem(f"tick_{e}") for e in self.ENGS}
            waits = (list(waits)
                     + [(s, s.n) for s in self.sems
                        if s.n > 0 and not s.name.startswith("tick_")]
                     + [(self.tick[e2], self.tick[e2].n)
                        for e2 in self.ENGS if e2 != eng])
        wf = []
        for wv in waits:
            if wv is None:
                continue
            s, v = wv
            if v is None:
                continue
            if self.waited[eng].get(s.name, -1) < v:
                self.waited[eng][s.name] = v
                wf.append((s.h, v))
        tgt = inc.bump(inc_k) if inc is not None else None

        def closure(e, wf=wf, fn=fn, inc=inc, inc_k=inc_k):
            for (h, v) in wf:
                e.wait_ge(h, v)
            inst = fn(e)
            if inc is not None:
                inst.then_inc(inc.h, inc_k)
        self.stream[eng].append(closure)
        if self.paranoid:
            tk = self.tick[eng]
            v = tk.bump()
            self.stream[eng].append(lambda e, h=tk.h: e.sem_inc(h, 1))
        return tgt

    def wait(self, eng, waits):
        for wv in waits:
            if wv is None:
                continue
            s, v = wv
            if v is None:
                continue
            if self.waited[eng].get(s.name, -1) < v:
                self.waited[eng][s.name] = v
                self.stream[eng].append(lambda e, h=s.h, v=v: e.wait_ge(h, v))

    def dma(self, eng, out_ap, in_ap, sem, waits=(), **kw):
        return self.emit(eng, lambda e: e.dma_start(out_ap, in_ap, **kw),
                         waits=waits, inc=sem, inc_k=16)


class Rot:

    def __init__(self, bufs):
        self.bufs = list(bufs)
        self.n = len(self.bufs)
        self.free = [None] * self.n

    def slot(self, i):
        return self.bufs[i % self.n], ([self.free[i % self.n]]
                                       if self.free[i % self.n] else [])

    def release(self, i, sem, val):
        self.free[i % self.n] = (sem, val)


class Builder:
    def __init__(self, dbg=(), phase_limit=99, paranoid=False):
        self.dbg = set(dbg)
        self.phase_limit = phase_limit
        nc = self.nc = bass.Bass("TRN2", target_bir_lowering=False)
        es = self.es = ExitStack()
        self.P = Prog(nc, es)
        self.P.paranoid = paranoid

        ein = lambda n, s: nc.dram_tensor(n, s, F32, kind="ExternalInput").ap()
        self.x_tm = ein("x_tm", [NTT, 128, D])
        self.ident_d = ein("ident", [128, 128])
        self.w_in_d = ein("l0a_w_in", [D, 2 * D])
        self.gates_w_d = ein("l0a_gates_w", [4, D, D])
        self.w_out_d = ein("l0a_w_out", [D, D])
        self.smallv_d = ein("l0a_smallv", [16, 128, NCH])
        self.convk_d = ein("l0a_convk", [128, NCH, 5])
        self.b_in_d = ein("l0a_b_in", [128, 16])
        self.wq_d = ein("l0f_wq", [D, D])
        self.wk_d = ein("l0f_wk", [D, D])
        self.wv_d = ein("l0f_wv", [D, D])
        self.wo_d = ein("l0f_wo", [D, D])
        self.wp_d = ein("l0f_wp", [D, D])
        self.selw_d = ein("l0f_selw", [D, 32])
        self.keys_d = ein("l0f_keys", [32, D, 128])
        self.values_d = ein("l0f_values", [32, 128, D])
        self.l1wq_d = ein("l1_wq", [D, D])
        self.l1wk_d = ein("l1_wk", [D, D])
        self.l1wv_d = ein("l1_wv", [D, D])
        self.l1wo_d = ein("l1_wo", [D, D])
        self.l1w1t_d = ein("l1_w1t", [D, 2048])
        self.l1b1_d = ein("l1_b1", [1, 2048])
        self.l1w2t_d = ein("l1_w2t", [2048, D])
        self.mask_d = ein("l1_mask", [128, 896])
        self.ones_d = ein("ones_r", [1, 128])

        def dram(name, shape):
            k = "ExternalOutput" if name in self.dbg else "Internal"
            return nc.dram_tensor(name, shape, F32, kind=k).ap()
        self.XN = dram("XN", [NCH, 128, T])
        self.RES = dram("RES", [NCH, 128, T])
        self.XQ1 = dram("XQ1", [NCH, 128, T])
        self.XPROJ = dram("XPROJ", [NCH, 128, T])
        self.SIGZ = dram("SIGZ", [NCH, 128, T])
        self.YGD = dram("YGD", [NCH, 128, T])
        self.Q0 = dram("Q0", [NCH, 128, T])
        self.K0 = dram("K0", [NCH, 128, T])
        self.V0 = dram("V0", [NCH, 128, T])
        self.PREV = dram("PREV", [NCH, 128, 4095])
        self.BDW = nc.dram_tensor("BDW", [2 * 278400], F32, kind="Internal")
        self.A0TM = dram("A0TM", [NTT, 128, D])
        self.SRC2 = dram("SRC2", [NCH, 128, T])
        self.X2D = dram("X2D", [NCH, 128, T])
        self.XN1 = dram("XN1", [NCH, 128, T])
        self.Q1 = dram("Q1", [NCH, 128, T])
        self.K1 = dram("K1", [NCH, 128, T])
        self.V1 = dram("V1", [NCH, 128, T])
        self.Q1R = dram("Q1R", [NCH, 128, T])
        self.K1R = dram("K1R", [NCH, 128, T])
        self.A1TM = dram("A1TM", [NTT, 128, D])
        self.MSKT = dram("MSKT", [NTT, 16, 128, 128])
        self.XN2D = dram("XN2D", [NCH, 128, T]) if "XN2D" in self.dbg else None
        self.warot = Rot([0, 1])
        self.pos_free = None
        self.pt_guard = None
        self.hg_guard = None
        self.dram = dram

        self.out_d = nc.dram_tensor("out", [NTT, 128, D], F32,
                                    kind="ExternalOutput").ap()

        sb = self.sb = lambda n, s: es.enter_context(nc.sbuf_tensor(n, s, F32))
        self.ident = sb("identS", [128, 128])
        self.smallv = sb("smallvS", [128, 16 * NCH])
        self.convk = sb("convkS", [128, NCH * 5])
        self.b_in_s = sb("binS", [128, 16])
        self.XA = [sb(f"xa{i}", [128, T]) for i in range(NCH)]
        self.ST = [sb(f"st{i}", [128, T]) for i in range(8)]
        self.WA = sb("wa", [128, 8192])
        self.PR = sb("pr", [128, 4160])
        self.STG = [sb(f"stg{i}", [128, 1024]) for i in range(4)]
        self.TW = [sb(f"tw{i}", [128, 1024]) for i in range(2)]
        self.bn6 = sb("bn6", [128, 12])
        self.stat = sb("stat", [128, 4])
        self.vsm = sb("vsm", [128, 16])
        self.selw_s = sb("selwS", [128, 256])
        self.ones_sb = sb("onesS", [1, 128])
        self.gate_sb = sb("gateS", [128, 512])
        self.ki32 = es.enter_context(
            nc.sbuf_tensor("ki32S", [128, 512], mybir.dt.int32))
        self.epsc = sb("epscS", [128, 1])
        self.PS = [es.enter_context(nc.psum_tensor(f"ps{i}", [128, 512], F32))
                   for i in range(8)]
        self.bank_free = [None] * 8

        P = self.P
        self.SLD = P.sem("SLD")
        self.GST = P.sem("GST")
        self.MM = P.sem("MM")
        self.EV = P.sem("EV")
        self.ACS = P.sem("ACS")
        self.GLD = P.sem("GLD")
        self.VST = P.sem("VST")
        self.GC = P.sem("GC")

        self.SVI = {n: i for i, n in enumerate(
            ["ln_g", "ln_b", "g1", "b1", "a0ba", "b0bb", "c0bc", "d0bd",
             "b_out", "conv_b", "g2", "b2", "sv12", "sv13", "sv14", "sv15"])}

        self.t_ident = P.dma("sync", self.ident[:], self.ident_d, self.SLD)
        self.t_smallv = P.dma("sync", self.smallv[:],
                              self.smallv_d.rearrange("v p c -> p v c"), self.SLD)
        P.dma("sync", self.convk[:], self.convk_d, self.SLD)
        self.t_consts = P.dma("sync", self.b_in_s[:], self.b_in_d, self.SLD)
        self.t_ones = P.dma("sync", self.ones_sb[:], self.ones_d, self.SLD)
        P.emit("vector", lambda e: e.memset(self.vsm[:], 0.0))
        P.emit("vector", lambda e: e.memset(self.epsc[:], 1e-5))
        P.emit("vector", lambda e: e.memset(self.stat[:], 0.0))
        P.emit("vector", lambda e: e.memset(self.bn6[:], 0.0))

    def spacers(self, n=4, col=4):
        self.P.emit("vector", lambda e: e.drain())

    def svec(self, name, ch):
        i = self.SVI[name]
        return self.smallv[:, i * NCH + ch: i * NCH + ch + 1]

    def bankw(self, i):
        f = self.bank_free[i]
        return [f] if f else []

    def emit_ln_stats_apply(self, x_ap, out_ap, rdy, extra=()):
        P, EV, ACS, ALUx = self.P, self.EV, self.ACS, ALU
        stat, bn6 = self.stat, self.bn6
        P.emit("vector", lambda e: e.bn_stats(bn6[:, 0:6], x_ap[:, 0:512]),
               waits=list(rdy) + list(extra))
        P.emit("vector", lambda e: e.bn_stats(bn6[:, 6:12], x_ap[:, 512:1024]))
        P.emit("vector", lambda e: e.drain())
        t_ag = P.emit("vector", lambda e: e.bn_aggr(
            stat[:, 0:2], bn6[:].rearrange("p (a b) -> p a b", b=3)), inc=EV)
        t_ve = P.emit("scalar", lambda e: e.activation(
            stat[:, 2:3], stat[:, 1:2], AF.Identity, bias=self.epsc[:]),
            waits=[(EV, t_ag)], inc=ACS)
        t_rc = P.emit("vector", lambda e: e.reciprocal(
            stat[:, 3:4], stat[:, 2:3]), waits=[(ACS, t_ve)], inc=EV)
        t_sq = P.emit("scalar", lambda e: e.activation(
            self.vsm[:, 1:2], stat[:, 3:4], AF.Sqrt),
            waits=[(EV, t_rc)], inc=ACS)
        P.wait("vector", [(ACS, t_sq)])
        return P.emit("vector", lambda e: e.tensor_scalar(
            out_ap, x_ap, stat[:, 0:1], self.vsm[:, 1:2],
            op0=ALUx.subtract, op1=ALUx.mult), inc=EV)

    def emit_transpose_1024(self, src_tile, rdy, banks, tt):
        P, MM = self.P, self.MM
        t_mm = None
        for j in range(NCH):
            bi = banks[j // 4]
            ps = self.PS[bi][:, (j % 4) * 128:(j % 4) * 128 + 128]
            w = self.bankw(bi) if j % 4 == 0 else []
            if tt == 0 and j == 0:
                w = w + [(self.SLD, self.t_ident)]
            t_mm = P.emit("tensor", lambda e, ps=ps, j=j:
                          e.transpose(ps, src_tile[:, j * 128:(j + 1) * 128],
                                      self.ident[:]),
                          waits=list(rdy) + w, inc=MM)
        return t_mm

    def phase_A(self):
        B, P = self, self.P
        SLD, GST, MM, EV, ACS = B.SLD, B.GST, B.MM, B.EV, B.ACS
        xt_r = Rot(B.TW[:2])
        xn_r = Rot([B.ST[0][:, 0:1024], B.ST[1][:, 0:1024]])
        nr2_r = Rot([B.ST[2][:, 0:1024], B.ST[3][:, 0:1024]])
        res_r = Rot([B.ST[4][:, 0:1024], B.ST[5][:, 0:1024]])
        stg_r = Rot(B.STG)

        LNDBG = B.dram("LNDBG", [NTT, 128, 8]) if "LNDBG" in B.dbg else None
        XNTM = B.dram("XNTM", [NTT, 128, 1024]) if "XNTM" in B.dbg else None
        for tt in range(NTT):
            cs = slice(tt * 128, tt * 128 + 128)
            xt, w = xt_r.slot(tt)
            t_ld = P.dma("sync", xt[:], B.x_tm[tt], SLD, waits=w)
            xn_t, w2 = xn_r.slot(tt)
            t_xn = B.emit_ln_stats_apply(xt[:], xn_t, [(SLD, t_ld)] + w2)
            if LNDBG is not None:
                P.dma("gpsimd", LNDBG[tt, :, 0:4], B.stat[:], GST,
                      waits=[(EV, t_xn)])
            if XNTM is not None:
                P.dma("gpsimd", XNTM[tt], xn_t, GST, waits=[(EV, t_xn)])
            res_t, w3 = res_r.slot(tt)
            t_res = P.emit("vector", lambda e, xt=xt, xn_t=xn_t, res_t=res_t:
                           e.scalar_tensor_tensor(res_t, xn_t, 1.0, xt[:],
                                                  op0=ALU.mult, op1=ALU.add),
                           waits=w3, inc=EV)
            xt_r.release(tt, EV, t_res)
            nr2_t, w4 = nr2_r.slot(tt)
            t_nr2 = B.emit_ln_stats_apply(xn_t, nr2_t, w4)

            t_mm = B.emit_transpose_1024(xn_t, [(EV, t_nr2)], (0, 1), tt)
            xn_r.release(tt, MM, t_mm)
            stg, ws = stg_r.slot(tt * 4 + 0)
            t_e = None
            for half in range(2):
                t_e = P.emit("scalar", lambda e, stg=stg, half=half:
                             e.copy(stg[:, half * 512:half * 512 + 512],
                                    B.PS[half][:]),
                             waits=[(MM, t_mm)] + ws + B.bankw(half), inc=ACS)
                self.bank_free[half] = (ACS, t_e)
            t_st = P.dma("gpsimd", B.XN[:, :, cs].rearrange("c p q -> p c q"),
                         stg[:], GST, waits=[(ACS, t_e)])
            stg_r.release(tt * 4 + 0, GST, t_st)

            t_mm = B.emit_transpose_1024(res_t, [(EV, t_res)], (2, 3), tt)
            res_r.release(tt, MM, t_mm)
            stg, ws = stg_r.slot(tt * 4 + 1)
            for half in range(2):
                t_e = P.emit("scalar", lambda e, stg=stg, half=half:
                             e.copy(stg[:, half * 512:half * 512 + 512],
                                    B.PS[2 + half][:]),
                             waits=[(MM, t_mm)] + ws + B.bankw(2 + half), inc=ACS)
                self.bank_free[2 + half] = (ACS, t_e)
            t_st = P.dma("gpsimd", B.RES[:, :, cs].rearrange("c p q -> p c q"),
                         stg[:], GST, waits=[(ACS, t_e)])
            stg_r.release(tt * 4 + 1, GST, t_st)

            t_mm = B.emit_transpose_1024(nr2_t, [(EV, t_nr2)], (4, 5), tt)
            nr2_r.release(tt, MM, t_mm)
            stg, ws = stg_r.slot(tt * 4 + 2)
            P.wait("vector", ws + [(SLD, B.t_smallv)])
            t_e1 = t_e2 = None
            for j in range(NCH):
                bi = 4 + j // 4
                psb = B.PS[bi][:, (j % 4) * 128:(j % 4) * 128 + 128]
                t_e1 = P.emit("vector", lambda e, j=j, psb=psb, cs=cs:
                              e.tensor_scalar(B.XA[j][:, cs], psb,
                                              B.svec("ln_g", j), B.svec("ln_b", j),
                                              op0=ALU.mult, op1=ALU.add),
                              waits=[(MM, t_mm)], inc=EV)
                t_e2 = P.emit("vector", lambda e, stg=stg, j=j, psb=psb:
                              e.tensor_scalar(stg[:, j * 128:(j + 1) * 128], psb,
                                              B.svec("g1", j), B.svec("b1", j),
                                              op0=ALU.mult, op1=ALU.add),
                              inc=EV)
            for bi in (4, 5):
                self.bank_free[bi] = (EV, t_e2)
            t_st = P.dma("gpsimd", B.XQ1[:, :, cs].rearrange("c p q -> p c q"),
                         stg[:], GST, waits=[(EV, t_e2)])
            stg_r.release(tt * 4 + 2, GST, t_st)
            if B.XN2D is not None:
                P.dma("gpsimd", B.XN2D[:, :, cs].rearrange("c p q -> p c q"),
                      stg[:], GST)
        self.xa_ready = EV.n
        self.t_phaseA_st = GST.n

    def gemm(self, nm, nn, nk, lhsT_fn, rhs_fn, evac_fn, banks=(0, 1, 2, 3),
             nfree=512):
        P, MM = self.P, self.MM
        for m in range(nm):
            for n in range(nn):
                bi = banks[(m * nn + n) % len(banks)]
                ps = self.PS[bi][:, 0:nfree]
                t_mm = None
                for k in range(nk):
                    la, lw = lhsT_fn(m, k)
                    ra, rw = rhs_fn(k, n)
                    w = list(lw) + list(rw) + (self.bankw(bi) if k == 0 else [])
                    t_mm = P.emit(
                        "tensor",
                        lambda e, ps=ps, la=la, ra=ra, s=(k == 0), t=(k == nk - 1):
                        e.matmul(ps, la, ra, start=s, stop=t),
                        waits=w, inc=(MM if k == nk - 1 else None))
                self.bank_free[bi] = evac_fn(m, n, ps, t_mm)

    def phase_C(self):
        B, P = self, self.P
        SLD, GST, MM, EV, ACS = B.SLD, B.GST, B.MM, B.EV, B.ACS
        xa_rdy = [(EV, self.xa_ready)]
        w_in_r = self.w_in_d.rearrange("(kc k) n -> k kc n", k=128)
        stg_r = Rot([B.TW[0][:, 0:512], B.TW[0][:, 512:1024],
                     B.TW[1][:, 0:512], B.TW[1][:, 512:1024]])
        si = 0
        for g in range(4):
            wa_half = B.WA[:, (g % 2) * 4096:(g % 2) * 4096 + 4096]
            _, ww = self.warot.slot(g)
            t_wl = P.dma("sync", wa_half, w_in_r[:, :, g * 512:(g + 1) * 512],
                         SLD, waits=ww)

            def lhsT_fn(m, k, wa_half=wa_half, t_wl=t_wl):
                return (wa_half[:, k * 512 + m * 128: k * 512 + m * 128 + 128],
                        [(SLD, t_wl)])

            def rhs_fn(k, n):
                return (B.XA[k][:, n * 512:(n + 1) * 512], xa_rdy)

            def evac(m, n, ps, t_mm, g=g):
                nonlocal si
                gm = g * 4 + m
                stg, ws = stg_r.slot(si)
                func = AF.Identity if gm < 8 else AF.Sigmoid
                t_e = P.emit("scalar", lambda e, stg=stg, ps=ps, func=func, gm=gm:
                             e.activation(stg, ps, func,
                                          bias=B.b_in_s[:, gm:gm + 1]),
                             waits=[(MM, t_mm)] + ws, inc=ACS)
                dst = (B.XPROJ[gm] if gm < 8 else B.SIGZ[gm - 8])
                t_st = P.dma("gpsimd", dst[:, n * 512:(n + 1) * 512], stg,
                             GST, waits=[(ACS, t_e)])
                stg_r.release(si, GST, t_st)
                si += 1
                return (ACS, t_e)

            self.gemm(4, 4, NCH, lhsT_fn, rhs_fn, evac)
            self.warot.release(g, MM, MM.n)
        self.xa_guard = (MM, MM.n)
        self.t_xproj_done = B.GST.n

    def phase_DE(self):
        B, P = self, self.P
        SLD, GST, MM, EV, ACS = B.SLD, B.GST, B.MM, B.EV, B.ACS
        xp_r = Rot([B.ST[0], B.ST[1]])
        for j in range(NCH):
            xp, w = xp_r.slot(j)
            t_ld = P.dma("sync", xp[:], B.XPROJ[j], SLD,
                         waits=w + [(GST, self.t_xproj_done)])
            k = lambda s, j=j: B.convk[:, j * 5 + s: j * 5 + s + 1]
            wv = [(SLD, t_ld)] + ([self.xa_guard] if j == 0 else [])
            P.emit("vector", lambda e, xp=xp, j=j, k=k: e.tensor_scalar(
                B.XA[j][:], xp[:], k(2), B.svec("conv_b", j),
                op0=ALU.mult, op1=ALU.add), waits=wv)
            for s, (o0, o1, i0, i1) in (
                    (0, (2, 2048, 0, 2046)), (1, (1, 2048, 0, 2047)),
                    (3, (0, 2047, 1, 2048)), (4, (0, 2046, 2, 2048))):
                P.emit("vector", lambda e, xp=xp, j=j, k=k, s=s, o0=o0, o1=o1,
                       i0=i0, i1=i1: e.scalar_tensor_tensor(
                           B.XA[j][:, o0:o1], xp[:, i0:i1], k(s),
                           B.XA[j][:, o0:o1], op0=ALU.mult, op1=ALU.add))
            t_cv = P.emit("vector", lambda e: e.tensor_scalar(
                B.vsm[:, 0:1], B.vsm[:, 0:1], 1.0, None, op0=ALU.mult), inc=EV)
            xp_r.release(j, EV, t_cv)
        self.xconv_ready = (EV, EV.n)

        gw_r = self.gates_w_d
        st_xp, st_sz = B.ST[0], B.ST[1]
        st_g = [B.ST[2], B.ST[3], B.ST[4], B.ST[5]]
        gfuncs = [AF.Tanh, AF.Sigmoid, AF.Sigmoid, AF.Identity]
        gbias = ["a0ba", "b0bb", "c0bc", "d0bd"]
        last_scan = None
        for j in range(NCH):
            wh = self.warot
            wa_half, ww = wh.slot(j)
            wa_half = B.WA[:, (j % 2) * 4096:(j % 2) * 4096 + 4096]
            t_wl = None
            for g in range(4):
                t_wl = P.dma(
                    "sync", wa_half[:, g * 1024:(g + 1) * 1024],
                    gw_r[g].rearrange("(kc k) m -> k kc m", k=128)
                    [:, :, j * 128:(j + 1) * 128], SLD,
                    waits=ww if g == 0 else [])
            wxp = ([(EV, last_scan)] if last_scan else [self.xconv_ready])
            wxp = wxp + [(GST, self.t_xproj_done)]
            t_xp = P.dma("gpsimd", st_xp[:], B.XPROJ[j], B.GLD, waits=wxp)
            t_sz = P.dma("gpsimd", st_sz[:], B.SIGZ[j], B.GLD)

            def lhsT_fn(m, k, wa_half=wa_half, t_wl=t_wl):
                return (wa_half[:, m * 1024 + k * 128: m * 1024 + k * 128 + 128],
                        [(SLD, t_wl)])

            def rhs_fn(k, n):
                return (B.XA[k][:, n * 512:(n + 1) * 512], [self.xconv_ready])

            def evac(m, n, ps, t_mm, j=j):
                t_e = P.emit("scalar", lambda e, m=m, n=n, ps=ps, j=j:
                             e.activation(st_g[m][:, n * 512:(n + 1) * 512], ps,
                                          gfuncs[m],
                                          bias=B.svec(gbias[m], j)),
                             waits=[(MM, t_mm)] + ([(EV, last_scan)]
                                                   if (last_scan and m == 0 and n == 0) else []),
                             inc=ACS)
                return (ACS, t_e)

            self.gemm(4, 4, NCH, lhsT_fn, rhs_fn, evac)
            wh.release(j, MM, MM.n)
            t_gates = (ACS, ACS.n)
            wv = [t_gates, (B.GLD, t_sz)]
            P.emit("vector", lambda e: e.tensor_tensor(
                st_g[1][:], st_g[1][:], st_xp[:], op=ALU.mult), waits=wv)
            P.emit("vector", lambda e: e.tensor_tensor(
                st_g[3][:], st_g[3][:], st_xp[:], op=ALU.mult))
            P.emit("vector", lambda e: e.tensor_tensor_scan(
                st_xp[:], st_g[0][:], st_g[1][:], 0.0,
                op0=ALU.mult, op1=ALU.add))
            P.emit("vector", lambda e: e.tensor_tensor(
                st_g[2][:], st_g[2][:], st_xp[:], op=ALU.mult))
            P.emit("vector", lambda e: e.tensor_tensor(
                st_g[2][:], st_g[2][:], st_g[3][:], op=ALU.add))
            t_yg = P.emit("vector", lambda e: e.tensor_tensor(
                st_g[2][:], st_g[2][:], st_sz[:], op=ALU.mult), inc=EV)
            t_st = P.dma("gpsimd", B.YGD[j], st_g[2][:], GST,
                         waits=[(EV, t_yg)])
            last_scan = t_yg
        self.yg_done = (B.GST, B.GST.n)
        self.xa_guard = (MM, MM.n)

    def phase_F(self):
        B, P = self, self.P
        SLD, GST, MM, EV, ACS = B.SLD, B.GST, B.MM, B.EV, B.ACS
        t_lds = []
        for j in range(NCH):
            t_lds.append(P.dma("sync", B.XA[j][:], B.YGD[j], SLD,
                               waits=[self.xa_guard, self.yg_done]))
        w_out_r = self.w_out_d.rearrange("(kc k) m -> k kc m", k=128)
        res_r = Rot([B.ST[0], B.ST[1]])
        out_r = Rot([B.ST[2], B.ST[3]])
        for g in range(2):
            wa_half, ww = self.warot.slot(g)
            wa_half = B.WA[:, (g % 2) * 4096:(g % 2) * 4096 + 4096]
            t_wl = P.dma("sync", wa_half,
                         w_out_r[:, :, g * 512:(g + 1) * 512], SLD, waits=ww)
            for mj in range(4):
                m = g * 4 + mj
                res_t, wr = res_r.slot(m)
                t_res = P.dma("sync", res_t[:], B.RES[m], SLD,
                              waits=wr + [(GST, self.t_phaseA_st)])
                out_t, wo = out_r.slot(m)

                def lhsT_fn(_m, k, wa_half=wa_half, t_wl=t_wl, mj=mj):
                    return (wa_half[:, k * 512 + mj * 128: k * 512 + mj * 128 + 128],
                            [(SLD, t_wl)])

                def rhs_fn(k, n):
                    return (B.XA[k][:, n * 512:(n + 1) * 512],
                            [(SLD, t_lds[k])])

                def evac(_m, n, ps, t_mm, m=m, res_t=res_t, out_t=out_t,
                         t_res=t_res, wo=wo):
                    t_e = P.emit(
                        "vector", lambda e, n=n, ps=ps:
                        e.scalar_tensor_tensor(
                            out_t[:, n * 512:(n + 1) * 512], ps,
                            B.svec("b_out", m), res_t[:, n * 512:(n + 1) * 512],
                            op0=ALU.add, op1=ALU.add),
                        waits=[(MM, t_mm), (SLD, t_res)] + wo, inc=EV)
                    return (EV, t_e)

                self.gemm(1, 4, NCH, lhsT_fn, rhs_fn, evac)
                t_ev = (EV, EV.n)
                res_r.release(m, EV, t_ev[1])
                t_st = P.dma("gpsimd", B.RES[m], out_t[:], GST, waits=[t_ev])
                out_r.release(m, GST, t_st)
            self.warot.release(g, MM, MM.n)
        self.xa_guard = (MM, MM.n)
        self.res_ssm_done = (GST, GST.n)

    def phase_G(self):
        B, P = self, self.P
        SLD, GST, MM, EV, ACS = B.SLD, B.GST, B.MM, B.EV, B.ACS
        t_lds = []
        for j in range(NCH):
            t_lds.append(P.dma("sync", B.XA[j][:], B.XQ1[j], SLD,
                               waits=[self.xa_guard, (GST, self.t_phaseA_st)]))
        stg_r = Rot([B.TW[0][:, 0:512], B.TW[0][:, 512:1024],
                     B.TW[1][:, 0:512], B.TW[1][:, 512:1024]])
        P.wait("scalar", [(GST, GST.n)])
        si = 0
        for wi, (wd, dst) in enumerate(
                [(self.wq_d, B.Q0), (self.wk_d, B.K0), (self.wv_d, B.V0)]):
            w_r = wd.rearrange("(kc k) m -> k kc m", k=128)
            for g in range(2):
                wa_half = B.WA[:, (g % 2) * 4096:(g % 2) * 4096 + 4096]
                _, ww = self.warot.slot(wi * 2 + g)
                t_wl = P.dma("sync", wa_half,
                             w_r[:, :, g * 512:(g + 1) * 512], SLD, waits=ww)

                def lhsT_fn(m, k, wa_half=wa_half, t_wl=t_wl):
                    return (wa_half[:, k * 512 + m * 128: k * 512 + m * 128 + 128],
                            [(SLD, t_wl)])

                def rhs_fn(k, n):
                    return (B.XA[k][:, n * 512:(n + 1) * 512],
                            [(SLD, t_lds[k])])

                def evac(m, n, ps, t_mm, g=g, dst=dst):
                    nonlocal si
                    gm = g * 4 + m
                    stg, ws = stg_r.slot(si)
                    t_e = P.emit("scalar", lambda e, stg=stg, ps=ps:
                                 e.copy(stg, ps),
                                 waits=[(MM, t_mm)] + ws, inc=ACS)
                    t_st = P.dma("gpsimd", dst[gm][:, n * 512:(n + 1) * 512],
                                 stg, GST, waits=[(ACS, t_e)])
                    stg_r.release(si, GST, t_st)
                    si += 1
                    return (ACS, t_e)

                self.gemm(4, 4, NCH, lhsT_fn, rhs_fn, evac)
                self.warot.release(wi * 2 + g, MM, MM.n)
        self.xa_guard = (MM, MM.n)
        self.qkv_done = (GST, GST.n)

    def phase_H(self):
        B, P = self, self.P
        SLD, GST, MM, EV, ACS, GLD = B.SLD, B.GST, B.MM, B.EV, B.ACS, B.GLD
        for eng in ("gpsimd", "vector", "scalar"):
            P.wait(eng, [self.qkv_done])
        LN1E4 = float(np.log(10000.0))
        TWO_PI = 2.0 * np.pi
        C1, C2, C3 = 6.28125, 0.0019350051879882812, 3.0199402139940485e-07
        _, w0 = self.warot.slot(0)
        _, w1 = self.warot.slot(1)
        wp_r = self.wp_d.rearrange("(kc k) m -> k kc m", k=128)
        t_wp = P.dma("sync", B.WA[:].rearrange("p (k m) -> p k m", m=1024),
                     wp_r, SLD, waits=w0 + w1)
        t_if = None
        for k in range(NCH):
            koff = (k % 4) * 128
            t_i = P.emit("gpsimd", lambda e, k=k, koff=koff: e.iota(
                B.vsm[:, 8 + k:9 + k], pattern=[[1, 1]], base=koff,
                channel_multiplier=1,
                allow_small_or_imprecise_dtypes=True), inc=B.GC)
            t_if = P.emit("scalar", lambda e, k=k: e.activation(
                B.vsm[:, 8 + k:9 + k], B.vsm[:, 8 + k:9 + k], AF.Exp,
                scale=-LN1E4 / 512.0),
                waits=[(B.GC, t_i)], inc=ACS)
        pos_t = B.TW[0][:, 0:512]
        kf_t = B.TW[0][:, 512:1024]
        pe_r = Rot([B.TW[1][:, 0:512], B.TW[1][:, 512:1024]])
        stg_r = Rot([B.STG[0], B.STG[1], B.STG[2], B.STG[3]])
        pi = 0
        for n in range(8):
            w = 512 if n < 7 else 511
            for k in range(NCH):
                pe_t, wpe = pe_r.slot(pi)
                t_i = P.emit("gpsimd", lambda e, n=n, w=w: e.iota(
                    pos_t, pattern=[[1, 512]], base=n * 512 - 2047,
                    channel_multiplier=0,
                    allow_small_or_imprecise_dtypes=True),
                    waits=[(EV, self.pos_free)] if self.pos_free else [],
                    inc=B.GC)
                P.emit("vector", lambda e, k=k, pe_t=pe_t: e.tensor_scalar(
                    pe_t, pos_t, B.vsm[:, 8 + k:9 + k], None, op0=ALU.mult),
                    waits=[(B.GC, t_i), (ACS, t_if)] + wpe)
                P.emit("vector", lambda e, pe_t=pe_t: e.tensor_scalar(
                    kf_t, pe_t, 1.0 / TWO_PI, None, op0=ALU.mult))
                P.emit("vector", lambda e: e.tensor_copy(B.ki32[:], kf_t))
                P.emit("vector", lambda e: e.tensor_copy(kf_t, B.ki32[:]))
                for cc in (C1, C2, C3):
                    P.emit("vector", lambda e, pe_t=pe_t, cc=cc:
                           e.scalar_tensor_tensor(pe_t, kf_t, -cc, pe_t,
                                                  op0=ALU.mult, op1=ALU.add))
                if k >= 4:
                    P.emit("vector", lambda e, pe_t=pe_t: e.tensor_scalar(
                        pe_t, pe_t, float(np.pi / 2), None, op0=ALU.add))
                P.emit("vector", lambda e, pe_t=pe_t: e.tensor_scalar(
                    kf_t, pe_t, float(np.pi), None, op0=ALU.is_gt))
                P.emit("vector", lambda e, pe_t=pe_t: e.scalar_tensor_tensor(
                    pe_t, kf_t, -TWO_PI, pe_t, op0=ALU.mult, op1=ALU.add))
                P.emit("vector", lambda e, pe_t=pe_t: e.tensor_scalar(
                    kf_t, pe_t, float(-np.pi), None, op0=ALU.is_lt))
                t_w = P.emit("vector", lambda e, pe_t=pe_t:
                             e.scalar_tensor_tensor(pe_t, kf_t, TWO_PI, pe_t,
                                                    op0=ALU.mult, op1=ALU.add),
                             inc=EV)
                self.pos_free = t_w
                t_sin = P.emit("scalar", lambda e, pe_t=pe_t: e.activation(
                    pe_t, pe_t, AF.Sin), waits=[(EV, t_w)], inc=ACS)
                for m in range(NCH):
                    bw = self.bankw(m) if k == 0 else []
                    t_mm = P.emit(
                        "tensor",
                        lambda e, m=m, k=k, pe_t=pe_t, w=w:
                        e.matmul(B.PS[m][:, 0:w],
                                 B.WA[:, k * 1024 + m * 128: k * 1024 + m * 128 + 128],
                                 pe_t[:, 0:w], start=(k == 0), stop=(k == 7)),
                        waits=[(ACS, t_sin), (SLD, t_wp)] + bw,
                        inc=(MM if m == NCH - 1 else None))
                pe_r.release(pi, MM, t_mm)
                pi += 1
            for m in range(NCH):
                stg, ws = stg_r.slot(n * 8 + m)
                stg = stg[:, 0:512]
                t_e = P.emit("scalar", lambda e, m=m, stg=stg, w=w:
                             e.copy(stg[:, 0:w], B.PS[m][:, 0:w]),
                             waits=[(MM, t_mm)] + ws, inc=ACS)
                self.bank_free[m] = (ACS, t_e)
                t_st = P.dma("gpsimd", B.PREV[m][:, n * 512:n * 512 + w],
                             stg[:, 0:w], GST, waits=[(ACS, t_e)])
                stg_r.release(n * 8 + m, GST, t_st)
        self.warot.release(0, MM, MM.n)
        self.warot.release(1, MM, MM.n)
        self.prev_done = (GST, GST.n)

    def phase_I(self):
        B, P = self, self.P
        SLD, GST, MM, EV, ACS, GLD = B.SLD, B.GST, B.MM, B.EV, B.ACS, B.GLD
        qc_t, kc_t, vc_t, vtm_t = B.ST[0], B.ST[1], B.ST[2], B.ST[3]
        sc_r = Rot([B.ST[4], B.ST[5]])
        P_t, PT_t = B.ST[6], B.ST[7]
        stg_r = Rot([B.TW[0][:, 0:512], B.TW[0][:, 512:1024],
                     B.TW[1][:, 0:512], B.TW[1][:, 512:1024]])
        bds_bufs = [(B.STG[0], B.STG[1]), (B.STG[2], B.STG[3])]
        bds_free = [None, None]
        bdreg_free = [None, None]
        t_bds_tgt = {}
        pair_guard = None
        st_i = 0
        NIT = NCH * NTT * 2

        def head_of(it):
            c, r = divmod(it, NTT * 2)
            qt, hh = divmod(r, 2)
            return c, qt, hh

        def emit_bd(it, t_q, t_p):
            nonlocal st_i
            c, qt, hh = head_of(it)
            pb = hh * 64
            reg = it % 2
            qsl = qc_t[pb:pb + 64, qt * 128:(qt + 1) * 128]
            c0 = 1920 - qt * 128
            for n in range(5):
                wd = 512 if n < 4 else 127
                bi = 4 + (n % 2)
                t_mm = P.emit(
                    "tensor",
                    lambda e, n=n, wd=wd, bi=bi, qsl=qsl, pb=pb, c0=c0:
                    e.matmul(B.PS[bi][:, 0:wd], qsl,
                             B.PR[pb:pb + 64,
                                  c0 + n * 512: c0 + n * 512 + wd],
                             start=True, stop=True),
                    waits=[(SLD, t_p), (SLD, t_q)] + self.bankw(bi), inc=MM)
                stg, ws = stg_r.slot(st_i)
                t_e = P.emit("scalar",
                             lambda e, stg=stg, wd=wd, bi=bi:
                             e.copy(stg[:, 0:wd], B.PS[bi][:, 0:wd]),
                             waits=[(MM, t_mm)] + ws, inc=ACS)
                self.bank_free[bi] = (ACS, t_e)
                wreg = [bdreg_free[reg]] if (bdreg_free[reg] and n == 0) else []
                t_st = P.dma(
                    "gpsimd",
                    B.BDW.ap()[reg * 278400: reg * 278400 + 278400]
                    .rearrange("(p m) -> p m", m=2175)
                    [:, n * 512:n * 512 + wd],
                    stg[:, 0:wd], GST, waits=[(ACS, t_e)] + wreg)
                stg_r.release(st_i, GST, t_st)
                st_i += 1
            t_bdst = GST.n
            base = B.BDW.ap()[reg * 278400 + 127:
                              reg * 278400 + 127 + 128 * 2174] \
                .rearrange("(p m) -> p m", m=2174)
            pr = it % 2
            ba, bb = bds_bufs[pr]
            wfree = [bds_free[pr]] if bds_free[pr] else []
            P.dma("sync", ba[:], base[:, 0:1024], B.VST,
                  waits=[(GST, t_bdst)] + wfree)
            t_l = P.dma("sync", bb[:], base[:, 1024:2048], B.VST)
            t_bds_tgt[it] = t_l
            bdreg_free[reg] = (B.VST, t_l)

        for c in range(NCH):
            wld = [pair_guard] if pair_guard else []
            wld = wld + [self.qkv_done, self.prev_done]
            t_q = P.dma("sync", qc_t[:], B.Q0[c], SLD, waits=wld)
            t_k = P.dma("sync", kc_t[:], B.K0[c], SLD)
            t_v = P.dma("sync", vc_t[:], B.V0[c], SLD)
            t_p = P.dma("sync", B.PR[:, 0:4095], B.PREV[c], SLD)
            t_vtm_e = None
            for tc in range(NTT):
                bi = 6 + (tc % 2)
                t_mm = P.emit("tensor", lambda e, tc=tc, bi=bi:
                              e.transpose(B.PS[bi][:, 0:128],
                                          vc_t[:, tc * 128:(tc + 1) * 128],
                                          B.ident[:]),
                              waits=[(SLD, t_v)] + self.bankw(bi), inc=MM)
                t_vtm_e = P.emit("scalar", lambda e, tc=tc, bi=bi:
                                 e.copy(vtm_t[:, tc * 128:(tc + 1) * 128],
                                        B.PS[bi][:, 0:128]),
                                 waits=[(MM, t_mm)], inc=ACS)
                self.bank_free[bi] = (ACS, t_vtm_e)
            vtm_rdy = (ACS, t_vtm_e)
            prev_work = None
            emit_bd(c * NTT * 2, t_q, t_p)
            for qt in range(NTT):
                for hh in range(2):
                    h = 2 * c + hh
                    pb = hh * 64
                    it = (c * NTT + qt) * 2 + hh
                    qsl = qc_t[pb:pb + 64, qt * 128:(qt + 1) * 128]
                    sc_t, wsc = sc_r.slot(it)
                    t_ace = None
                    for n in range(4):
                        t_ac = P.emit(
                            "tensor", lambda e, n=n, qsl=qsl, pb=pb:
                            e.matmul(B.PS[n][:, 0:512], qsl,
                                     kc_t[pb:pb + 64, n * 512:(n + 1) * 512],
                                     start=True, stop=True),
                            waits=[(SLD, t_q), (SLD, t_k)] + self.bankw(n),
                            inc=MM)
                        t_ace = P.emit(
                            "scalar", lambda e, n=n, sc_t=sc_t:
                            e.activation(sc_t[:, n * 512:(n + 1) * 512],
                                         B.PS[n][:, 0:512], AF.Identity,
                                         scale=0.125),
                            waits=[(MM, t_ac)] + wsc, inc=ACS)
                        self.bank_free[n] = (ACS, t_ace)
                    if it + 1 < (c + 1) * NTT * 2:
                        emit_bd(it + 1, t_q, t_p)
                    if prev_work is not None:
                        prev_work()
                        prev_work = None
                    pr = it % 2
                    ba, bb = bds_bufs[pr]
                    t_l = t_bds_tgt[it]
                    P.emit("vector", lambda e, sc_t=sc_t, ba=ba:
                           e.scalar_tensor_tensor(
                               sc_t[:, 0:1024], ba[:], 0.125, sc_t[:, 0:1024],
                               op0=ALU.mult, op1=ALU.add),
                           waits=[(B.VST, t_l), (ACS, t_ace)])
                    t_cb = P.emit("vector", lambda e, sc_t=sc_t, bb=bb:
                                  e.scalar_tensor_tensor(
                                      sc_t[:, 1024:2048], bb[:], 0.125,
                                      sc_t[:, 1024:2048],
                                      op0=ALU.mult, op1=ALU.add), inc=EV)
                    bds_free[pr] = (EV, t_cb)
                    t_nb = P.emit("vector", lambda e, sc_t=sc_t:
                                  e.tensor_reduce(
                                      B.vsm[:, 1:2], sc_t[:],
                                      axis=mybir.AxisListType.X,
                                      op=ALU.max, negate=True), inc=EV)
                    t_exp = P.emit("scalar", lambda e, sc_t=sc_t: e.activation(
                        P_t[:], sc_t[:], AF.Exp, bias=B.vsm[:, 1:2],
                        scale=1.0, accum_out=B.vsm[:, 2:3]),
                        waits=[(EV, t_nb), (ACS, self.pt_guard)], inc=ACS)
                    sc_r.release(it, ACS, t_exp)
                    t_rc = P.emit("vector", lambda e: e.reciprocal(
                        B.vsm[:, 3:4], B.vsm[:, 2:3]),
                        waits=[(ACS, t_exp)], inc=EV)

                    def work(h=h, qt=qt, pb=pb, t_exp=t_exp, t_rc=t_rc,
                             vtm_rdy=vtm_rdy, it=it):
                        t_pte = None
                        for kc in range(NTT):
                            bi = 6 + (kc % 2)
                            t_mm = P.emit(
                                "tensor", lambda e, kc=kc, bi=bi:
                                e.transpose(B.PS[bi][:, 0:128],
                                            P_t[:, kc * 128:(kc + 1) * 128],
                                            B.ident[:]),
                                waits=[(ACS, t_exp)] + self.bankw(bi), inc=MM)
                            t_pte = P.emit(
                                "scalar", lambda e, kc=kc, bi=bi:
                                e.copy(PT_t[:, kc * 128:(kc + 1) * 128],
                                       B.PS[bi][:, 0:128]),
                                waits=[(MM, t_mm)], inc=ACS)
                            self.bank_free[bi] = (ACS, t_pte)
                        self.pt_guard = t_pte
                        bi = 6
                        t_mm = None
                        for kc in range(NTT):
                            t_mm = P.emit(
                                "tensor",
                                lambda e, kc=kc, pb=pb, s=(kc == 0),
                                t=(kc == NTT - 1):
                                e.matmul(B.PS[bi][:, 0:64],
                                         PT_t[:, kc * 128:(kc + 1) * 128],
                                         vtm_t[:, kc * 128 + pb:
                                               kc * 128 + pb + 64],
                                         start=s, stop=t),
                                waits=([(ACS, t_pte), vtm_rdy]
                                       + (self.bankw(bi) if kc == 0 else [])),
                                inc=(MM if kc == NTT - 1 else None))
                        stg, ws = stg_r.slot(st_i_l[0])
                        t_e = P.emit(
                            "scalar", lambda e, stg=stg:
                            e.activation(stg[:, 0:64], B.PS[bi][:, 0:64],
                                         AF.Identity, scale=B.vsm[:, 3:4]),
                            waits=[(MM, t_mm), (EV, t_rc)] + ws, inc=ACS)
                        self.bank_free[bi] = (ACS, t_e)
                        t_st = P.dma("gpsimd",
                                     B.A0TM[qt][:, h * 64:h * 64 + 64],
                                     stg[:, 0:64], GST, waits=[(ACS, t_e)])
                        stg_r.release(st_i_l[0], GST, t_st)
                        st_i_l[0] += 1
                    st_i_l = [st_i]
                    prev_work = work
                    st_i += 1
            if prev_work is not None:
                prev_work()
                prev_work = None
            pair_guard = (MM, MM.n)
        self.a0_done = (GST, GST.n)

    def phase_J(self):
        B, P = self, self.P
        SLD, GST, MM, EV, ACS = B.SLD, B.GST, B.MM, B.EV, B.ACS
        at_r = Rot([B.TW[0], B.TW[1]])
        for tt in range(NTT):
            at_t, wa = at_r.slot(tt)
            t_ld = P.dma("sync", at_t[:], B.A0TM[tt], SLD,
                         waits=wa + [self.a0_done, self.xa_guard])
            t_mm = B.emit_transpose_1024(at_t, [(SLD, t_ld)], (0, 1), 1)
            at_r.release(tt, MM, t_mm)
            for j in range(NCH):
                bi = j // 4
                psb = B.PS[bi][:, (j % 4) * 128:(j % 4) * 128 + 128]
                t_e = P.emit("scalar", lambda e, j=j, psb=psb, tt=tt:
                             e.copy(B.XA[j][:, tt * 128:(tt + 1) * 128], psb),
                             waits=[(MM, t_mm)], inc=ACS)
            for bi in (0, 1):
                self.bank_free[bi] = (ACS, t_e)
        attn_rdy = [(ACS, ACS.n)]
        wo_r = self.wo_d.rearrange("(kc k) m -> k kc m", k=128)
        xn_r = Rot([B.ST[0], B.ST[1]])
        out_r = Rot([B.ST[2], B.ST[3]])
        for g in range(2):
            wa_half = B.WA[:, (g % 2) * 4096:(g % 2) * 4096 + 4096]
            _, ww = self.warot.slot(g)
            t_wl = P.dma("sync", wa_half,
                         wo_r[:, :, g * 512:(g + 1) * 512], SLD, waits=ww)
            for mj in range(4):
                m = g * 4 + mj
                xn_t, wx = xn_r.slot(m)
                t_xn = P.dma("sync", xn_t[:], B.XN[m], SLD, waits=wx)
                out_t, wo2 = out_r.slot(m)

                def lhsT_fn(_m, k, wa_half=wa_half, t_wl=t_wl, mj=mj):
                    return (wa_half[:, k * 512 + mj * 128: k * 512 + mj * 128 + 128],
                            [(SLD, t_wl)])

                def rhs_fn(k, n):
                    return (B.XA[k][:, n * 512:(n + 1) * 512], attn_rdy)

                def evac(_m, n, ps, t_mm, m=m, xn_t=xn_t, out_t=out_t,
                         t_xn=t_xn, wo2=wo2):
                    t_e = P.emit(
                        "vector", lambda e, n=n, ps=ps:
                        e.scalar_tensor_tensor(
                            out_t[:, n * 512:(n + 1) * 512], ps, 1.0,
                            xn_t[:, n * 512:(n + 1) * 512],
                            op0=ALU.mult, op1=ALU.add),
                        waits=[(MM, t_mm), (SLD, t_xn)] + wo2, inc=EV)
                    return (EV, t_e)

                self.gemm(1, 4, NCH, lhsT_fn, rhs_fn, evac, banks=(2, 3, 4, 5))
                t_ev = EV.n
                xn_r.release(m, EV, t_ev)
                t_st = P.dma("gpsimd", B.SRC2[m], out_t[:], GST,
                             waits=[(EV, t_ev)])
                out_r.release(m, GST, t_st)
            self.warot.release(g, MM, MM.n)
        self.xa_guard = (MM, MM.n)
        self.src2_done = (GST, GST.n)

    def phase_K(self):
        B, P = self, self.P
        SLD, GST, MM, EV, ACS = B.SLD, B.GST, B.MM, B.EV, B.ACS
        t_lds = []
        for j in range(NCH):
            t_lds.append(P.dma("sync", B.XA[j][:], B.SRC2[j], SLD,
                               waits=[self.xa_guard, self.src2_done]))
        src2_rdy = [(SLD, t_lds[-1])]
        tm_r = Rot([B.TW[0], B.TW[1]])
        stg_r = Rot([B.STG[0], B.STG[1]])
        for tt in range(NTT):
            cs = slice(tt * 128, tt * 128 + 128)
            t_mm = None
            for j in range(NCH):
                bi = j // 4
                ps = B.PS[bi][:, (j % 4) * 128:(j % 4) * 128 + 128]
                w = src2_rdy + ([(SLD, t_lds[j])] if True else [])
                t_mm = P.emit("tensor", lambda e, j=j, ps=ps, cs=cs:
                              e.transpose(ps, B.XA[j][:, cs], B.ident[:]),
                              waits=w + (self.bankw(bi) if j % 4 == 0 else []),
                              inc=MM)
            tm_t, wt = tm_r.slot(tt)
            P.emit("vector", lambda e: e.bn_stats(
                B.bn6[:, 0:6], B.PS[0][:, 0:512]),
                waits=[(MM, t_mm)] + wt)
            P.emit("vector", lambda e: e.bn_stats(
                B.bn6[:, 6:12], B.PS[1][:, 0:512]))
            P.emit("vector", lambda e: e.drain())
            P.emit("vector", lambda e: e.bn_aggr(
                B.stat[:, 0:2], B.bn6[:].rearrange("p (a b) -> p a b", b=3)))
            t_ag2 = P.emit("vector", lambda e: e.engine_nop(), inc=EV)
            t_ve = P.emit("scalar", lambda e: e.activation(
                B.stat[:, 2:3], B.stat[:, 1:2], AF.Identity,
                bias=B.epsc[:]), waits=[(EV, t_ag2)], inc=ACS)
            t_rc = P.emit("vector", lambda e: e.reciprocal(
                B.stat[:, 3:4], B.stat[:, 2:3]), waits=[(ACS, t_ve)], inc=EV)
            t_sq = P.emit("scalar", lambda e: e.activation(
                B.vsm[:, 1:2], B.stat[:, 3:4], AF.Sqrt),
                waits=[(EV, t_rc)], inc=ACS)
            P.wait("vector", [(ACS, t_sq)])
            t_ap = None
            for bi in range(2):
                t_ap = P.emit("vector", lambda e, bi=bi, tm_t=tm_t:
                              e.tensor_scalar(
                                  tm_t[:, bi * 512:(bi + 1) * 512],
                                  B.PS[bi][:, 0:512],
                                  B.stat[:, 0:1], B.vsm[:, 1:2],
                                  op0=ALU.subtract, op1=ALU.mult), inc=EV)
                self.bank_free[bi] = (EV, t_ap)
            t_mm2 = B.emit_transpose_1024(tm_t[:], [(EV, t_ap)], (2, 3), 1)
            tm_r.release(tt, MM, t_mm2)
            stg, ws = stg_r.slot(tt)
            P.wait("vector", ws)
            t_e = None
            for j in range(NCH):
                bi = 2 + j // 4
                psb = B.PS[bi][:, (j % 4) * 128:(j % 4) * 128 + 128]
                P.emit("vector", lambda e, j=j, psb=psb, cs=cs:
                       e.tensor_scalar(B.XA[j][:, cs], psb,
                                       B.svec("g2", j), B.svec("b2", j),
                                       op0=ALU.mult, op1=ALU.add),
                       waits=[(MM, t_mm2)], inc=EV)
                t_e = P.emit("vector", lambda e, j=j, psb=psb, stg=stg:
                             e.tensor_scalar(stg[:, j * 128:(j + 1) * 128], psb,
                                             B.svec("g2", j), B.svec("b2", j),
                                             op0=ALU.mult, op1=ALU.add),
                             inc=EV)
            for bi in (2, 3):
                self.bank_free[bi] = (EV, t_e)
            t_st = P.dma("gpsimd", B.X2D[:, :, cs].rearrange("c p q -> p c q"),
                         stg[:], GST, waits=[(EV, t_e)])
            stg_r.release(tt, GST, t_st)
        self.x2_ready = (EV, EV.n)
        self.x2d_done = (GST, GST.n)

    def phase_L(self):
        B, P = self, self.P
        SLD, GST, MM, EV, ACS = B.SLD, B.GST, B.MM, B.EV, B.ACS
        t_sw = P.dma("sync", B.selw_s[:], self.selw_d.rearrange(
            "(kc k) x -> k kc x", k=128), SLD)
        sel_t = B.TW[0][:, 0:512]
        for tt in range(NTT):
            bi = 4 + (tt % 2)
            t_mm = None
            for k in range(NCH):
                t_mm = P.emit(
                    "tensor",
                    lambda e, k=k, tt=tt, bi=bi, s=(k == 0), t=(k == 7):
                    e.matmul(B.PS[bi][:, 0:32],
                             B.XA[k][:, tt * 128:(tt + 1) * 128],
                             B.selw_s[:, k * 32:(k + 1) * 32],
                             start=s, stop=t),
                    waits=[self.x2_ready, (SLD, t_sw)]
                    + (self.bankw(bi) if k == 0 else []),
                    inc=(MM if k == 7 else None))
            t_e = P.emit("scalar", lambda e, tt=tt, bi=bi: e.activation(
                sel_t[:, tt * 32:(tt + 1) * 32], B.PS[bi][:, 0:32],
                AF.Sigmoid), waits=[(MM, t_mm)], inc=ACS)
            self.bank_free[bi] = (ACS, t_e)
        sel_rdy = (ACS, ACS.n)
        wrk3 = B.TW[1][:, 0:512].rearrange("p (a b) -> p a b", b=32)
        ge3 = B.TW[1][:, 512:1024].rearrange("p (a b) -> p a b", b=32)
        sel3 = sel_t[:].rearrange("p (a b) -> p a b", b=32)
        t4b = B.vsm[:, 0:16].rearrange("p (a b) -> p a b", b=1) \
            .broadcast_to([128, 16, 32])
        P.emit("vector", lambda e: e.tensor_copy(wrk3, sel3), waits=[sel_rdy])
        for it in range(4):
            P.emit("vector", lambda e: e.tensor_reduce(
                B.vsm[:, 0:16], wrk3, axis=mybir.AxisListType.X, op=ALU.max))
            B.spacers()
            if it < 3:
                P.emit("vector", lambda e: e.tensor_tensor(
                    ge3, wrk3, t4b, op=ALU.is_ge))
                P.emit("vector", lambda e: e.scalar_tensor_tensor(
                    wrk3, ge3, -1e30, wrk3, op0=ALU.mult, op1=ALU.add))
        P.emit("vector", lambda e: e.tensor_tensor(
            ge3, sel3, t4b, op=ALU.is_ge))
        t_g = P.emit("vector", lambda e: e.tensor_tensor(
            B.gate_sb[:].rearrange("p (a b) -> p a b", b=32), sel3, ge3,
            op=ALU.mult), inc=EV)
        self.gates_ready = (EV, t_g)
        self.xa_guard = (MM, MM.n)
        if "GATED" in self.dbg:
            gd = self.dram("GATED", [128, 512])
            P.dma("gpsimd", gd, B.gate_sb[:], GST, waits=[self.gates_ready])

    def phase_M(self):
        B, P = self, self.P
        SLD, GST, MM, EV, ACS, GLD = B.SLD, B.GST, B.MM, B.EV, B.ACS, B.GLD
        keys_r = self.keys_d.rearrange("x (kc p) h -> kc p x h", p=128)
        x2_r = Rot([B.TW[0], B.TW[1]])
        stg_r = Rot([B.STG[0][:, 0:512], B.STG[0][:, 512:1024],
                     B.STG[1][:, 0:512], B.STG[1][:, 512:1024]])
        st2_r = Rot([B.STG[2][:, 0:512], B.STG[2][:, 512:1024],
                     B.STG[3][:, 0:512], B.STG[3][:, 512:1024]])
        si = 0
        xi = 0
        vals_r = self.values_d
        st_guard = None
        for sst in range(4):
            for half in range(2):
                eb = half * 16
                t_keys = []
                for k in range(NCH):
                    t_keys.append(P.dma(
                        "sync", B.XA[k][:],
                        keys_r[k][:, eb:eb + 16, :], SLD,
                        waits=[self.xa_guard]))
                for ti in range(4):
                    tt = sst * 4 + ti
                    x2_t, wx = x2_r.slot(xi)
                    t_x2 = P.dma("sync", x2_t[:],
                                 B.X2D[:, :, tt * 128:(tt + 1) * 128]
                                 .rearrange("c p q -> p c q"), SLD,
                                 waits=wx + [self.x2d_done, self.gates_ready])
                    t_mm = None
                    for k in range(NCH):
                        for g in range(4):
                            bi = g
                            ws = ([(SLD, t_x2), (SLD, t_keys[k])]
                                  + (self.bankw(bi) if k == 0 else []))
                            t_mm = P.emit(
                                "tensor",
                                lambda e, k=k, g=g, bi=bi, x2_t=x2_t,
                                s=(k == 0), t=(k == NCH - 1):
                                e.matmul(B.PS[bi][:],
                                         x2_t[:, k * 128:(k + 1) * 128],
                                         B.XA[k][:, g * 512:(g + 1) * 512],
                                         start=s, stop=t),
                                waits=ws,
                                inc=(MM if (k == 7 and g == 3) else None))
                    x2_r.release(xi, MM, t_mm)
                    xi += 1
                    self.xa_guard = (MM, t_mm)
                    t_hg = None
                    for e_ in range(16):
                        bi, qo = e_ // 4, (e_ % 4) * 128
                        ge = eb + e_
                        whg = [(EV, self.hg_guard)] if (self.hg_guard and e_ == 0) else []
                        t_hg = P.emit(
                            "scalar",
                            lambda e, e_=e_, bi=bi, qo=qo, tt=tt, ge=ge:
                            e.activation(B.PR[:, e_ * 128:(e_ + 1) * 128],
                                         B.PS[bi][:, qo:qo + 128], AF.Relu,
                                         scale=B.gate_sb[:, tt * 32 + ge:
                                                         tt * 32 + ge + 1]),
                            waits=[(MM, t_mm), self.gates_ready] + whg,
                            inc=ACS)
                    for bi in range(4):
                        self.bank_free[bi] = (ACS, t_hg)
                    t_pte = None
                    for eg in range(4):
                        bi = 4 + eg % 2
                        t_mm2 = None
                        for j in range(4):
                            e_ = eg * 4 + j
                            t_mm2 = P.emit(
                                "tensor", lambda e, e_=e_, j=j, bi=bi:
                                e.transpose(B.PS[bi][:, j * 128:(j + 1) * 128],
                                            B.PR[:, e_ * 128:(e_ + 1) * 128],
                                            B.ident[:]),
                                waits=[(ACS, t_hg)]
                                + (self.bankw(bi) if j == 0 else [])
                                + ([st_guard] if (st_guard and eg == 0 and j == 0)
                                   else []),
                                inc=(MM if j == 3 else None))
                        t_pte = P.emit(
                            "vector", lambda e, eg=eg, bi=bi, ti=ti, half=half:
                            e.tensor_copy(
                                B.ST[half * 4 + eg][:]
                                .rearrange("p (a b) -> p a b", b=512)
                                [:, :, ti * 128:(ti + 1) * 128],
                                B.PS[bi][:, 0:512]
                                .rearrange("p (a b) -> p a b", b=128)),
                            waits=[(MM, t_mm2)], inc=EV)
                        self.bank_free[bi] = (EV, t_pte)
                    self.hg_guard = t_pte
            hgT_rdy = (EV, EV.n)
            t_s3 = None
            for m in range(NCH):
                wa_half = B.WA[:, (m % 2) * 4096:(m % 2) * 4096 + 4096]
                _, ww = self.warot.slot(m)
                t_wl = P.dma("sync", wa_half.rearrange("p (a b) -> p a b", b=128),
                             vals_r[:, :, m * 128:(m + 1) * 128]
                             .transpose([1, 0, 2]), SLD, waits=ww)
                bi = 6 + (m % 2)
                t_mm = None
                for e_ in range(32):
                    t_mm = P.emit(
                        "tensor",
                        lambda e, e_=e_, bi=bi, wa_half=wa_half,
                        s=(e_ == 0), t=(e_ == 31):
                        e.matmul(B.PS[bi][:, 0:512],
                                 wa_half[:, e_ * 128:(e_ + 1) * 128],
                                 B.ST[e_ // 4][:, (e_ % 4) * 512:
                                               (e_ % 4) * 512 + 512],
                                 start=s, stop=t),
                        waits=[(SLD, t_wl), hgT_rdy]
                        + (self.bankw(bi) if e_ == 0 else []),
                        inc=(MM if e_ == 31 else None))
                self.warot.release(m, MM, t_mm)
                t_s3 = t_mm
                stg, ws = stg_r.slot(si)
                st2, ws2 = st2_r.slot(si)
                t_res = P.dma("sync", stg,
                              B.RES[m][:, sst * 512:(sst + 1) * 512], SLD,
                              waits=ws + [self.res_ssm_done])
                t_s2 = P.dma("sync", st2,
                             B.SRC2[m][:, sst * 512:(sst + 1) * 512], SLD,
                             waits=ws2 + [self.src2_done])
                t_a1 = P.emit("vector", lambda e, bi=bi, stg=stg:
                              e.scalar_tensor_tensor(
                                  stg, B.PS[bi][:, 0:512], 1.0, stg,
                                  op0=ALU.mult, op1=ALU.add),
                              waits=[(MM, t_mm), (SLD, t_res)], inc=EV)
                self.bank_free[bi] = (EV, t_a1)
                t_a2 = P.emit("vector", lambda e, stg=stg, st2=st2:
                              e.tensor_tensor(stg, stg, st2, op=ALU.add),
                              waits=[(SLD, t_s2)], inc=EV)
                st2_r.release(si, EV, t_a2)
                t_st = P.dma("gpsimd", B.RES[m][:, sst * 512:(sst + 1) * 512],
                             stg, GST, waits=[(EV, t_a2)])
                stg_r.release(si, GST, t_st)
                si += 1
            st_guard = (MM, t_s3)
        self.x1_done = (GST, GST.n)
        self.xa_guard = (MM, MM.n)

    def phase_N(self):
        B, P = self, self.P
        SLD, GST, MM, EV, ACS = B.SLD, B.GST, B.MM, B.EV, B.ACS
        t_lds = []
        for j in range(NCH):
            t_lds.append(P.dma("sync", B.XA[j][:], B.RES[j], SLD,
                               waits=[self.xa_guard, self.x1_done]))
        tm_r = Rot([B.TW[0], B.TW[1]])
        stg_r = Rot([B.STG[0], B.STG[1]])
        for tt in range(NTT):
            cs = slice(tt * 128, tt * 128 + 128)
            t_mm = None
            for j in range(NCH):
                bi = j // 4
                ps = B.PS[bi][:, (j % 4) * 128:(j % 4) * 128 + 128]
                t_mm = P.emit("tensor", lambda e, j=j, ps=ps, cs=cs:
                              e.transpose(ps, B.XA[j][:, cs], B.ident[:]),
                              waits=[(SLD, t_lds[j])]
                              + (self.bankw(bi) if j % 4 == 0 else []),
                              inc=MM)
            tm_t, wt = tm_r.slot(tt)
            P.emit("vector", lambda e: e.bn_stats(
                B.bn6[:, 0:6], B.PS[0][:, 0:512]),
                waits=[(MM, t_mm)] + wt)
            P.emit("vector", lambda e: e.bn_stats(
                B.bn6[:, 6:12], B.PS[1][:, 0:512]))
            B.spacers()
            t_ag2 = P.emit("vector", lambda e: e.bn_aggr(
                B.stat[:, 0:2], B.bn6[:].rearrange("p (a b) -> p a b", b=3)),
                inc=EV)
            t_ve = P.emit("scalar", lambda e: e.activation(
                B.stat[:, 2:3], B.stat[:, 1:2], AF.Identity,
                bias=B.epsc[:]), waits=[(EV, t_ag2)], inc=ACS)
            t_rc = P.emit("vector", lambda e: e.reciprocal(
                B.stat[:, 3:4], B.stat[:, 2:3]), waits=[(ACS, t_ve)], inc=EV)
            t_sq = P.emit("scalar", lambda e: e.activation(
                B.vsm[:, 1:2], B.stat[:, 3:4], AF.Sqrt),
                waits=[(EV, t_rc)], inc=ACS)
            P.wait("vector", [(ACS, t_sq)])
            t_ap = None
            for bi in range(2):
                t_ap = P.emit("vector", lambda e, bi=bi, tm_t=tm_t:
                              e.tensor_scalar(
                                  tm_t[:, bi * 512:(bi + 1) * 512],
                                  B.PS[bi][:, 0:512],
                                  B.stat[:, 0:1], B.vsm[:, 1:2],
                                  op0=ALU.subtract, op1=ALU.mult), inc=EV)
                self.bank_free[bi] = (EV, t_ap)
            t_mm2 = B.emit_transpose_1024(tm_t[:], [(EV, t_ap)], (2, 3), 1)
            tm_r.release(tt, MM, t_mm2)
            stg, ws = stg_r.slot(tt)
            P.wait("scalar", ws)
            t_e = None
            for bi2 in range(2):
                t_e = P.emit("scalar", lambda e, bi2=bi2, stg=stg:
                             e.copy(stg[:, bi2 * 512:bi2 * 512 + 512],
                                    B.PS[2 + bi2][:]),
                             waits=[(MM, t_mm2)], inc=ACS)
            P.wait("vector", [(ACS, t_e)])
            t_e2 = None
            for j in range(NCH):
                t_e2 = P.emit("vector", lambda e, j=j, stg=stg, cs=cs:
                              e.tensor_copy(B.XA[j][:, cs],
                                            stg[:, j * 128:(j + 1) * 128]),
                              inc=EV)
            for bi2 in (2, 3):
                self.bank_free[bi2] = (ACS, t_e)
            t_st = P.dma("gpsimd", B.XN1[:, :, cs].rearrange("c p q -> p c q"),
                         stg[:], GST, waits=[(EV, t_e2)])
            stg_r.release(tt, GST, t_st)
        self.xn1_ready = (EV, EV.n)
        self.xn1d_done = (GST, GST.n)

    def phase_O(self):
        B, P = self, self.P
        SLD, GST, MM, EV, ACS = B.SLD, B.GST, B.MM, B.EV, B.ACS
        stg_r = Rot([B.TW[0][:, 0:512], B.TW[0][:, 512:1024],
                     B.TW[1][:, 0:512], B.TW[1][:, 512:1024]])
        P.wait("scalar", [(GST, GST.n)])
        si = 0
        for wi, (wd, dst) in enumerate(
                [(self.l1wq_d, B.Q1), (self.l1wk_d, B.K1), (self.l1wv_d, B.V1)]):
            w_r = wd.rearrange("(kc k) m -> k kc m", k=128)
            for g in range(2):
                wa_half = B.WA[:, (g % 2) * 4096:(g % 2) * 4096 + 4096]
                _, ww = self.warot.slot(wi * 2 + g)
                t_wl = P.dma("sync", wa_half,
                             w_r[:, :, g * 512:(g + 1) * 512], SLD, waits=ww)

                def lhsT_fn(m, k, wa_half=wa_half, t_wl=t_wl):
                    return (wa_half[:, k * 512 + m * 128: k * 512 + m * 128 + 128],
                            [(SLD, t_wl)])

                def rhs_fn(k, n):
                    return (B.XA[k][:, n * 512:(n + 1) * 512],
                            [self.xn1_ready])

                def evac(m, n, ps, t_mm, g=g, dst=dst):
                    nonlocal si
                    gm = g * 4 + m
                    stg, ws = stg_r.slot(si)
                    t_e = P.emit("scalar", lambda e, stg=stg, ps=ps:
                                 e.copy(stg, ps),
                                 waits=[(MM, t_mm)] + ws, inc=ACS)
                    t_st = P.dma("gpsimd", dst[gm][:, n * 512:(n + 1) * 512],
                                 stg, GST, waits=[(ACS, t_e)])
                    stg_r.release(si, GST, t_st)
                    si += 1
                    return (ACS, t_e)

                self.gemm(4, 4, NCH, lhsT_fn, rhs_fn, evac)
                self.warot.release(wi * 2 + g, MM, MM.n)
        self.xa_guard = (MM, MM.n)
        self.qkv1_done = (GST, GST.n)

    def phase_PQ(self):
        B, P = self, self.P
        SLD, GST, MM, EV, ACS, GLD = B.SLD, B.GST, B.MM, B.EV, B.ACS, B.GLD
        LN1E4 = float(np.log(10000.0))
        TWO_PI = 2.0 * np.pi
        C1, C2, C3 = 6.28125, 0.0019350051879882812, 3.0199402139940485e-07
        for eng in ("gpsimd", "vector", "scalar"):
            P.wait(eng, [self.qkv1_done])
        sin_t, cos_t = B.ST[6], B.ST[7]
        kf_t = B.ST[0]
        t_i = P.emit("gpsimd", lambda e: e.iota(
            B.vsm[0:64, 5:6], pattern=[[1, 1]], base=0, channel_multiplier=1,
            allow_small_or_imprecise_dtypes=True), inc=B.GC)
        t_i = P.emit("gpsimd", lambda e: e.iota(
            B.vsm[64:128, 5:6], pattern=[[1, 1]], base=0, channel_multiplier=1,
            allow_small_or_imprecise_dtypes=True), inc=B.GC)
        t_if = P.emit("scalar", lambda e: e.activation(
            B.vsm[:, 5:6], B.vsm[:, 5:6], AF.Exp, scale=-LN1E4 / 64.0),
            waits=[(B.GC, t_i)], inc=ACS)
        t_i2 = P.emit("gpsimd", lambda e: e.iota(
            sin_t[:], pattern=[[1, T]], base=0, channel_multiplier=0,
            allow_small_or_imprecise_dtypes=True), inc=B.GC)
        P.emit("vector", lambda e: e.tensor_scalar(
            cos_t[:], sin_t[:], B.vsm[:, 5:6], None,
            op0=ALU.mult), waits=[(B.GC, t_i2), (ACS, t_if)])
        t_tab = None
        for tgt, shift in ((sin_t, 0.0), (cos_t, float(np.pi / 2))):
            P.emit("vector", lambda e: e.tensor_scalar(
                kf_t[:], cos_t[:], 1.0 / TWO_PI, None, op0=ALU.mult))
            for pc in range(4):
                P.emit("vector", lambda e, pc=pc: e.tensor_copy(
                    B.ki32[:], kf_t[:, pc * 512:(pc + 1) * 512]))
                P.emit("vector", lambda e, pc=pc: e.tensor_copy(
                    kf_t[:, pc * 512:(pc + 1) * 512], B.ki32[:]))
            first = True
            for cc in (C1, C2, C3):
                P.emit("vector", lambda e, cc=cc, tgt=tgt, first=first:
                       e.scalar_tensor_tensor(
                           tgt[:], kf_t[:], -cc,
                           cos_t[:] if first else tgt[:],
                           op0=ALU.mult, op1=ALU.add))
                first = False
            if shift:
                P.emit("vector", lambda e, tgt=tgt, shift=shift:
                       e.tensor_scalar(tgt[:], tgt[:], shift,
                                       None, op0=ALU.add))
            P.emit("vector", lambda e, tgt=tgt: e.tensor_scalar(
                kf_t[:], tgt[:], float(np.pi), None, op0=ALU.is_gt))
            P.emit("vector", lambda e, tgt=tgt: e.scalar_tensor_tensor(
                tgt[:], kf_t[:], -TWO_PI, tgt[:],
                op0=ALU.mult, op1=ALU.add))
            P.emit("vector", lambda e, tgt=tgt: e.tensor_scalar(
                kf_t[:], tgt[:], float(-np.pi), None, op0=ALU.is_lt))
            t_wv = P.emit("vector", lambda e, tgt=tgt: e.scalar_tensor_tensor(
                tgt[:], kf_t[:], TWO_PI, tgt[:],
                op0=ALU.mult, op1=ALU.add), inc=EV)
            t_tab = P.emit("scalar", lambda e, tgt=tgt: e.activation(
                tgt[:], tgt[:], AF.Sin),
                waits=[(EV, t_wv)], inc=ACS)
        tab_rdy = (ACS, t_tab)
        qin, qsw, rot = B.ST[0], B.ST[1], B.ST[2]
        t_prev = None
        for src, dst in ((B.Q1, B.Q1R), (B.K1, B.K1R)):
            for h in range(NCH):
                w = [t_prev] if t_prev else [(EV, EV.n)]
                t_ld = P.dma("sync", qin[:], src[h], SLD,
                             waits=w + [self.qkv1_done])
                P.dma("sync", qsw[0:64, :], src[h][64:128, :], SLD)
                t_ld2 = P.dma("sync", qsw[64:128, :], src[h][0:64, :], SLD)
                wv = [(SLD, t_ld2), tab_rdy]
                P.emit("vector", lambda e: e.tensor_tensor(
                    rot[0:64, :], qin[0:64, :], cos_t[0:64, :], op=ALU.mult),
                    waits=wv)
                P.emit("vector", lambda e: e.tensor_tensor(
                    qsw[0:64, :], qsw[0:64, :], sin_t[0:64, :], op=ALU.mult))
                P.emit("vector", lambda e: e.tensor_tensor(
                    rot[0:64, :], rot[0:64, :], qsw[0:64, :],
                    op=ALU.subtract))
                P.emit("vector", lambda e: e.tensor_tensor(
                    rot[64:128, :], qin[64:128, :], cos_t[64:128, :],
                    op=ALU.mult))
                P.emit("vector", lambda e: e.tensor_tensor(
                    qsw[64:128, :], qsw[64:128, :], sin_t[64:128, :],
                    op=ALU.mult))
                t_r = P.emit("vector", lambda e: e.tensor_tensor(
                    rot[64:128, :], rot[64:128, :], qsw[64:128, :],
                    op=ALU.add), inc=EV)
                t_st = P.dma("gpsimd", dst[h], rot[:], GST,
                             waits=[(EV, t_r)])
                t_prev = (GST, t_st)
        self.rope_done = (GST, GST.n)

    def phase_R(self):
        B, P = self, self.P
        SLD, GST, MM, EV, ACS = B.SLD, B.GST, B.MM, B.EV, B.ACS
        qc_t, kc_t, vc_t, vtm_t = B.ST[0], B.ST[1], B.ST[2], B.ST[3]
        sc_r = Rot([B.ST[4], B.ST[5]])
        P_t, PT_t = B.ST[6], B.ST[7]
        B.mask_sb = B.PR[:, 0:896]
        self.t_mask = P.dma("sync", B.mask_sb, self.mask_d, B.SLD)
        stg_r = Rot([B.TW[0][:, 0:512], B.TW[0][:, 512:1024],
                     B.TW[1][:, 0:512], B.TW[1][:, 512:1024]])
        pair_guard = None
        st_i = 0
        pt_guard1 = None
        prev_work = None
        for h in range(NCH):
            wld = ([pair_guard] if pair_guard else []) + [self.rope_done]
            t_q = P.dma("sync", qc_t[:], B.Q1R[h], SLD, waits=wld)
            t_k = P.dma("sync", kc_t[:], B.K1R[h], SLD)
            t_v = P.dma("sync", vc_t[:], B.V1[h], SLD,
                        waits=[self.qkv1_done])
            t_vtm_e = None
            for tc in range(NTT):
                bi = 6 + (tc % 2)
                t_mm = P.emit("tensor", lambda e, tc=tc, bi=bi:
                              e.transpose(B.PS[bi][:, 0:128],
                                          vc_t[:, tc * 128:(tc + 1) * 128],
                                          B.ident[:]),
                              waits=[(SLD, t_v)] + self.bankw(bi), inc=MM)
                t_vtm_e = P.emit("scalar", lambda e, tc=tc, bi=bi:
                                 e.copy(vtm_t[:, tc * 128:(tc + 1) * 128],
                                        B.PS[bi][:, 0:128]),
                                 waits=[(MM, t_mm)], inc=ACS)
                self.bank_free[bi] = (ACS, t_vtm_e)
            vtm_rdy = (ACS, t_vtm_e)
            for qt in range(NTT):
                ncn = qt // 4 + 1
                off = 128 * (qt % 4)
                qsl = qc_t[:, qt * 128:(qt + 1) * 128]
                t_ac = None
                for n in range(ncn):
                    t_ac = P.emit(
                        "tensor", lambda e, n=n, qsl=qsl:
                        e.matmul(B.PS[n][:, 0:512], qsl,
                                 kc_t[:, n * 512:(n + 1) * 512],
                                 start=True, stop=True),
                        waits=[(SLD, t_q), (SLD, t_k)] + self.bankw(n),
                        inc=(MM if n == ncn - 1 else None))
                if prev_work is not None:
                    prev_work()
                    prev_work = None
                sc_t, wsc = sc_r.slot(st_i)
                t_cb = None
                for n in range(ncn):
                    if n < ncn - 1:
                        t_cb = P.emit(
                            "vector", lambda e, n=n, sc_t=sc_t:
                            e.tensor_scalar(sc_t[:, n * 512:(n + 1) * 512],
                                            B.PS[n][:, 0:512],
                                            1.0 / 128.0, None, op0=ALU.mult),
                            waits=[(MM, t_ac)] + wsc, inc=EV)
                    else:
                        t_cb = P.emit(
                            "vector", lambda e, n=n, sc_t=sc_t, off=off:
                            e.scalar_tensor_tensor(
                                sc_t[:, n * 512:(n + 1) * 512],
                                B.PS[n][:, 0:512], 1.0 / 128.0,
                                B.mask_sb[:, 384 - off: 896 - off],
                                op0=ALU.mult, op1=ALU.add),
                            waits=[(MM, t_ac), (SLD, self.t_mask)] + wsc,
                            inc=EV)
                    self.bank_free[n] = (EV, t_cb)
                nk = ncn * 512
                t_nb = P.emit("vector", lambda e, sc_t=sc_t, nk=nk:
                              e.tensor_reduce(
                                  B.vsm[:, 1:2], sc_t[:, 0:nk],
                                  axis=mybir.AxisListType.X,
                                  op=ALU.max, negate=True), inc=EV)
                t_exp = P.emit("scalar", lambda e, sc_t=sc_t, nk=nk:
                               e.activation(
                                   P_t[:, 0:nk], sc_t[:, 0:nk], AF.Exp,
                                   bias=B.vsm[:, 1:2], scale=1.0,
                                   accum_out=B.vsm[:, 2:3]),
                               waits=[(EV, t_nb)]
                               + ([(ACS, pt_guard1)] if pt_guard1 else []),
                               inc=ACS)
                sc_r.release(st_i, ACS, t_exp)
                t_rc = P.emit("vector", lambda e: e.reciprocal(
                    B.vsm[:, 3:4], B.vsm[:, 2:3]),
                    waits=[(ACS, t_exp)], inc=EV)

                def work(h=h, qt=qt, t_exp=t_exp, t_rc=t_rc,
                         vtm_rdy=vtm_rdy, nkc=(qt + 1), sti=st_i):
                    nonlocal pt_guard1
                    t_pte = None
                    for kc in range(nkc):
                        bi = 6 + (kc % 2)
                        t_mm = P.emit(
                            "tensor", lambda e, kc=kc, bi=bi:
                            e.transpose(B.PS[bi][:, 0:128],
                                        P_t[:, kc * 128:(kc + 1) * 128],
                                        B.ident[:]),
                            waits=[(ACS, t_exp)] + self.bankw(bi), inc=MM)
                        t_pte = P.emit(
                            "scalar", lambda e, kc=kc, bi=bi:
                            e.copy(PT_t[:, kc * 128:(kc + 1) * 128],
                                   B.PS[bi][:, 0:128]),
                            waits=[(MM, t_mm)], inc=ACS)
                        self.bank_free[bi] = (ACS, t_pte)
                    pt_guard1 = t_pte
                    bi = 6
                    t_mm = None
                    for kc in range(nkc):
                        t_mm = P.emit(
                            "tensor",
                            lambda e, kc=kc, s=(kc == 0), t=(kc == nkc - 1):
                            e.matmul(B.PS[bi][:, 0:128],
                                     PT_t[:, kc * 128:(kc + 1) * 128],
                                     vtm_t[:, kc * 128:(kc + 1) * 128],
                                     start=s, stop=t),
                            waits=([(ACS, t_pte), vtm_rdy]
                                   + (self.bankw(bi) if kc == 0 else [])),
                            inc=(MM if kc == nkc - 1 else None))
                    stg, ws = stg_r.slot(sti)
                    t_e = P.emit(
                        "scalar", lambda e, stg=stg:
                        e.activation(stg[:, 0:128], B.PS[bi][:, 0:128],
                                     AF.Identity, scale=B.vsm[:, 3:4]),
                        waits=[(MM, t_mm), (EV, t_rc)] + ws, inc=ACS)
                    self.bank_free[bi] = (ACS, t_e)
                    t_st = P.dma("gpsimd",
                                 B.A1TM[qt][:, h * 128:(h + 1) * 128],
                                 stg[:, 0:128], GST, waits=[(ACS, t_e)])
                    stg_r.release(sti, GST, t_st)
                prev_work = work
                st_i += 1
            if prev_work is not None:
                prev_work()
                prev_work = None
            pair_guard = (MM, MM.n)
        self.a1_done = (GST, GST.n)

    def phase_S(self):
        B, P = self, self.P
        SLD, GST, MM, EV, ACS = B.SLD, B.GST, B.MM, B.EV, B.ACS
        at_r = Rot([B.TW[0], B.TW[1]])
        t_e = None
        for tt in range(NTT):
            at_t, wa = at_r.slot(tt)
            t_ld = P.dma("sync", at_t[:], B.A1TM[tt], SLD,
                         waits=wa + [self.a1_done, self.xa_guard])
            t_mm = B.emit_transpose_1024(at_t, [(SLD, t_ld)], (0, 1), 1)
            at_r.release(tt, MM, t_mm)
            for j in range(NCH):
                bi = j // 4
                psb = B.PS[bi][:, (j % 4) * 128:(j % 4) * 128 + 128]
                t_e = P.emit("scalar", lambda e, j=j, psb=psb, tt=tt:
                             e.copy(B.XA[j][:, tt * 128:(tt + 1) * 128], psb),
                             waits=[(MM, t_mm)], inc=ACS)
            for bi in (0, 1):
                self.bank_free[bi] = (ACS, t_e)
        attn_rdy = [(ACS, ACS.n)]
        wo_r = self.l1wo_d.rearrange("(kc k) m -> k kc m", k=128)
        res_r = Rot([B.ST[0], B.ST[1]])
        out_r = Rot([B.ST[2], B.ST[3]])
        for g in range(2):
            wa_half = B.WA[:, (g % 2) * 4096:(g % 2) * 4096 + 4096]
            _, ww = self.warot.slot(g)
            t_wl = P.dma("sync", wa_half,
                         wo_r[:, :, g * 512:(g + 1) * 512], SLD, waits=ww)
            for mj in range(4):
                m = g * 4 + mj
                res_t, wr = res_r.slot(m)
                t_res = P.dma("sync", res_t[:], B.RES[m], SLD,
                              waits=wr + [self.x1_done])
                out_t, wo2 = out_r.slot(m)

                def lhsT_fn(_m, k, wa_half=wa_half, t_wl=t_wl, mj=mj):
                    return (wa_half[:, k * 512 + mj * 128: k * 512 + mj * 128 + 128],
                            [(SLD, t_wl)])

                def rhs_fn(k, n):
                    return (B.XA[k][:, n * 512:(n + 1) * 512], attn_rdy)

                def evac(_m, n, ps, t_mm, m=m, res_t=res_t, out_t=out_t,
                         t_res=t_res, wo2=wo2):
                    t_e = P.emit(
                        "vector", lambda e, n=n, ps=ps:
                        e.scalar_tensor_tensor(
                            out_t[:, n * 512:(n + 1) * 512], ps, 1.0,
                            res_t[:, n * 512:(n + 1) * 512],
                            op0=ALU.mult, op1=ALU.add),
                        waits=[(MM, t_mm), (SLD, t_res)] + wo2, inc=EV)
                    return (EV, t_e)

                self.gemm(1, 4, NCH, lhsT_fn, rhs_fn, evac, banks=(2, 3, 4, 5))
                t_ev = EV.n
                res_r.release(m, EV, t_ev)
                t_st = P.dma("gpsimd", B.RES[m], out_t[:], GST,
                             waits=[(EV, t_ev)])
                out_r.release(m, GST, t_st)
            self.warot.release(g, MM, MM.n)
        self.xa_guard = (MM, MM.n)
        self.res_attn1_done = (GST, GST.n)

    def phase_T(self):
        B, P = self, self.P
        SLD, GST, MM, EV, ACS = B.SLD, B.GST, B.MM, B.EV, B.ACS
        if "RESS" in self.dbg:
            RESS = self.dram("RESS", [NCH, 128, T])
            for m in range(NCH):
                P.dma("gpsimd", RESS[m], B.RES[m], GST,
                      waits=[self.res_attn1_done])
        t_w1 = []
        for k in range(NCH):
            t_w1.append(P.dma("sync", B.XA[k][:], self.l1w1t_d
                              .rearrange("(kc k) m -> kc k m", k=128)[k],
                              SLD, waits=[self.xa_guard]))
        if "XAD" in self.dbg:
            self.XAD = self.dram("XAD", [NCH, 128, 2048])
            for k in range(NCH):
                P.dma("gpsimd", self.XAD[k], B.XA[k][:], GST,
                      waits=[(SLD, t_w1[k])])
        xt_r = Rot([B.TW[0], B.TW[1]])
        lg_t, dec_t, dmap_t, scr_t = B.ST[0], B.ST[1], B.ST[2], B.ST[3]
        b1bc = B.ST[4]
        self.t_b1row = P.dma("sync", b1bc[:],
                             self.l1b1_d.squeeze(0).partition_broadcast(128),
                             SLD, waits=[(B.ACS, B.ACS.n), (B.EV, B.EV.n)])
        stg_r = Rot([B.STG[0], B.STG[1]])
        lg_guard = None
        mk_done = None
        for tt in range(NTT):
            xt, wx = xt_r.slot(tt)
            t_x = P.dma("sync", xt[:],
                        B.XN1[:, :, tt * 128:(tt + 1) * 128]
                        .rearrange("c p q -> p c q"), SLD,
                        waits=wx + [self.xn1d_done])
            if "XTD" in self.dbg:
                if not hasattr(self, "XTD"):
                    self.XTD = self.dram("XTD", [NTT, 128, 1024])
                t_xd = P.dma("gpsimd", self.XTD[tt], xt[:], GST,
                             waits=[(SLD, t_x)])
                xt_r.release(tt, GST, t_xd)
            t_lg = None
            for n in range(4):
                t_mm = None
                for k in range(NCH):
                    t_mm = P.emit(
                        "tensor", lambda e, n=n, k=k, xt=xt:
                        e.matmul(B.PS[n][:, 0:512],
                                 xt[:, k * 128:(k + 1) * 128],
                                 B.XA[k][:, n * 512:(n + 1) * 512],
                                 start=(k == 0), stop=(k == NCH - 1)),
                        waits=[(SLD, t_x), (SLD, t_w1[k])]
                        + (self.bankw(n) if k == 0 else []),
                        inc=(MM if k == NCH - 1 else None))
                t_bv = P.emit("vector", lambda e, n=n: e.scalar_tensor_tensor(
                    lg_t[:, n * 512:(n + 1) * 512], B.PS[n][:, 0:512], 1.0,
                    b1bc[:, n * 512:(n + 1) * 512],
                    op0=ALU.mult, op1=ALU.add),
                    waits=[(MM, t_mm), (SLD, self.t_b1row)]
                    + ([lg_guard] if (lg_guard and n == 0) else []),
                    inc=EV)
                t_lg = P.emit("scalar", lambda e, n=n: e.activation(
                    lg_t[:, n * 512:(n + 1) * 512],
                    lg_t[:, n * 512:(n + 1) * 512], AF.Silu),
                    waits=[(EV, t_bv)], inc=ACS)
                self.bank_free[n] = (EV, t_bv)
                if n == 3:
                    xt_r.release(tt, MM, t_mm)
            t_dump = None
            if "LGD" in self.dbg:
                if not hasattr(self, "LGD"):
                    self.LGD = self.dram("LGD", [NTT, 128, 2048])
                t_dump = P.dma("gpsimd", self.LGD[tt], lg_t[:], GST,
                               waits=[(ACS, t_lg)])
            P.wait("vector", [(ACS, t_lg)])
            P.emit("vector", lambda e: e.tensor_scalar(
                dec_t[:], lg_t[:], 0.0, None, op0=ALU.is_gt))
            P.emit("vector", lambda e: e.tensor_scalar(
                scr_t[:], dec_t[:], -1.0, 1.0, op0=ALU.mult, op1=ALU.add))
            P.emit("vector", lambda e: e.memset(dmap_t[:], 0.0))
            dmap3 = dmap_t[:, 0:2040].rearrange("p (t n) -> p t n", n=255)
            dec3 = dec_t[:, 0:2040].rearrange("p (t n) -> p t n", n=255)
            decc3 = scr_t[:, 0:2040].rearrange("p (t n) -> p t n", n=255)
            for tr in range(8):
                P.emit("vector", lambda e, tr=tr: e.memset(
                    dmap_t[:, tr * 255:tr * 255 + 1], 1.0))
            B.spacers()
            for d in range(7):
                lo, hi = 2 ** d - 1, 2 ** (d + 1) - 1
                mcur = dmap3[:, :, lo:hi]
                lvl = dmap3[:, :, hi:hi + 2 ** (d + 1)] \
                    .rearrange("p t (n two) -> p t n two", two=2)
                P.emit("vector", lambda e, lvl=lvl, mcur=mcur, lo=lo, hi=hi:
                       e.tensor_tensor(lvl[:, :, :, 1], mcur,
                                       dec3[:, :, lo:hi], op=ALU.mult))
                P.emit("vector", lambda e, lvl=lvl, mcur=mcur, lo=lo, hi=hi:
                       e.tensor_tensor(lvl[:, :, :, 0], mcur,
                                       decc3[:, :, lo:hi], op=ALU.mult))
                B.spacers(6)
            t_mk = P.emit("vector", lambda e: e.tensor_tensor(
                lg_t[:], lg_t[:], dmap_t[:], op=ALU.mult),
                waits=([(GST, t_dump)] if t_dump else []), inc=EV)
            if "DMAPD" in self.dbg:
                if not hasattr(self, "DMAPD"):
                    self.DMAPD = self.dram("DMAPD", [NTT, 128, 2048])
                t_dump2 = P.dma("gpsimd", self.DMAPD[tt], dmap_t[:], GST,
                                waits=[(EV, t_mk)])
                P.wait("vector", [(GST, t_dump2)])
            for g4 in range(4):
                bi = 4 + (g4 % 2)
                t_mm2 = None
                for j in range(4):
                    kc = g4 * 4 + j
                    t_mm2 = P.emit(
                        "tensor", lambda e, kc=kc, j=j, bi=bi:
                        e.transpose(B.PS[bi][:, j * 128:(j + 1) * 128],
                                    lg_t[:, kc * 128:(kc + 1) * 128],
                                    B.ident[:]),
                        waits=[(EV, t_mk)] + (self.bankw(bi) if j == 0
                                              else []),
                        inc=(MM if j == 3 else None))
                stg, ws = stg_r.slot(tt * 4 + g4)
                t_e = P.emit("scalar", lambda e, stg=stg, bi=bi:
                             e.copy(stg[:, 0:512], B.PS[bi][:]),
                             waits=[(MM, t_mm2)] + ws, inc=ACS)
                self.bank_free[bi] = (ACS, t_e)
                t_st = P.dma("gpsimd",
                             B.MSKT[tt, g4 * 4:g4 * 4 + 4]
                             .rearrange("a p q -> p a q"),
                             stg[:, 0:512].rearrange("p (a q) -> p a q", q=128),
                             GST, waits=[(ACS, t_e)])
                stg_r.release(tt * 4 + g4, GST, t_st)
                mk_done = t_st
            lg_guard = (MM, MM.n)
        part1_mm = (MM, MM.n)
        t_w2 = []
        for k in range(NCH * 2):
            t_w2.append(P.dma(
                "sync", B.XA[k // 2][:, (k % 2) * 1024:(k % 2) * 1024 + 1024],
                self.l1w2t_d.rearrange("(kc k) m -> kc k m", k=128)[k],
                SLD, waits=[part1_mm, (GST, mk_done)]))
        msk_r = Rot([B.ST[0], B.ST[1], B.ST[2], B.ST[3]])
        res_r = Rot([B.TW[0], B.TW[1]])
        stg2_r = Rot([B.STG[0][:, 0:512], B.STG[0][:, 512:1024],
                      B.STG[1][:, 0:512], B.STG[1][:, 512:1024]])
        si = 0
        for tg in range(4):
            t_mks = []
            for kq in range(4):
                mk_t, wm = msk_r.slot(kq)
                for kk in range(4):
                    k = kq * 4 + kk
                    t_mks.append(P.dma(
                        "sync",
                        mk_t[:, kk * 512:(kk + 1) * 512]
                        .rearrange("p (a q) -> p a q", q=128),
                        B.MSKT[tg * 4:(tg + 1) * 4, k]
                        .rearrange("a p q -> p a q"), B.VST,
                        waits=(wm if kk == 0 else []) + [(GST, mk_done)]))
            for m in range(NCH):
                bi = 6 + (m % 2)
                t_mm = None
                for k in range(16):
                    mk_t, _ = msk_r.slot(k // 4)
                    t_mm = P.emit(
                        "tensor",
                        lambda e, k=k, m=m, bi=bi, mk_t=mk_t,
                        s=(k == 0), t=(k == 15):
                        e.matmul(B.PS[bi][:],
                                 B.XA[k // 2][:, (k % 2) * 1024 + m * 128:
                                              (k % 2) * 1024 + m * 128 + 128],
                                 mk_t[:, (k % 4) * 512:(k % 4) * 512 + 512],
                                 start=s, stop=t),
                        waits=[(B.VST, t_mks[-1]), (SLD, t_w2[-1])]
                        + (self.bankw(bi) if k == 0 else []),
                        inc=(MM if k == 15 else None))
                stg, ws = stg2_r.slot(si)
                t_res = P.dma("sync", stg,
                              B.RES[m][:, tg * 512:(tg + 1) * 512], SLD,
                              waits=ws + [self.res_attn1_done,
                                          (GST, mk_done)])
                t_a = P.emit("vector", lambda e, bi=bi, stg=stg:
                             e.scalar_tensor_tensor(
                                 stg, B.PS[bi][:, 0:512], 1.0, stg,
                                 op0=ALU.mult, op1=ALU.add),
                             waits=[(MM, t_mm), (SLD, t_res)], inc=EV)
                self.bank_free[bi] = (EV, t_a)
                t_st = P.dma("gpsimd", B.RES[m][:, tg * 512:(tg + 1) * 512],
                             stg, GST, waits=[(EV, t_a)])
                stg2_r.release(si, GST, t_st)
                si += 1
            for kq in range(4):
                msk_r.release(kq, MM, MM.n)
        self.xa_guard = (MM, MM.n)
        self.final_done = (GST, GST.n)

    def phase_U(self):
        B, P = self, self.P
        SLD, GST, MM, EV, ACS = B.SLD, B.GST, B.MM, B.EV, B.ACS
        t_lds = []
        for j in range(NCH):
            t_lds.append(P.dma("sync", B.XA[j][:], B.RES[j], SLD,
                               waits=[self.xa_guard, self.final_done]))
        stg_r = Rot([B.STG[0], B.STG[1]])
        for tt in range(NTT):
            cs = slice(tt * 128, tt * 128 + 128)
            t_mm = None
            for j in range(NCH):
                bi = j // 4
                ps = B.PS[bi][:, (j % 4) * 128:(j % 4) * 128 + 128]
                t_mm = P.emit("tensor", lambda e, j=j, ps=ps, cs=cs:
                              e.transpose(ps, B.XA[j][:, cs], B.ident[:]),
                              waits=[(SLD, t_lds[j])]
                              + (self.bankw(bi) if j % 4 == 0 else []),
                              inc=MM)
            stg, ws = stg_r.slot(tt)
            t_e = None
            for bi in range(2):
                t_e = P.emit("scalar", lambda e, bi=bi, stg=stg:
                             e.copy(stg[:, bi * 512:bi * 512 + 512],
                                    B.PS[bi][:]),
                             waits=[(MM, t_mm)] + ws, inc=ACS)
                self.bank_free[bi] = (ACS, t_e)
            t_st = P.dma("gpsimd", self.out_d[tt], stg[:], GST,
                         waits=[(ACS, t_e)])
            stg_r.release(tt, GST, t_st)
        P.wait("gpsimd", [(GST, GST.n)])

    def finalize_debug(self):
        B, P = self, self.P
        P.wait("gpsimd", [(B.GST, B.GST.n), (B.SLD, B.SLD.n)])
        t = P.dma("gpsimd", B.out_d[0], B.TW[0][:], B.GST)
        P.wait("gpsimd", [(B.GST, t)])
        self.assemble()

    def assemble(self):
        nc, P = self.nc, self.P
        with nc.Block() as block:
            @block.tensor
            def _(e):
                for f in P.stream["tensor"]:
                    f(e)

            @block.vector
            def _(e):
                for f in P.stream["vector"]:
                    f(e)

            @block.scalar
            def _(e):
                for f in P.stream["scalar"]:
                    f(e)

            @block.gpsimd
            def _(e):
                for f in P.stream["gpsimd"]:
                    f(e)

            @block.sync
            def _(e):
                for f in P.stream["sync"]:
                    f(e)
        self.es.close()
        return nc


def make_inmaps(inputs):
    inp = {k: np.ascontiguousarray(np.asarray(v, dtype=np.float32))
           for k, v in inputs.items()}
    shared = {}
    shared["ident"] = np.eye(128, dtype=np.float32)
    shared["l0a_w_in"] = inp["L0_attn_W_in"]
    shared["l0a_gates_w"] = np.stack([inp["L0_attn_Wa"], inp["L0_attn_Wb"],
                                      inp["L0_attn_Wc"], inp["L0_attn_Wd"]])
    shared["l0a_w_out"] = inp["L0_attn_W_out"]
    shared["l0a_b_in"] = np.ascontiguousarray(
        inp["L0_attn_b_in"].reshape(16, 128).T)

    def vec(v):
        return np.ascontiguousarray(np.asarray(v, np.float32).reshape(NCH, 128).T)
    sv = np.zeros((16, 128, NCH), np.float32)
    sv[0] = vec(inp["L0_attn_ln_g"])
    sv[1] = vec(inp["L0_attn_ln_b"])
    sv[2] = vec(inp["L0_ffn_ln1_g"])
    sv[3] = vec(inp["L0_ffn_ln1_b"])
    sv[4] = vec(inp["L0_attn_a0"] + inp["L0_attn_ba"])
    sv[5] = vec(inp["L0_attn_b0"] + inp["L0_attn_bb"])
    sv[6] = vec(inp["L0_attn_c0"] + inp["L0_attn_bc"])
    sv[7] = vec(inp["L0_attn_d0"] + inp["L0_attn_bd"])
    sv[8] = vec(inp["L0_attn_b_out"])
    sv[9] = vec(inp["L0_attn_conv_b"])
    sv[10] = vec(inp["L0_ffn_ln2_g"])
    sv[11] = vec(inp["L0_ffn_ln2_b"])
    shared["l0a_smallv"] = sv
    shared["l0a_convk"] = np.ascontiguousarray(
        inp["L0_attn_conv_k"].reshape(5, NCH, 128).transpose(2, 1, 0))
    shared["l0f_wq"] = inp["L0_ffn_wq"]
    shared["l0f_wk"] = inp["L0_ffn_wk"]
    shared["l0f_wv"] = inp["L0_ffn_wv"]
    shared["l0f_wo"] = inp["L0_ffn_wo"]
    shared["l0f_wp"] = inp["L0_ffn_wp"]
    shared["l0f_selw"] = inp["L0_ffn_sel_w"]
    shared["l0f_keys"] = inp["L0_ffn_keys"]
    shared["l0f_values"] = inp["L0_ffn_values"]
    shared["l1_wq"] = np.ascontiguousarray(inp["L1_attn_w_aq"].reshape(D, D))
    shared["l1_wk"] = np.ascontiguousarray(inp["L1_attn_w_ak"].reshape(D, D))
    shared["l1_wv"] = np.ascontiguousarray(inp["L1_attn_w_av"].reshape(D, D))
    shared["l1_wo"] = np.ascontiguousarray(inp["L1_attn_w_ao"].reshape(D, D))
    w1t = np.zeros((D, 2048), np.float32)
    w1t[:, 0:2040] = inp["L1_ffn_W_in"].T
    shared["l1_w1t"] = w1t
    b1 = np.zeros((1, 2048), np.float32)
    b1[0, 0:2040] = inp["L1_ffn_b_in"]
    shared["l1_b1"] = b1
    w2t = np.zeros((2048, D), np.float32)
    w2t[0:2040, :] = inp["L1_ffn_W_out"].T
    shared["l1_w2t"] = w2t
    rr = np.arange(128)[:, None]
    cc = np.arange(896)[None, :]
    shared["l1_mask"] = np.where(cc > rr + 384, np.float32(-1e30),
                                 np.float32(0.0)).astype(np.float32)
    shared["ones_r"] = np.ones((1, 128), np.float32)

    in_maps = []
    for b in range(8):
        m = dict(shared)
        m["x_tm"] = np.ascontiguousarray(
            inp["L0_attn_x"][b].reshape(NTT, 128, D))
        in_maps.append(m)
    return in_maps


import os


def build_full(dbg=(), upto=None):
    upto = upto or os.environ.get("BK_UPTO", "Z")
    B = Builder(dbg=dbg, paranoid=bool(int(os.environ.get("BK_PARANOID", "0"))))
    seq = [("A", B.phase_A), ("C", B.phase_C), ("E", B.phase_DE),
           ("F", B.phase_F), ("G", B.phase_G), ("H", B.phase_H),
           ("I", B.phase_I), ("J", B.phase_J), ("K", B.phase_K),
           ("L", B.phase_L), ("M", B.phase_M), ("N", B.phase_N),
           ("O", B.phase_O), ("P", B.phase_PQ), ("R", B.phase_R),
           ("S", B.phase_S), ("T", B.phase_T), ("U", B.phase_U)]
    done_all = True
    for name, fn in seq:
        fn()
        if name == upto:
            done_all = (name == "U")
            break
    if done_all:
        B.assemble()
    else:
        B.finalize_debug()
    return B.nc


def kernel(**inputs):
    in_maps = make_inmaps(inputs)
    nc = build_full()
    res = run_bass_kernel_spmd(nc, in_maps, core_ids=list(range(8)))
    out = np.stack([np.asarray(r["out"]).reshape(T, D) for r in res.results])
    return out


def kernel_debug(inputs, dbg, cores=(0,), upto=None):
    in_maps = make_inmaps(inputs)
    nc = build_full(dbg=dbg, upto=upto)
    res = run_bass_kernel_spmd(nc, [in_maps[c] for c in cores],
                               core_ids=list(cores))
    return [{k: np.asarray(r[k]) for k in r} for r in res.results]


# version 09
# speedup vs baseline: 1.0402x; 1.0402x over previous
import numpy as np
from contextlib import ExitStack

import concourse.bass as bass
import concourse.mybir as mybir
from concourse.bass_utils import run_bass_kernel_spmd

F32 = mybir.dt.float32
AF = mybir.ActivationFunctionType
ALU = mybir.AluOpType

T = 2048
D = 1024
NCH = 8
NTT = 16


class Sem:
    __slots__ = ("h", "n", "name")

    def __init__(self, h, name):
        self.h, self.n, self.name = h, 0, name

    def bump(self, k=1):
        self.n += k
        return self.n


class Prog:
    ENGS = ("tensor", "vector", "scalar", "gpsimd", "sync")

    def __init__(self, nc, es):
        self.nc = nc
        self.es = es
        self.stream = {e: [] for e in self.ENGS}
        self.waited = {e: {} for e in self.ENGS}
        self._semidx = 0
        self.sems = []
        self.paranoid = False

    def sem(self, name=None):
        name = name or f"s{self._semidx}"
        self._semidx += 1
        h = self.es.enter_context(self.nc.semaphore(name))
        s = Sem(h, name)
        self.sems.append(s)
        return s

    def emit(self, eng, fn, waits=(), inc=None, inc_k=1):
        if self.paranoid:
            if not hasattr(self, "tick"):
                self.tick = {e: self.sem(f"tick_{e}") for e in self.ENGS}
            waits = (list(waits)
                     + [(s, s.n) for s in self.sems
                        if s.n > 0 and not s.name.startswith("tick_")]
                     + [(self.tick[e2], self.tick[e2].n)
                        for e2 in self.ENGS if e2 != eng])
        wf = []
        for wv in waits:
            if wv is None:
                continue
            s, v = wv
            if v is None:
                continue
            if self.waited[eng].get(s.name, -1) < v:
                self.waited[eng][s.name] = v
                wf.append((s.h, v))
        tgt = inc.bump(inc_k) if inc is not None else None

        def closure(e, wf=wf, fn=fn, inc=inc, inc_k=inc_k):
            for (h, v) in wf:
                e.wait_ge(h, v)
            inst = fn(e)
            if inc is not None:
                inst.then_inc(inc.h, inc_k)
        self.stream[eng].append(closure)
        if self.paranoid:
            tk = self.tick[eng]
            v = tk.bump()
            self.stream[eng].append(lambda e, h=tk.h: e.sem_inc(h, 1))
        return tgt

    def wait(self, eng, waits):
        for wv in waits:
            if wv is None:
                continue
            s, v = wv
            if v is None:
                continue
            if self.waited[eng].get(s.name, -1) < v:
                self.waited[eng][s.name] = v
                self.stream[eng].append(lambda e, h=s.h, v=v: e.wait_ge(h, v))

    def dma(self, eng, out_ap, in_ap, sem, waits=(), **kw):
        return self.emit(eng, lambda e: e.dma_start(out_ap, in_ap, **kw),
                         waits=waits, inc=sem, inc_k=16)


class Rot:

    def __init__(self, bufs):
        self.bufs = list(bufs)
        self.n = len(self.bufs)
        self.free = [None] * self.n

    def slot(self, i):
        return self.bufs[i % self.n], ([self.free[i % self.n]]
                                       if self.free[i % self.n] else [])

    def release(self, i, sem, val):
        self.free[i % self.n] = (sem, val)


class Builder:
    def __init__(self, dbg=(), phase_limit=99, paranoid=False):
        self.dbg = set(dbg)
        self.phase_limit = phase_limit
        nc = self.nc = bass.Bass("TRN2", target_bir_lowering=False)
        es = self.es = ExitStack()
        self.P = Prog(nc, es)
        self.P.paranoid = paranoid

        ein = lambda n, s: nc.dram_tensor(n, s, F32, kind="ExternalInput").ap()
        self.x_tm = ein("x_tm", [NTT, 128, D])
        self.ident_d = ein("ident", [128, 128])
        self.w_in_d = ein("l0a_w_in", [D, 2 * D])
        self.gates_w_d = ein("l0a_gates_w", [4, D, D])
        self.w_out_d = ein("l0a_w_out", [D, D])
        self.smallv_d = ein("l0a_smallv", [16, 128, NCH])
        self.convk_d = ein("l0a_convk", [128, NCH, 5])
        self.b_in_d = ein("l0a_b_in", [128, 16])
        self.wq_d = ein("l0f_wq", [D, D])
        self.wk_d = ein("l0f_wk", [D, D])
        self.wv_d = ein("l0f_wv", [D, D])
        self.wo_d = ein("l0f_wo", [D, D])
        self.wp_d = ein("l0f_wp", [D, D])
        self.selw_d = ein("l0f_selw", [D, 32])
        self.keys_d = ein("l0f_keys", [32, D, 128])
        self.values_d = ein("l0f_values", [32, 128, D])
        self.l1wq_d = ein("l1_wq", [D, D])
        self.l1wk_d = ein("l1_wk", [D, D])
        self.l1wv_d = ein("l1_wv", [D, D])
        self.l1wo_d = ein("l1_wo", [D, D])
        self.l1w1t_d = ein("l1_w1t", [D, 2048])
        self.l1b1_d = ein("l1_b1", [1, 2048])
        self.l1w2t_d = ein("l1_w2t", [2048, D])
        self.mask_d = ein("l1_mask", [128, 896])
        self.ones_d = ein("ones_r", [1, 128])

        def dram(name, shape):
            k = "ExternalOutput" if name in self.dbg else "Internal"
            return nc.dram_tensor(name, shape, F32, kind=k).ap()
        self.XN = dram("XN", [NCH, 128, T])
        self.RES = dram("RES", [NCH, 128, T])
        self.XQ1 = dram("XQ1", [NCH, 128, T])
        self.XPROJ = dram("XPROJ", [NCH, 128, T])
        self.SIGZ = dram("SIGZ", [NCH, 128, T])
        self.YGD = dram("YGD", [NCH, 128, T])
        self.Q0 = dram("Q0", [NCH, 128, T])
        self.K0 = dram("K0", [NCH, 128, T])
        self.V0 = dram("V0", [NCH, 128, T])
        self.PREV = dram("PREV", [NCH, 128, 4095])
        self.BDW = nc.dram_tensor("BDW", [2 * 278400], F32, kind="Internal")
        self.A0TM = dram("A0TM", [NTT, 128, D])
        self.SRC2 = dram("SRC2", [NCH, 128, T])
        self.X2D = dram("X2D", [NCH, 128, T])
        self.XN1 = dram("XN1", [NCH, 128, T])
        self.Q1 = dram("Q1", [NCH, 128, T])
        self.K1 = dram("K1", [NCH, 128, T])
        self.V1 = dram("V1", [NCH, 128, T])
        self.Q1R = dram("Q1R", [NCH, 128, T])
        self.K1R = dram("K1R", [NCH, 128, T])
        self.A1TM = dram("A1TM", [NTT, 128, D])
        self.MSKT = dram("MSKT", [NTT, 16, 128, 128])
        self.XN2D = dram("XN2D", [NCH, 128, T]) if "XN2D" in self.dbg else None
        self.warot = Rot([0, 1])
        self.pos_free = None
        self.pt_guard = None
        self.hg_guard = None
        self.dram = dram

        self.out_d = nc.dram_tensor("out", [NTT, 128, D], F32,
                                    kind="ExternalOutput").ap()

        sb = self.sb = lambda n, s: es.enter_context(nc.sbuf_tensor(n, s, F32))
        self.ident = sb("identS", [128, 128])
        self.smallv = sb("smallvS", [128, 16 * NCH])
        self.convk = sb("convkS", [128, NCH * 5])
        self.b_in_s = sb("binS", [128, 16])
        self.XA = [sb(f"xa{i}", [128, T]) for i in range(NCH)]
        self.ST = [sb(f"st{i}", [128, T]) for i in range(8)]
        self.WA = sb("wa", [128, 8192])
        self.PR = sb("pr", [128, 4160])
        self.STG = [sb(f"stg{i}", [128, 1024]) for i in range(4)]
        self.TW = [sb(f"tw{i}", [128, 1024]) for i in range(2)]
        self.bn6 = sb("bn6", [128, 12])
        self.stat = sb("stat", [128, 4])
        self.vsm = sb("vsm", [128, 16])
        self.selw_s = sb("selwS", [128, 256])
        self.ones_sb = sb("onesS", [1, 128])
        self.gate_sb = sb("gateS", [128, 512])
        self.ki32 = es.enter_context(
            nc.sbuf_tensor("ki32S", [128, 512], mybir.dt.int32))
        self.epsc = sb("epscS", [128, 1])
        self.PS = [es.enter_context(nc.psum_tensor(f"ps{i}", [128, 512], F32))
                   for i in range(8)]
        self.bank_free = [None] * 8

        P = self.P
        self.SLD = P.sem("SLD")
        self.GST = P.sem("GST")
        self.MM = P.sem("MM")
        self.EV = P.sem("EV")
        self.ACS = P.sem("ACS")
        self.GLD = P.sem("GLD")
        self.VST = P.sem("VST")
        self.GC = P.sem("GC")

        self.SVI = {n: i for i, n in enumerate(
            ["ln_g", "ln_b", "g1", "b1", "a0ba", "b0bb", "c0bc", "d0bd",
             "b_out", "conv_b", "g2", "b2", "sv12", "sv13", "sv14", "sv15"])}

        self.t_ident = P.dma("sync", self.ident[:], self.ident_d, self.SLD)
        self.t_smallv = P.dma("sync", self.smallv[:],
                              self.smallv_d.rearrange("v p c -> p v c"), self.SLD)
        P.dma("sync", self.convk[:], self.convk_d, self.SLD)
        self.t_consts = P.dma("sync", self.b_in_s[:], self.b_in_d, self.SLD)
        self.t_ones = P.dma("sync", self.ones_sb[:], self.ones_d, self.SLD)
        P.emit("vector", lambda e: e.memset(self.vsm[:], 0.0))
        P.emit("vector", lambda e: e.memset(self.epsc[:], 1e-5))
        P.emit("vector", lambda e: e.memset(self.stat[:], 0.0))
        P.emit("vector", lambda e: e.memset(self.bn6[:], 0.0))

    def spacers(self, n=4, col=4):
        self.P.emit("vector", lambda e: e.drain())

    def svec(self, name, ch):
        i = self.SVI[name]
        return self.smallv[:, i * NCH + ch: i * NCH + ch + 1]

    def bankw(self, i):
        f = self.bank_free[i]
        return [f] if f else []

    def emit_ln_stats_apply(self, x_ap, out_ap, rdy, extra=()):
        P, EV, ACS, ALUx = self.P, self.EV, self.ACS, ALU
        stat, bn6 = self.stat, self.bn6
        P.emit("vector", lambda e: e.bn_stats(bn6[:, 0:6], x_ap[:, 0:512]),
               waits=list(rdy) + list(extra))
        P.emit("vector", lambda e: e.bn_stats(bn6[:, 6:12], x_ap[:, 512:1024]))
        P.emit("vector", lambda e: e.drain())
        t_ag = P.emit("vector", lambda e: e.bn_aggr(
            stat[:, 0:2], bn6[:].rearrange("p (a b) -> p a b", b=3)), inc=EV)
        t_ve = P.emit("scalar", lambda e: e.activation(
            stat[:, 2:3], stat[:, 1:2], AF.Identity, bias=self.epsc[:]),
            waits=[(EV, t_ag)], inc=ACS)
        t_rc = P.emit("vector", lambda e: e.reciprocal(
            stat[:, 3:4], stat[:, 2:3]), waits=[(ACS, t_ve)], inc=EV)
        t_sq = P.emit("scalar", lambda e: e.activation(
            self.vsm[:, 1:2], stat[:, 3:4], AF.Sqrt),
            waits=[(EV, t_rc)], inc=ACS)
        P.wait("vector", [(ACS, t_sq)])
        return P.emit("vector", lambda e: e.tensor_scalar(
            out_ap, x_ap, stat[:, 0:1], self.vsm[:, 1:2],
            op0=ALUx.subtract, op1=ALUx.mult), inc=EV)

    def emit_transpose_1024(self, src_tile, rdy, banks, tt):
        P, MM = self.P, self.MM
        t_mm = None
        for j in range(NCH):
            bi = banks[j // 4]
            ps = self.PS[bi][:, (j % 4) * 128:(j % 4) * 128 + 128]
            w = self.bankw(bi) if j % 4 == 0 else []
            if tt == 0 and j == 0:
                w = w + [(self.SLD, self.t_ident)]
            t_mm = P.emit("tensor", lambda e, ps=ps, j=j:
                          e.transpose(ps, src_tile[:, j * 128:(j + 1) * 128],
                                      self.ident[:]),
                          waits=list(rdy) + w, inc=MM)
        return t_mm

    def phase_A(self):
        B, P = self, self.P
        SLD, GST, MM, EV, ACS = B.SLD, B.GST, B.MM, B.EV, B.ACS
        xt_r = Rot(B.TW[:2])
        xn_r = Rot([B.ST[0][:, 0:1024], B.ST[1][:, 0:1024]])
        nr2_r = Rot([B.ST[2][:, 0:1024], B.ST[3][:, 0:1024]])
        res_r = Rot([B.ST[4][:, 0:1024], B.ST[5][:, 0:1024]])
        stg_r = Rot(B.STG)

        LNDBG = B.dram("LNDBG", [NTT, 128, 8]) if "LNDBG" in B.dbg else None
        XNTM = B.dram("XNTM", [NTT, 128, 1024]) if "XNTM" in B.dbg else None
        for tt in range(NTT):
            cs = slice(tt * 128, tt * 128 + 128)
            xt, w = xt_r.slot(tt)
            t_ld = P.dma("sync", xt[:], B.x_tm[tt], SLD, waits=w)
            xn_t, w2 = xn_r.slot(tt)
            t_xn = B.emit_ln_stats_apply(xt[:], xn_t, [(SLD, t_ld)] + w2)
            if LNDBG is not None:
                P.dma("gpsimd", LNDBG[tt, :, 0:4], B.stat[:], GST,
                      waits=[(EV, t_xn)])
            if XNTM is not None:
                P.dma("gpsimd", XNTM[tt], xn_t, GST, waits=[(EV, t_xn)])
            res_t, w3 = res_r.slot(tt)
            t_res = P.emit("vector", lambda e, xt=xt, xn_t=xn_t, res_t=res_t:
                           e.scalar_tensor_tensor(res_t, xn_t, 1.0, xt[:],
                                                  op0=ALU.mult, op1=ALU.add),
                           waits=w3, inc=EV)
            xt_r.release(tt, EV, t_res)
            nr2_t, w4 = nr2_r.slot(tt)
            t_nr2 = B.emit_ln_stats_apply(xn_t, nr2_t, w4)

            t_mm = B.emit_transpose_1024(xn_t, [(EV, t_nr2)], (0, 1), tt)
            xn_r.release(tt, MM, t_mm)
            stg, ws = stg_r.slot(tt * 4 + 0)
            t_e = None
            for half in range(2):
                t_e = P.emit("scalar", lambda e, stg=stg, half=half:
                             e.copy(stg[:, half * 512:half * 512 + 512],
                                    B.PS[half][:]),
                             waits=[(MM, t_mm)] + ws + B.bankw(half), inc=ACS)
                self.bank_free[half] = (ACS, t_e)
            t_st = P.dma("gpsimd", B.XN[:, :, cs].rearrange("c p q -> p c q"),
                         stg[:], GST, waits=[(ACS, t_e)])
            stg_r.release(tt * 4 + 0, GST, t_st)

            t_mm = B.emit_transpose_1024(res_t, [(EV, t_res)], (2, 3), tt)
            res_r.release(tt, MM, t_mm)
            stg, ws = stg_r.slot(tt * 4 + 1)
            for half in range(2):
                t_e = P.emit("scalar", lambda e, stg=stg, half=half:
                             e.copy(stg[:, half * 512:half * 512 + 512],
                                    B.PS[2 + half][:]),
                             waits=[(MM, t_mm)] + ws + B.bankw(2 + half), inc=ACS)
                self.bank_free[2 + half] = (ACS, t_e)
            t_st = P.dma("gpsimd", B.RES[:, :, cs].rearrange("c p q -> p c q"),
                         stg[:], GST, waits=[(ACS, t_e)])
            stg_r.release(tt * 4 + 1, GST, t_st)

            t_mm = B.emit_transpose_1024(nr2_t, [(EV, t_nr2)], (4, 5), tt)
            nr2_r.release(tt, MM, t_mm)
            stg, ws = stg_r.slot(tt * 4 + 2)
            P.wait("vector", ws + [(SLD, B.t_smallv)])
            t_e1 = t_e2 = None
            for j in range(NCH):
                bi = 4 + j // 4
                psb = B.PS[bi][:, (j % 4) * 128:(j % 4) * 128 + 128]
                t_e1 = P.emit("vector", lambda e, j=j, psb=psb, cs=cs:
                              e.tensor_scalar(B.XA[j][:, cs], psb,
                                              B.svec("ln_g", j), B.svec("ln_b", j),
                                              op0=ALU.mult, op1=ALU.add),
                              waits=[(MM, t_mm)], inc=EV)
                t_e2 = P.emit("vector", lambda e, stg=stg, j=j, psb=psb:
                              e.tensor_scalar(stg[:, j * 128:(j + 1) * 128], psb,
                                              B.svec("g1", j), B.svec("b1", j),
                                              op0=ALU.mult, op1=ALU.add),
                              inc=EV)
            for bi in (4, 5):
                self.bank_free[bi] = (EV, t_e2)
            t_st = P.dma("gpsimd", B.XQ1[:, :, cs].rearrange("c p q -> p c q"),
                         stg[:], GST, waits=[(EV, t_e2)])
            stg_r.release(tt * 4 + 2, GST, t_st)
            if B.XN2D is not None:
                P.dma("gpsimd", B.XN2D[:, :, cs].rearrange("c p q -> p c q"),
                      stg[:], GST)
        self.xa_ready = EV.n
        self.t_phaseA_st = GST.n

    def gemm(self, nm, nn, nk, lhsT_fn, rhs_fn, evac_fn, banks=(0, 1, 2, 3),
             nfree=512):
        P, MM = self.P, self.MM
        for m in range(nm):
            for n in range(nn):
                bi = banks[(m * nn + n) % len(banks)]
                ps = self.PS[bi][:, 0:nfree]
                t_mm = None
                for k in range(nk):
                    la, lw = lhsT_fn(m, k)
                    ra, rw = rhs_fn(k, n)
                    w = list(lw) + list(rw) + (self.bankw(bi) if k == 0 else [])
                    t_mm = P.emit(
                        "tensor",
                        lambda e, ps=ps, la=la, ra=ra, s=(k == 0), t=(k == nk - 1):
                        e.matmul(ps, la, ra, start=s, stop=t),
                        waits=w, inc=(MM if k == nk - 1 else None))
                self.bank_free[bi] = evac_fn(m, n, ps, t_mm)

    def phase_C(self):
        B, P = self, self.P
        SLD, GST, MM, EV, ACS = B.SLD, B.GST, B.MM, B.EV, B.ACS
        xa_rdy = [(EV, self.xa_ready)]
        w_in_r = self.w_in_d.rearrange("(kc k) n -> k kc n", k=128)
        stg_r = Rot([B.TW[0][:, 0:512], B.TW[0][:, 512:1024],
                     B.TW[1][:, 0:512], B.TW[1][:, 512:1024]])
        si = 0
        for g in range(4):
            wa_half = B.WA[:, (g % 2) * 4096:(g % 2) * 4096 + 4096]
            _, ww = self.warot.slot(g)
            t_wl = P.dma("sync", wa_half, w_in_r[:, :, g * 512:(g + 1) * 512],
                         SLD, waits=ww)

            def lhsT_fn(m, k, wa_half=wa_half, t_wl=t_wl):
                return (wa_half[:, k * 512 + m * 128: k * 512 + m * 128 + 128],
                        [(SLD, t_wl)])

            def rhs_fn(k, n):
                return (B.XA[k][:, n * 512:(n + 1) * 512], xa_rdy)

            def evac(m, n, ps, t_mm, g=g):
                nonlocal si
                gm = g * 4 + m
                stg, ws = stg_r.slot(si)
                func = AF.Identity if gm < 8 else AF.Sigmoid
                t_e = P.emit("scalar", lambda e, stg=stg, ps=ps, func=func, gm=gm:
                             e.activation(stg, ps, func,
                                          bias=B.b_in_s[:, gm:gm + 1]),
                             waits=[(MM, t_mm)] + ws, inc=ACS)
                dst = (B.XPROJ[gm] if gm < 8 else B.SIGZ[gm - 8])
                t_st = P.dma("gpsimd", dst[:, n * 512:(n + 1) * 512], stg,
                             GST, waits=[(ACS, t_e)])
                stg_r.release(si, GST, t_st)
                si += 1
                return (ACS, t_e)

            self.gemm(4, 4, NCH, lhsT_fn, rhs_fn, evac)
            self.warot.release(g, MM, MM.n)
        self.xa_guard = (MM, MM.n)
        self.t_xproj_done = B.GST.n

    def phase_DE(self):
        B, P = self, self.P
        SLD, GST, MM, EV, ACS = B.SLD, B.GST, B.MM, B.EV, B.ACS
        xp_r = Rot([B.ST[0], B.ST[1]])
        for j in range(NCH):
            xp, w = xp_r.slot(j)
            t_ld = P.dma("sync", xp[:], B.XPROJ[j], SLD,
                         waits=w + [(GST, self.t_xproj_done)])
            k = lambda s, j=j: B.convk[:, j * 5 + s: j * 5 + s + 1]
            wv = [(SLD, t_ld)] + ([self.xa_guard] if j == 0 else [])
            P.emit("vector", lambda e, xp=xp, j=j, k=k: e.tensor_scalar(
                B.XA[j][:], xp[:], k(2), B.svec("conv_b", j),
                op0=ALU.mult, op1=ALU.add), waits=wv)
            for s, (o0, o1, i0, i1) in (
                    (0, (2, 2048, 0, 2046)), (1, (1, 2048, 0, 2047)),
                    (3, (0, 2047, 1, 2048)), (4, (0, 2046, 2, 2048))):
                P.emit("vector", lambda e, xp=xp, j=j, k=k, s=s, o0=o0, o1=o1,
                       i0=i0, i1=i1: e.scalar_tensor_tensor(
                           B.XA[j][:, o0:o1], xp[:, i0:i1], k(s),
                           B.XA[j][:, o0:o1], op0=ALU.mult, op1=ALU.add))
            t_cv = P.emit("vector", lambda e: e.tensor_scalar(
                B.vsm[:, 0:1], B.vsm[:, 0:1], 1.0, None, op0=ALU.mult), inc=EV)
            xp_r.release(j, EV, t_cv)
        self.xconv_ready = (EV, EV.n)

        gw_r = self.gates_w_d
        st_xp, st_sz = B.ST[0], B.ST[1]
        st_g = [B.ST[2], B.ST[3], B.ST[4], B.ST[5]]
        gfuncs = [AF.Tanh, AF.Sigmoid, AF.Sigmoid, AF.Identity]
        gbias = ["a0ba", "b0bb", "c0bc", "d0bd"]
        last_scan = None
        for j in range(NCH):
            wh = self.warot
            wa_half, ww = wh.slot(j)
            wa_half = B.WA[:, (j % 2) * 4096:(j % 2) * 4096 + 4096]
            t_wl = None
            for g in range(4):
                t_wl = P.dma(
                    "sync", wa_half[:, g * 1024:(g + 1) * 1024],
                    gw_r[g].rearrange("(kc k) m -> k kc m", k=128)
                    [:, :, j * 128:(j + 1) * 128], SLD,
                    waits=ww if g == 0 else [])
            wxp = ([(EV, last_scan)] if last_scan else [self.xconv_ready])
            wxp = wxp + [(GST, self.t_xproj_done)]
            t_xp = P.dma("gpsimd", st_xp[:], B.XPROJ[j], B.GLD, waits=wxp)
            t_sz = P.dma("gpsimd", st_sz[:], B.SIGZ[j], B.GLD)

            def lhsT_fn(m, k, wa_half=wa_half, t_wl=t_wl):
                return (wa_half[:, m * 1024 + k * 128: m * 1024 + k * 128 + 128],
                        [(SLD, t_wl)])

            def rhs_fn(k, n):
                return (B.XA[k][:, n * 512:(n + 1) * 512], [self.xconv_ready])

            def evac(m, n, ps, t_mm, j=j):
                t_e = P.emit("scalar", lambda e, m=m, n=n, ps=ps, j=j:
                             e.activation(st_g[m][:, n * 512:(n + 1) * 512], ps,
                                          gfuncs[m],
                                          bias=B.svec(gbias[m], j)),
                             waits=[(MM, t_mm)] + ([(EV, last_scan)]
                                                   if (last_scan and m == 0 and n == 0) else []),
                             inc=ACS)
                return (ACS, t_e)

            self.gemm(4, 4, NCH, lhsT_fn, rhs_fn, evac)
            wh.release(j, MM, MM.n)
            t_gates = (ACS, ACS.n)
            wv = [t_gates, (B.GLD, t_sz)]
            P.emit("vector", lambda e: e.tensor_tensor(
                st_g[1][:], st_g[1][:], st_xp[:], op=ALU.mult), waits=wv)
            P.emit("vector", lambda e: e.tensor_tensor(
                st_g[3][:], st_g[3][:], st_xp[:], op=ALU.mult))
            P.emit("vector", lambda e: e.tensor_tensor_scan(
                st_xp[:], st_g[0][:], st_g[1][:], 0.0,
                op0=ALU.mult, op1=ALU.add))
            P.emit("vector", lambda e: e.tensor_tensor(
                st_g[2][:], st_g[2][:], st_xp[:], op=ALU.mult))
            P.emit("vector", lambda e: e.tensor_tensor(
                st_g[2][:], st_g[2][:], st_g[3][:], op=ALU.add))
            t_yg = P.emit("vector", lambda e: e.tensor_tensor(
                st_g[2][:], st_g[2][:], st_sz[:], op=ALU.mult), inc=EV)
            t_st = P.dma("gpsimd", B.YGD[j], st_g[2][:], GST,
                         waits=[(EV, t_yg)])
            last_scan = t_yg
        self.yg_done = (B.GST, B.GST.n)
        self.xa_guard = (MM, MM.n)

    def phase_F(self):
        B, P = self, self.P
        SLD, GST, MM, EV, ACS = B.SLD, B.GST, B.MM, B.EV, B.ACS
        t_lds = []
        for j in range(NCH):
            t_lds.append(P.dma("sync", B.XA[j][:], B.YGD[j], SLD,
                               waits=[self.xa_guard, self.yg_done]))
        w_out_r = self.w_out_d.rearrange("(kc k) m -> k kc m", k=128)
        res_r = Rot([B.ST[0], B.ST[1]])
        out_r = Rot([B.ST[2], B.ST[3]])
        for g in range(2):
            wa_half, ww = self.warot.slot(g)
            wa_half = B.WA[:, (g % 2) * 4096:(g % 2) * 4096 + 4096]
            t_wl = P.dma("sync", wa_half,
                         w_out_r[:, :, g * 512:(g + 1) * 512], SLD, waits=ww)
            for mj in range(4):
                m = g * 4 + mj
                res_t, wr = res_r.slot(m)
                t_res = P.dma("sync", res_t[:], B.RES[m], SLD,
                              waits=wr + [(GST, self.t_phaseA_st)])
                out_t, wo = out_r.slot(m)

                def lhsT_fn(_m, k, wa_half=wa_half, t_wl=t_wl, mj=mj):
                    return (wa_half[:, k * 512 + mj * 128: k * 512 + mj * 128 + 128],
                            [(SLD, t_wl)])

                def rhs_fn(k, n):
                    return (B.XA[k][:, n * 512:(n + 1) * 512],
                            [(SLD, t_lds[k])])

                def evac(_m, n, ps, t_mm, m=m, res_t=res_t, out_t=out_t,
                         t_res=t_res, wo=wo):
                    t_e = P.emit(
                        "vector", lambda e, n=n, ps=ps:
                        e.scalar_tensor_tensor(
                            out_t[:, n * 512:(n + 1) * 512], ps,
                            B.svec("b_out", m), res_t[:, n * 512:(n + 1) * 512],
                            op0=ALU.add, op1=ALU.add),
                        waits=[(MM, t_mm), (SLD, t_res)] + wo, inc=EV)
                    return (EV, t_e)

                self.gemm(1, 4, NCH, lhsT_fn, rhs_fn, evac)
                t_ev = (EV, EV.n)
                res_r.release(m, EV, t_ev[1])
                t_st = P.dma("gpsimd", B.RES[m], out_t[:], GST, waits=[t_ev])
                out_r.release(m, GST, t_st)
            self.warot.release(g, MM, MM.n)
        self.xa_guard = (MM, MM.n)
        self.res_ssm_done = (GST, GST.n)

    def phase_G(self):
        B, P = self, self.P
        SLD, GST, MM, EV, ACS = B.SLD, B.GST, B.MM, B.EV, B.ACS
        t_lds = []
        for j in range(NCH):
            t_lds.append(P.dma("sync", B.XA[j][:], B.XQ1[j], SLD,
                               waits=[self.xa_guard, (GST, self.t_phaseA_st)]))
        stg_r = Rot([B.TW[0][:, 0:512], B.TW[0][:, 512:1024],
                     B.TW[1][:, 0:512], B.TW[1][:, 512:1024]])
        P.wait("scalar", [(GST, GST.n)])
        si = 0
        for wi, (wd, dst) in enumerate(
                [(self.wq_d, B.Q0), (self.wk_d, B.K0), (self.wv_d, B.V0)]):
            w_r = wd.rearrange("(kc k) m -> k kc m", k=128)
            for g in range(2):
                wa_half = B.WA[:, (g % 2) * 4096:(g % 2) * 4096 + 4096]
                _, ww = self.warot.slot(wi * 2 + g)
                t_wl = P.dma("sync", wa_half,
                             w_r[:, :, g * 512:(g + 1) * 512], SLD, waits=ww)

                def lhsT_fn(m, k, wa_half=wa_half, t_wl=t_wl):
                    return (wa_half[:, k * 512 + m * 128: k * 512 + m * 128 + 128],
                            [(SLD, t_wl)])

                def rhs_fn(k, n):
                    return (B.XA[k][:, n * 512:(n + 1) * 512],
                            [(SLD, t_lds[k])])

                def evac(m, n, ps, t_mm, g=g, dst=dst):
                    nonlocal si
                    gm = g * 4 + m
                    stg, ws = stg_r.slot(si)
                    t_e = P.emit("scalar", lambda e, stg=stg, ps=ps:
                                 e.copy(stg, ps),
                                 waits=[(MM, t_mm)] + ws, inc=ACS)
                    t_st = P.dma("gpsimd", dst[gm][:, n * 512:(n + 1) * 512],
                                 stg, GST, waits=[(ACS, t_e)])
                    stg_r.release(si, GST, t_st)
                    si += 1
                    return (ACS, t_e)

                self.gemm(4, 4, NCH, lhsT_fn, rhs_fn, evac)
                self.warot.release(wi * 2 + g, MM, MM.n)
        self.xa_guard = (MM, MM.n)
        self.qkv_done = (GST, GST.n)

    def phase_H(self):
        B, P = self, self.P
        SLD, GST, MM, EV, ACS, GLD = B.SLD, B.GST, B.MM, B.EV, B.ACS, B.GLD
        for eng in ("gpsimd", "vector", "scalar"):
            P.wait(eng, [self.qkv_done])
        LN1E4 = float(np.log(10000.0))
        TWO_PI = 2.0 * np.pi
        C1, C2, C3 = 6.28125, 0.0019350051879882812, 3.0199402139940485e-07
        _, w0 = self.warot.slot(0)
        _, w1 = self.warot.slot(1)
        wp_r = self.wp_d.rearrange("(kc k) m -> k kc m", k=128)
        t_wp = P.dma("sync", B.WA[:].rearrange("p (k m) -> p k m", m=1024),
                     wp_r, SLD, waits=w0 + w1)
        t_if = None
        for k in range(NCH):
            koff = (k % 4) * 128
            t_i = P.emit("gpsimd", lambda e, k=k, koff=koff: e.iota(
                B.vsm[:, 8 + k:9 + k], pattern=[[1, 1]], base=koff,
                channel_multiplier=1,
                allow_small_or_imprecise_dtypes=True), inc=B.GC)
            t_if = P.emit("scalar", lambda e, k=k: e.activation(
                B.vsm[:, 8 + k:9 + k], B.vsm[:, 8 + k:9 + k], AF.Exp,
                scale=-LN1E4 / 512.0),
                waits=[(B.GC, t_i)], inc=ACS)
        pos_t = B.TW[0][:, 0:512]
        kf_t = B.TW[0][:, 512:1024]
        pe_r = Rot([B.TW[1][:, 0:512], B.TW[1][:, 512:1024]])
        stg_r = Rot([B.STG[0], B.STG[1], B.STG[2], B.STG[3]])
        pi = 0
        for n in range(8):
            w = 512 if n < 7 else 511
            for k in range(NCH):
                pe_t, wpe = pe_r.slot(pi)
                t_i = P.emit("gpsimd", lambda e, n=n, w=w: e.iota(
                    pos_t, pattern=[[1, 512]], base=n * 512 - 2047,
                    channel_multiplier=0,
                    allow_small_or_imprecise_dtypes=True),
                    waits=[(EV, self.pos_free)] if self.pos_free else [],
                    inc=B.GC)
                P.emit("vector", lambda e, k=k, pe_t=pe_t: e.tensor_scalar(
                    pe_t, pos_t, B.vsm[:, 8 + k:9 + k], None, op0=ALU.mult),
                    waits=[(B.GC, t_i), (ACS, t_if)] + wpe)
                P.emit("vector", lambda e, pe_t=pe_t: e.tensor_scalar(
                    kf_t, pe_t, 1.0 / TWO_PI, None, op0=ALU.mult))
                P.emit("vector", lambda e: e.tensor_copy(B.ki32[:], kf_t))
                P.emit("vector", lambda e: e.tensor_copy(kf_t, B.ki32[:]))
                for cc in (C1, C2, C3):
                    P.emit("vector", lambda e, pe_t=pe_t, cc=cc:
                           e.scalar_tensor_tensor(pe_t, kf_t, -cc, pe_t,
                                                  op0=ALU.mult, op1=ALU.add))
                if k >= 4:
                    P.emit("vector", lambda e, pe_t=pe_t: e.tensor_scalar(
                        pe_t, pe_t, float(np.pi / 2), None, op0=ALU.add))
                P.emit("vector", lambda e, pe_t=pe_t: e.tensor_scalar(
                    kf_t, pe_t, float(np.pi), None, op0=ALU.is_gt))
                P.emit("vector", lambda e, pe_t=pe_t: e.scalar_tensor_tensor(
                    pe_t, kf_t, -TWO_PI, pe_t, op0=ALU.mult, op1=ALU.add))
                P.emit("vector", lambda e, pe_t=pe_t: e.tensor_scalar(
                    kf_t, pe_t, float(-np.pi), None, op0=ALU.is_lt))
                t_w = P.emit("vector", lambda e, pe_t=pe_t:
                             e.scalar_tensor_tensor(pe_t, kf_t, TWO_PI, pe_t,
                                                    op0=ALU.mult, op1=ALU.add),
                             inc=EV)
                self.pos_free = t_w
                t_sin = P.emit("scalar", lambda e, pe_t=pe_t: e.activation(
                    pe_t, pe_t, AF.Sin), waits=[(EV, t_w)], inc=ACS)
                for m in range(NCH):
                    bw = self.bankw(m) if k == 0 else []
                    t_mm = P.emit(
                        "tensor",
                        lambda e, m=m, k=k, pe_t=pe_t, w=w:
                        e.matmul(B.PS[m][:, 0:w],
                                 B.WA[:, k * 1024 + m * 128: k * 1024 + m * 128 + 128],
                                 pe_t[:, 0:w], start=(k == 0), stop=(k == 7)),
                        waits=[(ACS, t_sin), (SLD, t_wp)] + bw,
                        inc=(MM if m == NCH - 1 else None))
                pe_r.release(pi, MM, t_mm)
                pi += 1
            for m in range(NCH):
                stg, ws = stg_r.slot(n * 8 + m)
                stg = stg[:, 0:512]
                t_e = P.emit("scalar", lambda e, m=m, stg=stg, w=w:
                             e.copy(stg[:, 0:w], B.PS[m][:, 0:w]),
                             waits=[(MM, t_mm)] + ws, inc=ACS)
                self.bank_free[m] = (ACS, t_e)
                t_st = P.dma("gpsimd", B.PREV[m][:, n * 512:n * 512 + w],
                             stg[:, 0:w], GST, waits=[(ACS, t_e)])
                stg_r.release(n * 8 + m, GST, t_st)
        self.warot.release(0, MM, MM.n)
        self.warot.release(1, MM, MM.n)
        self.prev_done = (GST, GST.n)

    def phase_I(self):
        B, P = self, self.P
        SLD, GST, MM, EV, ACS, GLD = B.SLD, B.GST, B.MM, B.EV, B.ACS, B.GLD
        qc_t, kc_t, vc_t, vtm_t = B.ST[0], B.ST[1], B.ST[2], B.ST[3]
        sc_r = Rot([B.ST[4], B.ST[5]])
        P_t, PT_t = B.ST[6], B.ST[7]
        stg_r = Rot([B.TW[0][:, 0:512], B.TW[0][:, 512:1024],
                     B.TW[1][:, 0:512], B.TW[1][:, 512:1024]])
        bds_bufs = [(B.STG[0], B.STG[1]), (B.STG[2], B.STG[3])]
        bds_free = [None, None]
        bdreg_free = [None, None]
        t_bds_tgt = {}
        pair_guard = None
        st_i = 0
        NIT = NCH * NTT * 2

        def head_of(it):
            c, r = divmod(it, NTT * 2)
            qt, hh = divmod(r, 2)
            return c, qt, hh

        def emit_bd(it, t_q, t_p):
            nonlocal st_i
            c, qt, hh = head_of(it)
            pb = hh * 64
            reg = it % 2
            qsl = qc_t[pb:pb + 64, qt * 128:(qt + 1) * 128]
            c0 = 1920 - qt * 128
            for n in range(5):
                wd = 512 if n < 4 else 127
                bi = 4 + (n % 2)
                t_mm = P.emit(
                    "tensor",
                    lambda e, n=n, wd=wd, bi=bi, qsl=qsl, pb=pb, c0=c0:
                    e.matmul(B.PS[bi][:, 0:wd], qsl,
                             B.PR[pb:pb + 64,
                                  c0 + n * 512: c0 + n * 512 + wd],
                             start=True, stop=True),
                    waits=[(SLD, t_p), (SLD, t_q)] + self.bankw(bi), inc=MM)
                stg, ws = stg_r.slot(st_i)
                t_e = P.emit("scalar",
                             lambda e, stg=stg, wd=wd, bi=bi:
                             e.copy(stg[:, 0:wd], B.PS[bi][:, 0:wd]),
                             waits=[(MM, t_mm)] + ws, inc=ACS)
                self.bank_free[bi] = (ACS, t_e)
                wreg = [bdreg_free[reg]] if (bdreg_free[reg] and n == 0) else []
                t_st = P.dma(
                    "gpsimd",
                    B.BDW.ap()[reg * 278400: reg * 278400 + 278400]
                    .rearrange("(p m) -> p m", m=2175)
                    [:, n * 512:n * 512 + wd],
                    stg[:, 0:wd], GST, waits=[(ACS, t_e)] + wreg)
                stg_r.release(st_i, GST, t_st)
                st_i += 1
            t_bdst = GST.n
            base = B.BDW.ap()[reg * 278400 + 127:
                              reg * 278400 + 127 + 128 * 2174] \
                .rearrange("(p m) -> p m", m=2174)
            pr = it % 2
            ba, bb = bds_bufs[pr]
            wfree = [bds_free[pr]] if bds_free[pr] else []
            P.dma("sync", ba[:], base[:, 0:1024], B.VST,
                  waits=[(GST, t_bdst)] + wfree)
            t_l = P.dma("sync", bb[:], base[:, 1024:2048], B.VST)
            t_bds_tgt[it] = t_l
            bdreg_free[reg] = (B.VST, t_l)

        for c in range(NCH):
            wld = [pair_guard] if pair_guard else []
            wld = wld + [self.qkv_done, self.prev_done]
            t_q = P.dma("sync", qc_t[:], B.Q0[c], SLD, waits=wld)
            t_k = P.dma("sync", kc_t[:], B.K0[c], SLD)
            t_v = P.dma("sync", vc_t[:], B.V0[c], SLD)
            t_p = P.dma("sync", B.PR[:, 0:4095], B.PREV[c], SLD)
            t_vtm_e = None
            for tc in range(NTT):
                bi = 6 + (tc % 2)
                t_mm = P.emit("tensor", lambda e, tc=tc, bi=bi:
                              e.transpose(B.PS[bi][:, 0:128],
                                          vc_t[:, tc * 128:(tc + 1) * 128],
                                          B.ident[:]),
                              waits=[(SLD, t_v)] + self.bankw(bi), inc=MM)
                t_vtm_e = P.emit("scalar", lambda e, tc=tc, bi=bi:
                                 e.copy(vtm_t[:, tc * 128:(tc + 1) * 128],
                                        B.PS[bi][:, 0:128]),
                                 waits=[(MM, t_mm)], inc=ACS)
                self.bank_free[bi] = (ACS, t_vtm_e)
            vtm_rdy = (ACS, t_vtm_e)
            prev_work = None
            emit_bd(c * NTT * 2, t_q, t_p)
            for qt in range(NTT):
                for hh in range(2):
                    h = 2 * c + hh
                    pb = hh * 64
                    it = (c * NTT + qt) * 2 + hh
                    qsl = qc_t[pb:pb + 64, qt * 128:(qt + 1) * 128]
                    sc_t, wsc = sc_r.slot(it)
                    t_ace = None
                    for n in range(4):
                        t_ac = P.emit(
                            "tensor", lambda e, n=n, qsl=qsl, pb=pb:
                            e.matmul(B.PS[n][:, 0:512], qsl,
                                     kc_t[pb:pb + 64, n * 512:(n + 1) * 512],
                                     start=True, stop=True),
                            waits=[(SLD, t_q), (SLD, t_k)] + self.bankw(n),
                            inc=MM)
                        t_ace = P.emit(
                            "scalar", lambda e, n=n, sc_t=sc_t:
                            e.activation(sc_t[:, n * 512:(n + 1) * 512],
                                         B.PS[n][:, 0:512], AF.Identity,
                                         scale=0.125),
                            waits=[(MM, t_ac)] + wsc, inc=ACS)
                        self.bank_free[n] = (ACS, t_ace)
                    if it + 1 < (c + 1) * NTT * 2:
                        emit_bd(it + 1, t_q, t_p)
                    if prev_work is not None:
                        prev_work()
                        prev_work = None
                    pr = it % 2
                    ba, bb = bds_bufs[pr]
                    t_l = t_bds_tgt[it]
                    P.emit("vector", lambda e, sc_t=sc_t, ba=ba:
                           e.scalar_tensor_tensor(
                               sc_t[:, 0:1024], ba[:], 0.125, sc_t[:, 0:1024],
                               op0=ALU.mult, op1=ALU.add),
                           waits=[(B.VST, t_l), (ACS, t_ace)])
                    t_cb = P.emit("vector", lambda e, sc_t=sc_t, bb=bb:
                                  e.scalar_tensor_tensor(
                                      sc_t[:, 1024:2048], bb[:], 0.125,
                                      sc_t[:, 1024:2048],
                                      op0=ALU.mult, op1=ALU.add), inc=EV)
                    bds_free[pr] = (EV, t_cb)
                    t_nb = P.emit("vector", lambda e, sc_t=sc_t:
                                  e.tensor_reduce(
                                      B.vsm[:, 1:2], sc_t[:],
                                      axis=mybir.AxisListType.X,
                                      op=ALU.max, negate=True), inc=EV)
                    t_exp = P.emit("scalar", lambda e, sc_t=sc_t: e.activation(
                        P_t[:], sc_t[:], AF.Exp, bias=B.vsm[:, 1:2],
                        scale=1.0, accum_out=B.vsm[:, 2:3]),
                        waits=[(EV, t_nb), (EV, self.pt_guard)], inc=ACS)
                    sc_r.release(it, ACS, t_exp)
                    t_rc = P.emit("vector", lambda e: e.reciprocal(
                        B.vsm[:, 3:4], B.vsm[:, 2:3]),
                        waits=[(ACS, t_exp)], inc=EV)

                    def work(h=h, qt=qt, pb=pb, t_exp=t_exp, t_rc=t_rc,
                             vtm_rdy=vtm_rdy, it=it):
                        t_pte = None
                        for kc in range(NTT):
                            bi = 6 + (kc % 2)
                            t_mm = P.emit(
                                "tensor", lambda e, kc=kc, bi=bi:
                                e.transpose(B.PS[bi][:, 0:128],
                                            P_t[:, kc * 128:(kc + 1) * 128],
                                            B.ident[:]),
                                waits=[(ACS, t_exp)] + self.bankw(bi), inc=MM)
                            t_pte = P.emit(
                                "vector", lambda e, kc=kc, bi=bi:
                                e.tensor_copy(PT_t[:, kc * 128:(kc + 1) * 128],
                                              B.PS[bi][:, 0:128]),
                                waits=[(MM, t_mm)], inc=EV)
                            self.bank_free[bi] = (EV, t_pte)
                        self.pt_guard_sem = EV
                        self.pt_guard = t_pte
                        bi = 6
                        t_mm = None
                        for kc in range(NTT):
                            t_mm = P.emit(
                                "tensor",
                                lambda e, kc=kc, pb=pb, s=(kc == 0),
                                t=(kc == NTT - 1):
                                e.matmul(B.PS[bi][:, 0:64],
                                         PT_t[:, kc * 128:(kc + 1) * 128],
                                         vtm_t[:, kc * 128 + pb:
                                               kc * 128 + pb + 64],
                                         start=s, stop=t),
                                waits=([(EV, t_pte), vtm_rdy]
                                       + (self.bankw(bi) if kc == 0 else [])),
                                inc=(MM if kc == NTT - 1 else None))
                        stg, ws = stg_r.slot(st_i_l[0])
                        t_e = P.emit(
                            "scalar", lambda e, stg=stg:
                            e.activation(stg[:, 0:64], B.PS[bi][:, 0:64],
                                         AF.Identity, scale=B.vsm[:, 3:4]),
                            waits=[(MM, t_mm), (EV, t_rc)] + ws, inc=ACS)
                        self.bank_free[bi] = (ACS, t_e)
                        t_st = P.dma("gpsimd",
                                     B.A0TM[qt][:, h * 64:h * 64 + 64],
                                     stg[:, 0:64], GST, waits=[(ACS, t_e)])
                        stg_r.release(st_i_l[0], GST, t_st)
                        st_i_l[0] += 1
                    st_i_l = [st_i]
                    prev_work = work
                    st_i += 1
            if prev_work is not None:
                prev_work()
                prev_work = None
            pair_guard = (MM, MM.n)
        self.a0_done = (GST, GST.n)

    def phase_J(self):
        B, P = self, self.P
        SLD, GST, MM, EV, ACS = B.SLD, B.GST, B.MM, B.EV, B.ACS
        at_r = Rot([B.TW[0], B.TW[1]])
        for tt in range(NTT):
            at_t, wa = at_r.slot(tt)
            t_ld = P.dma("sync", at_t[:], B.A0TM[tt], SLD,
                         waits=wa + [self.a0_done, self.xa_guard])
            t_mm = B.emit_transpose_1024(at_t, [(SLD, t_ld)], (0, 1), 1)
            at_r.release(tt, MM, t_mm)
            for j in range(NCH):
                bi = j // 4
                psb = B.PS[bi][:, (j % 4) * 128:(j % 4) * 128 + 128]
                t_e = P.emit("scalar", lambda e, j=j, psb=psb, tt=tt:
                             e.copy(B.XA[j][:, tt * 128:(tt + 1) * 128], psb),
                             waits=[(MM, t_mm)], inc=ACS)
            for bi in (0, 1):
                self.bank_free[bi] = (ACS, t_e)
        attn_rdy = [(ACS, ACS.n)]
        wo_r = self.wo_d.rearrange("(kc k) m -> k kc m", k=128)
        xn_r = Rot([B.ST[0], B.ST[1]])
        out_r = Rot([B.ST[2], B.ST[3]])
        for g in range(2):
            wa_half = B.WA[:, (g % 2) * 4096:(g % 2) * 4096 + 4096]
            _, ww = self.warot.slot(g)
            t_wl = P.dma("sync", wa_half,
                         wo_r[:, :, g * 512:(g + 1) * 512], SLD, waits=ww)
            for mj in range(4):
                m = g * 4 + mj
                xn_t, wx = xn_r.slot(m)
                t_xn = P.dma("sync", xn_t[:], B.XN[m], SLD, waits=wx)
                out_t, wo2 = out_r.slot(m)

                def lhsT_fn(_m, k, wa_half=wa_half, t_wl=t_wl, mj=mj):
                    return (wa_half[:, k * 512 + mj * 128: k * 512 + mj * 128 + 128],
                            [(SLD, t_wl)])

                def rhs_fn(k, n):
                    return (B.XA[k][:, n * 512:(n + 1) * 512], attn_rdy)

                def evac(_m, n, ps, t_mm, m=m, xn_t=xn_t, out_t=out_t,
                         t_xn=t_xn, wo2=wo2):
                    t_e = P.emit(
                        "vector", lambda e, n=n, ps=ps:
                        e.scalar_tensor_tensor(
                            out_t[:, n * 512:(n + 1) * 512], ps, 1.0,
                            xn_t[:, n * 512:(n + 1) * 512],
                            op0=ALU.mult, op1=ALU.add),
                        waits=[(MM, t_mm), (SLD, t_xn)] + wo2, inc=EV)
                    return (EV, t_e)

                self.gemm(1, 4, NCH, lhsT_fn, rhs_fn, evac, banks=(2, 3, 4, 5))
                t_ev = EV.n
                xn_r.release(m, EV, t_ev)
                t_st = P.dma("gpsimd", B.SRC2[m], out_t[:], GST,
                             waits=[(EV, t_ev)])
                out_r.release(m, GST, t_st)
            self.warot.release(g, MM, MM.n)
        self.xa_guard = (MM, MM.n)
        self.src2_done = (GST, GST.n)

    def phase_K(self):
        B, P = self, self.P
        SLD, GST, MM, EV, ACS = B.SLD, B.GST, B.MM, B.EV, B.ACS
        t_lds = []
        for j in range(NCH):
            t_lds.append(P.dma("sync", B.XA[j][:], B.SRC2[j], SLD,
                               waits=[self.xa_guard, self.src2_done]))
        src2_rdy = [(SLD, t_lds[-1])]
        tm_r = Rot([B.TW[0], B.TW[1]])
        stg_r = Rot([B.STG[0], B.STG[1]])
        for tt in range(NTT):
            cs = slice(tt * 128, tt * 128 + 128)
            t_mm = None
            for j in range(NCH):
                bi = j // 4
                ps = B.PS[bi][:, (j % 4) * 128:(j % 4) * 128 + 128]
                w = src2_rdy + ([(SLD, t_lds[j])] if True else [])
                t_mm = P.emit("tensor", lambda e, j=j, ps=ps, cs=cs:
                              e.transpose(ps, B.XA[j][:, cs], B.ident[:]),
                              waits=w + (self.bankw(bi) if j % 4 == 0 else []),
                              inc=MM)
            tm_t, wt = tm_r.slot(tt)
            P.emit("vector", lambda e: e.bn_stats(
                B.bn6[:, 0:6], B.PS[0][:, 0:512]),
                waits=[(MM, t_mm)] + wt)
            P.emit("vector", lambda e: e.bn_stats(
                B.bn6[:, 6:12], B.PS[1][:, 0:512]))
            P.emit("vector", lambda e: e.drain())
            P.emit("vector", lambda e: e.bn_aggr(
                B.stat[:, 0:2], B.bn6[:].rearrange("p (a b) -> p a b", b=3)))
            t_ag2 = P.emit("vector", lambda e: e.engine_nop(), inc=EV)
            t_ve = P.emit("scalar", lambda e: e.activation(
                B.stat[:, 2:3], B.stat[:, 1:2], AF.Identity,
                bias=B.epsc[:]), waits=[(EV, t_ag2)], inc=ACS)
            t_rc = P.emit("vector", lambda e: e.reciprocal(
                B.stat[:, 3:4], B.stat[:, 2:3]), waits=[(ACS, t_ve)], inc=EV)
            t_sq = P.emit("scalar", lambda e: e.activation(
                B.vsm[:, 1:2], B.stat[:, 3:4], AF.Sqrt),
                waits=[(EV, t_rc)], inc=ACS)
            P.wait("vector", [(ACS, t_sq)])
            t_ap = None
            for bi in range(2):
                t_ap = P.emit("vector", lambda e, bi=bi, tm_t=tm_t:
                              e.tensor_scalar(
                                  tm_t[:, bi * 512:(bi + 1) * 512],
                                  B.PS[bi][:, 0:512],
                                  B.stat[:, 0:1], B.vsm[:, 1:2],
                                  op0=ALU.subtract, op1=ALU.mult), inc=EV)
                self.bank_free[bi] = (EV, t_ap)
            t_mm2 = B.emit_transpose_1024(tm_t[:], [(EV, t_ap)], (2, 3), 1)
            tm_r.release(tt, MM, t_mm2)
            stg, ws = stg_r.slot(tt)
            P.wait("vector", ws)
            t_e = None
            for j in range(NCH):
                bi = 2 + j // 4
                psb = B.PS[bi][:, (j % 4) * 128:(j % 4) * 128 + 128]
                P.emit("vector", lambda e, j=j, psb=psb, cs=cs:
                       e.tensor_scalar(B.XA[j][:, cs], psb,
                                       B.svec("g2", j), B.svec("b2", j),
                                       op0=ALU.mult, op1=ALU.add),
                       waits=[(MM, t_mm2)], inc=EV)
                t_e = P.emit("vector", lambda e, j=j, psb=psb, stg=stg:
                             e.tensor_scalar(stg[:, j * 128:(j + 1) * 128], psb,
                                             B.svec("g2", j), B.svec("b2", j),
                                             op0=ALU.mult, op1=ALU.add),
                             inc=EV)
            for bi in (2, 3):
                self.bank_free[bi] = (EV, t_e)
            t_st = P.dma("gpsimd", B.X2D[:, :, cs].rearrange("c p q -> p c q"),
                         stg[:], GST, waits=[(EV, t_e)])
            stg_r.release(tt, GST, t_st)
        self.x2_ready = (EV, EV.n)
        self.x2d_done = (GST, GST.n)

    def phase_L(self):
        B, P = self, self.P
        SLD, GST, MM, EV, ACS = B.SLD, B.GST, B.MM, B.EV, B.ACS
        t_sw = P.dma("sync", B.selw_s[:], self.selw_d.rearrange(
            "(kc k) x -> k kc x", k=128), SLD)
        sel_t = B.TW[0][:, 0:512]
        for tt in range(NTT):
            bi = 4 + (tt % 2)
            t_mm = None
            for k in range(NCH):
                t_mm = P.emit(
                    "tensor",
                    lambda e, k=k, tt=tt, bi=bi, s=(k == 0), t=(k == 7):
                    e.matmul(B.PS[bi][:, 0:32],
                             B.XA[k][:, tt * 128:(tt + 1) * 128],
                             B.selw_s[:, k * 32:(k + 1) * 32],
                             start=s, stop=t),
                    waits=[self.x2_ready, (SLD, t_sw)]
                    + (self.bankw(bi) if k == 0 else []),
                    inc=(MM if k == 7 else None))
            t_e = P.emit("scalar", lambda e, tt=tt, bi=bi: e.activation(
                sel_t[:, tt * 32:(tt + 1) * 32], B.PS[bi][:, 0:32],
                AF.Sigmoid), waits=[(MM, t_mm)], inc=ACS)
            self.bank_free[bi] = (ACS, t_e)
        sel_rdy = (ACS, ACS.n)
        wrk3 = B.TW[1][:, 0:512].rearrange("p (a b) -> p a b", b=32)
        ge3 = B.TW[1][:, 512:1024].rearrange("p (a b) -> p a b", b=32)
        sel3 = sel_t[:].rearrange("p (a b) -> p a b", b=32)
        t4b = B.vsm[:, 0:16].rearrange("p (a b) -> p a b", b=1) \
            .broadcast_to([128, 16, 32])
        P.emit("vector", lambda e: e.tensor_copy(wrk3, sel3), waits=[sel_rdy])
        for it in range(4):
            P.emit("vector", lambda e: e.tensor_reduce(
                B.vsm[:, 0:16], wrk3, axis=mybir.AxisListType.X, op=ALU.max))
            B.spacers()
            if it < 3:
                P.emit("vector", lambda e: e.tensor_tensor(
                    ge3, wrk3, t4b, op=ALU.is_ge))
                P.emit("vector", lambda e: e.scalar_tensor_tensor(
                    wrk3, ge3, -1e30, wrk3, op0=ALU.mult, op1=ALU.add))
        P.emit("vector", lambda e: e.tensor_tensor(
            ge3, sel3, t4b, op=ALU.is_ge))
        t_g = P.emit("vector", lambda e: e.tensor_tensor(
            B.gate_sb[:].rearrange("p (a b) -> p a b", b=32), sel3, ge3,
            op=ALU.mult), inc=EV)
        self.gates_ready = (EV, t_g)
        self.xa_guard = (MM, MM.n)
        if "GATED" in self.dbg:
            gd = self.dram("GATED", [128, 512])
            P.dma("gpsimd", gd, B.gate_sb[:], GST, waits=[self.gates_ready])

    def phase_M(self):
        B, P = self, self.P
        SLD, GST, MM, EV, ACS, GLD = B.SLD, B.GST, B.MM, B.EV, B.ACS, B.GLD
        keys_r = self.keys_d.rearrange("x (kc p) h -> kc p x h", p=128)
        x2_r = Rot([B.TW[0], B.TW[1]])
        stg_r = Rot([B.STG[0][:, 0:512], B.STG[0][:, 512:1024],
                     B.STG[1][:, 0:512], B.STG[1][:, 512:1024]])
        st2_r = Rot([B.STG[2][:, 0:512], B.STG[2][:, 512:1024],
                     B.STG[3][:, 0:512], B.STG[3][:, 512:1024]])
        si = 0
        xi = 0
        vals_r = self.values_d
        st_guard = None
        for sst in range(4):
            for half in range(2):
                eb = half * 16
                t_keys = []
                for k in range(NCH):
                    t_keys.append(P.dma(
                        "sync", B.XA[k][:],
                        keys_r[k][:, eb:eb + 16, :], SLD,
                        waits=[self.xa_guard]))
                for ti in range(4):
                    tt = sst * 4 + ti
                    x2_t, wx = x2_r.slot(xi)
                    t_x2 = P.dma("sync", x2_t[:],
                                 B.X2D[:, :, tt * 128:(tt + 1) * 128]
                                 .rearrange("c p q -> p c q"), SLD,
                                 waits=wx + [self.x2d_done, self.gates_ready])
                    t_mm = None
                    for k in range(NCH):
                        for g in range(4):
                            bi = g
                            ws = ([(SLD, t_x2), (SLD, t_keys[k])]
                                  + (self.bankw(bi) if k == 0 else []))
                            t_mm = P.emit(
                                "tensor",
                                lambda e, k=k, g=g, bi=bi, x2_t=x2_t,
                                s=(k == 0), t=(k == NCH - 1):
                                e.matmul(B.PS[bi][:],
                                         x2_t[:, k * 128:(k + 1) * 128],
                                         B.XA[k][:, g * 512:(g + 1) * 512],
                                         start=s, stop=t),
                                waits=ws,
                                inc=(MM if (k == 7 and g == 3) else None))
                    x2_r.release(xi, MM, t_mm)
                    xi += 1
                    self.xa_guard = (MM, t_mm)
                    t_hg = None
                    for e_ in range(16):
                        bi, qo = e_ // 4, (e_ % 4) * 128
                        ge = eb + e_
                        whg = [(EV, self.hg_guard)] if (self.hg_guard and e_ == 0) else []
                        t_hg = P.emit(
                            "scalar",
                            lambda e, e_=e_, bi=bi, qo=qo, tt=tt, ge=ge:
                            e.activation(B.PR[:, e_ * 128:(e_ + 1) * 128],
                                         B.PS[bi][:, qo:qo + 128], AF.Relu,
                                         scale=B.gate_sb[:, tt * 32 + ge:
                                                         tt * 32 + ge + 1]),
                            waits=[(MM, t_mm), self.gates_ready] + whg,
                            inc=ACS)
                    for bi in range(4):
                        self.bank_free[bi] = (ACS, t_hg)
                    t_pte = None
                    for eg in range(4):
                        bi = 4 + eg % 2
                        t_mm2 = None
                        for j in range(4):
                            e_ = eg * 4 + j
                            t_mm2 = P.emit(
                                "tensor", lambda e, e_=e_, j=j, bi=bi:
                                e.transpose(B.PS[bi][:, j * 128:(j + 1) * 128],
                                            B.PR[:, e_ * 128:(e_ + 1) * 128],
                                            B.ident[:]),
                                waits=[(ACS, t_hg)]
                                + (self.bankw(bi) if j == 0 else [])
                                + ([st_guard] if (st_guard and eg == 0 and j == 0)
                                   else []),
                                inc=(MM if j == 3 else None))
                        t_pte = P.emit(
                            "vector", lambda e, eg=eg, bi=bi, ti=ti, half=half:
                            e.tensor_copy(
                                B.ST[half * 4 + eg][:]
                                .rearrange("p (a b) -> p a b", b=512)
                                [:, :, ti * 128:(ti + 1) * 128],
                                B.PS[bi][:, 0:512]
                                .rearrange("p (a b) -> p a b", b=128)),
                            waits=[(MM, t_mm2)], inc=EV)
                        self.bank_free[bi] = (EV, t_pte)
                    self.hg_guard = t_pte
            hgT_rdy = (EV, EV.n)
            t_s3 = None
            for m in range(NCH):
                wa_half = B.WA[:, (m % 2) * 4096:(m % 2) * 4096 + 4096]
                _, ww = self.warot.slot(m)
                t_wl = P.dma("sync", wa_half.rearrange("p (a b) -> p a b", b=128),
                             vals_r[:, :, m * 128:(m + 1) * 128]
                             .transpose([1, 0, 2]), SLD, waits=ww)
                bi = 6 + (m % 2)
                t_mm = None
                for e_ in range(32):
                    t_mm = P.emit(
                        "tensor",
                        lambda e, e_=e_, bi=bi, wa_half=wa_half,
                        s=(e_ == 0), t=(e_ == 31):
                        e.matmul(B.PS[bi][:, 0:512],
                                 wa_half[:, e_ * 128:(e_ + 1) * 128],
                                 B.ST[e_ // 4][:, (e_ % 4) * 512:
                                               (e_ % 4) * 512 + 512],
                                 start=s, stop=t),
                        waits=[(SLD, t_wl), hgT_rdy]
                        + (self.bankw(bi) if e_ == 0 else []),
                        inc=(MM if e_ == 31 else None))
                self.warot.release(m, MM, t_mm)
                t_s3 = t_mm
                stg, ws = stg_r.slot(si)
                st2, ws2 = st2_r.slot(si)
                t_res = P.dma("sync", stg,
                              B.RES[m][:, sst * 512:(sst + 1) * 512], SLD,
                              waits=ws + [self.res_ssm_done])
                t_s2 = P.dma("sync", st2,
                             B.SRC2[m][:, sst * 512:(sst + 1) * 512], SLD,
                             waits=ws2 + [self.src2_done])
                t_a1 = P.emit("vector", lambda e, bi=bi, stg=stg:
                              e.scalar_tensor_tensor(
                                  stg, B.PS[bi][:, 0:512], 1.0, stg,
                                  op0=ALU.mult, op1=ALU.add),
                              waits=[(MM, t_mm), (SLD, t_res)], inc=EV)
                self.bank_free[bi] = (EV, t_a1)
                t_a2 = P.emit("vector", lambda e, stg=stg, st2=st2:
                              e.tensor_tensor(stg, stg, st2, op=ALU.add),
                              waits=[(SLD, t_s2)], inc=EV)
                st2_r.release(si, EV, t_a2)
                t_st = P.dma("gpsimd", B.RES[m][:, sst * 512:(sst + 1) * 512],
                             stg, GST, waits=[(EV, t_a2)])
                stg_r.release(si, GST, t_st)
                si += 1
            st_guard = (MM, t_s3)
        self.x1_done = (GST, GST.n)
        self.xa_guard = (MM, MM.n)

    def phase_N(self):
        B, P = self, self.P
        SLD, GST, MM, EV, ACS = B.SLD, B.GST, B.MM, B.EV, B.ACS
        t_lds = []
        for j in range(NCH):
            t_lds.append(P.dma("sync", B.XA[j][:], B.RES[j], SLD,
                               waits=[self.xa_guard, self.x1_done]))
        tm_r = Rot([B.TW[0], B.TW[1]])
        stg_r = Rot([B.STG[0], B.STG[1]])
        for tt in range(NTT):
            cs = slice(tt * 128, tt * 128 + 128)
            t_mm = None
            for j in range(NCH):
                bi = j // 4
                ps = B.PS[bi][:, (j % 4) * 128:(j % 4) * 128 + 128]
                t_mm = P.emit("tensor", lambda e, j=j, ps=ps, cs=cs:
                              e.transpose(ps, B.XA[j][:, cs], B.ident[:]),
                              waits=[(SLD, t_lds[j])]
                              + (self.bankw(bi) if j % 4 == 0 else []),
                              inc=MM)
            tm_t, wt = tm_r.slot(tt)
            P.emit("vector", lambda e: e.bn_stats(
                B.bn6[:, 0:6], B.PS[0][:, 0:512]),
                waits=[(MM, t_mm)] + wt)
            P.emit("vector", lambda e: e.bn_stats(
                B.bn6[:, 6:12], B.PS[1][:, 0:512]))
            B.spacers()
            t_ag2 = P.emit("vector", lambda e: e.bn_aggr(
                B.stat[:, 0:2], B.bn6[:].rearrange("p (a b) -> p a b", b=3)),
                inc=EV)
            t_ve = P.emit("scalar", lambda e: e.activation(
                B.stat[:, 2:3], B.stat[:, 1:2], AF.Identity,
                bias=B.epsc[:]), waits=[(EV, t_ag2)], inc=ACS)
            t_rc = P.emit("vector", lambda e: e.reciprocal(
                B.stat[:, 3:4], B.stat[:, 2:3]), waits=[(ACS, t_ve)], inc=EV)
            t_sq = P.emit("scalar", lambda e: e.activation(
                B.vsm[:, 1:2], B.stat[:, 3:4], AF.Sqrt),
                waits=[(EV, t_rc)], inc=ACS)
            P.wait("vector", [(ACS, t_sq)])
            t_ap = None
            for bi in range(2):
                t_ap = P.emit("vector", lambda e, bi=bi, tm_t=tm_t:
                              e.tensor_scalar(
                                  tm_t[:, bi * 512:(bi + 1) * 512],
                                  B.PS[bi][:, 0:512],
                                  B.stat[:, 0:1], B.vsm[:, 1:2],
                                  op0=ALU.subtract, op1=ALU.mult), inc=EV)
                self.bank_free[bi] = (EV, t_ap)
            t_mm2 = B.emit_transpose_1024(tm_t[:], [(EV, t_ap)], (2, 3), 1)
            tm_r.release(tt, MM, t_mm2)
            stg, ws = stg_r.slot(tt)
            P.wait("scalar", ws)
            t_e = None
            for bi2 in range(2):
                t_e = P.emit("scalar", lambda e, bi2=bi2, stg=stg:
                             e.copy(stg[:, bi2 * 512:bi2 * 512 + 512],
                                    B.PS[2 + bi2][:]),
                             waits=[(MM, t_mm2)], inc=ACS)
            P.wait("vector", [(ACS, t_e)])
            t_e2 = None
            for j in range(NCH):
                t_e2 = P.emit("vector", lambda e, j=j, stg=stg, cs=cs:
                              e.tensor_copy(B.XA[j][:, cs],
                                            stg[:, j * 128:(j + 1) * 128]),
                              inc=EV)
            for bi2 in (2, 3):
                self.bank_free[bi2] = (ACS, t_e)
            t_st = P.dma("gpsimd", B.XN1[:, :, cs].rearrange("c p q -> p c q"),
                         stg[:], GST, waits=[(EV, t_e2)])
            stg_r.release(tt, GST, t_st)
        self.xn1_ready = (EV, EV.n)
        self.xn1d_done = (GST, GST.n)

    def phase_O(self):
        B, P = self, self.P
        SLD, GST, MM, EV, ACS = B.SLD, B.GST, B.MM, B.EV, B.ACS
        stg_r = Rot([B.TW[0][:, 0:512], B.TW[0][:, 512:1024],
                     B.TW[1][:, 0:512], B.TW[1][:, 512:1024]])
        P.wait("scalar", [(GST, GST.n)])
        si = 0
        for wi, (wd, dst) in enumerate(
                [(self.l1wq_d, B.Q1), (self.l1wk_d, B.K1), (self.l1wv_d, B.V1)]):
            w_r = wd.rearrange("(kc k) m -> k kc m", k=128)
            for g in range(2):
                wa_half = B.WA[:, (g % 2) * 4096:(g % 2) * 4096 + 4096]
                _, ww = self.warot.slot(wi * 2 + g)
                t_wl = P.dma("sync", wa_half,
                             w_r[:, :, g * 512:(g + 1) * 512], SLD, waits=ww)

                def lhsT_fn(m, k, wa_half=wa_half, t_wl=t_wl):
                    return (wa_half[:, k * 512 + m * 128: k * 512 + m * 128 + 128],
                            [(SLD, t_wl)])

                def rhs_fn(k, n):
                    return (B.XA[k][:, n * 512:(n + 1) * 512],
                            [self.xn1_ready])

                def evac(m, n, ps, t_mm, g=g, dst=dst):
                    nonlocal si
                    gm = g * 4 + m
                    stg, ws = stg_r.slot(si)
                    t_e = P.emit("scalar", lambda e, stg=stg, ps=ps:
                                 e.copy(stg, ps),
                                 waits=[(MM, t_mm)] + ws, inc=ACS)
                    t_st = P.dma("gpsimd", dst[gm][:, n * 512:(n + 1) * 512],
                                 stg, GST, waits=[(ACS, t_e)])
                    stg_r.release(si, GST, t_st)
                    si += 1
                    return (ACS, t_e)

                self.gemm(4, 4, NCH, lhsT_fn, rhs_fn, evac)
                self.warot.release(wi * 2 + g, MM, MM.n)
        self.xa_guard = (MM, MM.n)
        self.qkv1_done = (GST, GST.n)

    def phase_PQ(self):
        B, P = self, self.P
        SLD, GST, MM, EV, ACS, GLD = B.SLD, B.GST, B.MM, B.EV, B.ACS, B.GLD
        LN1E4 = float(np.log(10000.0))
        TWO_PI = 2.0 * np.pi
        C1, C2, C3 = 6.28125, 0.0019350051879882812, 3.0199402139940485e-07
        for eng in ("gpsimd", "vector", "scalar"):
            P.wait(eng, [self.qkv1_done])
        sin_t, cos_t = B.ST[6], B.ST[7]
        kf_t = B.ST[0]
        t_i = P.emit("gpsimd", lambda e: e.iota(
            B.vsm[0:64, 5:6], pattern=[[1, 1]], base=0, channel_multiplier=1,
            allow_small_or_imprecise_dtypes=True), inc=B.GC)
        t_i = P.emit("gpsimd", lambda e: e.iota(
            B.vsm[64:128, 5:6], pattern=[[1, 1]], base=0, channel_multiplier=1,
            allow_small_or_imprecise_dtypes=True), inc=B.GC)
        t_if = P.emit("scalar", lambda e: e.activation(
            B.vsm[:, 5:6], B.vsm[:, 5:6], AF.Exp, scale=-LN1E4 / 64.0),
            waits=[(B.GC, t_i)], inc=ACS)
        t_i2 = P.emit("gpsimd", lambda e: e.iota(
            sin_t[:], pattern=[[1, T]], base=0, channel_multiplier=0,
            allow_small_or_imprecise_dtypes=True), inc=B.GC)
        P.emit("vector", lambda e: e.tensor_scalar(
            cos_t[:], sin_t[:], B.vsm[:, 5:6], None,
            op0=ALU.mult), waits=[(B.GC, t_i2), (ACS, t_if)])
        t_tab = None
        for tgt, shift in ((sin_t, 0.0), (cos_t, float(np.pi / 2))):
            P.emit("vector", lambda e: e.tensor_scalar(
                kf_t[:], cos_t[:], 1.0 / TWO_PI, None, op0=ALU.mult))
            for pc in range(4):
                P.emit("vector", lambda e, pc=pc: e.tensor_copy(
                    B.ki32[:], kf_t[:, pc * 512:(pc + 1) * 512]))
                P.emit("vector", lambda e, pc=pc: e.tensor_copy(
                    kf_t[:, pc * 512:(pc + 1) * 512], B.ki32[:]))
            first = True
            for cc in (C1, C2, C3):
                P.emit("vector", lambda e, cc=cc, tgt=tgt, first=first:
                       e.scalar_tensor_tensor(
                           tgt[:], kf_t[:], -cc,
                           cos_t[:] if first else tgt[:],
                           op0=ALU.mult, op1=ALU.add))
                first = False
            if shift:
                P.emit("vector", lambda e, tgt=tgt, shift=shift:
                       e.tensor_scalar(tgt[:], tgt[:], shift,
                                       None, op0=ALU.add))
            P.emit("vector", lambda e, tgt=tgt: e.tensor_scalar(
                kf_t[:], tgt[:], float(np.pi), None, op0=ALU.is_gt))
            P.emit("vector", lambda e, tgt=tgt: e.scalar_tensor_tensor(
                tgt[:], kf_t[:], -TWO_PI, tgt[:],
                op0=ALU.mult, op1=ALU.add))
            P.emit("vector", lambda e, tgt=tgt: e.tensor_scalar(
                kf_t[:], tgt[:], float(-np.pi), None, op0=ALU.is_lt))
            t_wv = P.emit("vector", lambda e, tgt=tgt: e.scalar_tensor_tensor(
                tgt[:], kf_t[:], TWO_PI, tgt[:],
                op0=ALU.mult, op1=ALU.add), inc=EV)
            t_tab = P.emit("scalar", lambda e, tgt=tgt: e.activation(
                tgt[:], tgt[:], AF.Sin),
                waits=[(EV, t_wv)], inc=ACS)
        tab_rdy = (ACS, t_tab)
        bufsets = [(B.ST[0], B.ST[1], B.ST[2]), (B.ST[3], B.ST[4], B.ST[5])]
        buf_free = [None, None]
        ci = 0
        t_prev = None
        for src, dst in ((B.Q1, B.Q1R), (B.K1, B.K1R)):
            for h in range(NCH):
                qin, qsw, rot = bufsets[ci % 2]
                w = ([buf_free[ci % 2]] if buf_free[ci % 2]
                     else [(EV, EV.n), (B.MM, B.MM.n), (ACS, ACS.n)])
                t_ld = P.dma("sync", qin[:], src[h], SLD,
                             waits=w + [self.qkv1_done])
                P.dma("sync", qsw[0:64, :], src[h][64:128, :], SLD)
                t_ld2 = P.dma("sync", qsw[64:128, :], src[h][0:64, :], SLD)
                wv = [(SLD, t_ld2), tab_rdy]
                P.emit("vector", lambda e, rot=rot, qin=qin: e.tensor_tensor(
                    rot[0:64, :], qin[0:64, :], cos_t[0:64, :], op=ALU.mult),
                    waits=wv)
                P.emit("vector", lambda e, qsw=qsw: e.tensor_tensor(
                    qsw[0:64, :], qsw[0:64, :], sin_t[0:64, :], op=ALU.mult))
                P.emit("vector", lambda e, rot=rot, qsw=qsw: e.tensor_tensor(
                    rot[0:64, :], rot[0:64, :], qsw[0:64, :],
                    op=ALU.subtract))
                P.emit("vector", lambda e, rot=rot, qin=qin: e.tensor_tensor(
                    rot[64:128, :], qin[64:128, :], cos_t[64:128, :],
                    op=ALU.mult))
                P.emit("vector", lambda e, qsw=qsw: e.tensor_tensor(
                    qsw[64:128, :], qsw[64:128, :], sin_t[64:128, :],
                    op=ALU.mult))
                t_r = P.emit("vector", lambda e, rot=rot, qsw=qsw: e.tensor_tensor(
                    rot[64:128, :], rot[64:128, :], qsw[64:128, :],
                    op=ALU.add), inc=EV)
                t_st = P.dma("gpsimd", dst[h], rot[:], GST,
                             waits=[(EV, t_r)])
                buf_free[ci % 2] = (GST, t_st)
                ci += 1
        self.rope_done = (GST, GST.n)

    def phase_R(self):
        B, P = self, self.P
        SLD, GST, MM, EV, ACS = B.SLD, B.GST, B.MM, B.EV, B.ACS
        qc_t, kc_t, vc_t, vtm_t = B.ST[0], B.ST[1], B.ST[2], B.ST[3]
        sc_r = Rot([B.ST[4], B.ST[5]])
        P_t, PT_t = B.ST[6], B.ST[7]
        B.mask_sb = B.PR[:, 0:896]
        self.t_mask = P.dma("sync", B.mask_sb, self.mask_d, B.SLD)
        stg_r = Rot([B.TW[0][:, 0:512], B.TW[0][:, 512:1024],
                     B.TW[1][:, 0:512], B.TW[1][:, 512:1024]])
        pair_guard = None
        st_i = 0
        pt_guard1 = None
        prev_work = None
        for h in range(NCH):
            wld = ([pair_guard] if pair_guard else []) + [self.rope_done]
            t_q = P.dma("sync", qc_t[:], B.Q1R[h], SLD, waits=wld)
            t_k = P.dma("sync", kc_t[:], B.K1R[h], SLD)
            t_v = P.dma("sync", vc_t[:], B.V1[h], SLD,
                        waits=[self.qkv1_done])
            t_vtm_e = None
            for tc in range(NTT):
                bi = 6 + (tc % 2)
                t_mm = P.emit("tensor", lambda e, tc=tc, bi=bi:
                              e.transpose(B.PS[bi][:, 0:128],
                                          vc_t[:, tc * 128:(tc + 1) * 128],
                                          B.ident[:]),
                              waits=[(SLD, t_v)] + self.bankw(bi), inc=MM)
                t_vtm_e = P.emit("scalar", lambda e, tc=tc, bi=bi:
                                 e.copy(vtm_t[:, tc * 128:(tc + 1) * 128],
                                        B.PS[bi][:, 0:128]),
                                 waits=[(MM, t_mm)], inc=ACS)
                self.bank_free[bi] = (ACS, t_vtm_e)
            vtm_rdy = (ACS, t_vtm_e)
            for qt in range(NTT):
                ncn = qt // 4 + 1
                off = 128 * (qt % 4)
                qsl = qc_t[:, qt * 128:(qt + 1) * 128]
                t_ac = None
                for n in range(ncn):
                    t_ac = P.emit(
                        "tensor", lambda e, n=n, qsl=qsl:
                        e.matmul(B.PS[n][:, 0:512], qsl,
                                 kc_t[:, n * 512:(n + 1) * 512],
                                 start=True, stop=True),
                        waits=[(SLD, t_q), (SLD, t_k)] + self.bankw(n),
                        inc=(MM if n == ncn - 1 else None))
                if prev_work is not None:
                    prev_work()
                    prev_work = None
                sc_t, wsc = sc_r.slot(st_i)
                t_cb = None
                for n in range(ncn):
                    if n < ncn - 1:
                        t_cb = P.emit(
                            "vector", lambda e, n=n, sc_t=sc_t:
                            e.tensor_scalar(sc_t[:, n * 512:(n + 1) * 512],
                                            B.PS[n][:, 0:512],
                                            1.0 / 128.0, None, op0=ALU.mult),
                            waits=[(MM, t_ac)] + wsc, inc=EV)
                    else:
                        t_cb = P.emit(
                            "vector", lambda e, n=n, sc_t=sc_t, off=off:
                            e.scalar_tensor_tensor(
                                sc_t[:, n * 512:(n + 1) * 512],
                                B.PS[n][:, 0:512], 1.0 / 128.0,
                                B.mask_sb[:, 384 - off: 896 - off],
                                op0=ALU.mult, op1=ALU.add),
                            waits=[(MM, t_ac), (SLD, self.t_mask)] + wsc,
                            inc=EV)
                    self.bank_free[n] = (EV, t_cb)
                nk = ncn * 512
                t_nb = P.emit("vector", lambda e, sc_t=sc_t, nk=nk:
                              e.tensor_reduce(
                                  B.vsm[:, 1:2], sc_t[:, 0:nk],
                                  axis=mybir.AxisListType.X,
                                  op=ALU.max, negate=True), inc=EV)
                t_exp = P.emit("scalar", lambda e, sc_t=sc_t, nk=nk:
                               e.activation(
                                   P_t[:, 0:nk], sc_t[:, 0:nk], AF.Exp,
                                   bias=B.vsm[:, 1:2], scale=1.0,
                                   accum_out=B.vsm[:, 2:3]),
                               waits=[(EV, t_nb)]
                               + ([(EV, pt_guard1)] if pt_guard1 else []),
                               inc=ACS)
                sc_r.release(st_i, ACS, t_exp)
                t_rc = P.emit("vector", lambda e: e.reciprocal(
                    B.vsm[:, 3:4], B.vsm[:, 2:3]),
                    waits=[(ACS, t_exp)], inc=EV)

                def work(h=h, qt=qt, t_exp=t_exp, t_rc=t_rc,
                         vtm_rdy=vtm_rdy, nkc=(qt + 1), sti=st_i):
                    nonlocal pt_guard1
                    t_pte = None
                    for kc in range(nkc):
                        bi = 6 + (kc % 2)
                        t_mm = P.emit(
                            "tensor", lambda e, kc=kc, bi=bi:
                            e.transpose(B.PS[bi][:, 0:128],
                                        P_t[:, kc * 128:(kc + 1) * 128],
                                        B.ident[:]),
                            waits=[(ACS, t_exp)] + self.bankw(bi), inc=MM)
                        t_pte = P.emit(
                            "vector", lambda e, kc=kc, bi=bi:
                            e.tensor_copy(PT_t[:, kc * 128:(kc + 1) * 128],
                                          B.PS[bi][:, 0:128]),
                            waits=[(MM, t_mm)], inc=EV)
                        self.bank_free[bi] = (EV, t_pte)
                    pt_guard1 = t_pte
                    bi = 6
                    t_mm = None
                    for kc in range(nkc):
                        t_mm = P.emit(
                            "tensor",
                            lambda e, kc=kc, s=(kc == 0), t=(kc == nkc - 1):
                            e.matmul(B.PS[bi][:, 0:128],
                                     PT_t[:, kc * 128:(kc + 1) * 128],
                                     vtm_t[:, kc * 128:(kc + 1) * 128],
                                     start=s, stop=t),
                            waits=([(EV, t_pte), vtm_rdy]
                                   + (self.bankw(bi) if kc == 0 else [])),
                            inc=(MM if kc == nkc - 1 else None))
                    stg, ws = stg_r.slot(sti)
                    t_e = P.emit(
                        "scalar", lambda e, stg=stg:
                        e.activation(stg[:, 0:128], B.PS[bi][:, 0:128],
                                     AF.Identity, scale=B.vsm[:, 3:4]),
                        waits=[(MM, t_mm), (EV, t_rc)] + ws, inc=ACS)
                    self.bank_free[bi] = (ACS, t_e)
                    t_st = P.dma("gpsimd",
                                 B.A1TM[qt][:, h * 128:(h + 1) * 128],
                                 stg[:, 0:128], GST, waits=[(ACS, t_e)])
                    stg_r.release(sti, GST, t_st)
                prev_work = work
                st_i += 1
            if prev_work is not None:
                prev_work()
                prev_work = None
            pair_guard = (MM, MM.n)
        self.a1_done = (GST, GST.n)

    def phase_S(self):
        B, P = self, self.P
        SLD, GST, MM, EV, ACS = B.SLD, B.GST, B.MM, B.EV, B.ACS
        at_r = Rot([B.TW[0], B.TW[1]])
        t_e = None
        for tt in range(NTT):
            at_t, wa = at_r.slot(tt)
            t_ld = P.dma("sync", at_t[:], B.A1TM[tt], SLD,
                         waits=wa + [self.a1_done, self.xa_guard])
            t_mm = B.emit_transpose_1024(at_t, [(SLD, t_ld)], (0, 1), 1)
            at_r.release(tt, MM, t_mm)
            for j in range(NCH):
                bi = j // 4
                psb = B.PS[bi][:, (j % 4) * 128:(j % 4) * 128 + 128]
                t_e = P.emit("scalar", lambda e, j=j, psb=psb, tt=tt:
                             e.copy(B.XA[j][:, tt * 128:(tt + 1) * 128], psb),
                             waits=[(MM, t_mm)], inc=ACS)
            for bi in (0, 1):
                self.bank_free[bi] = (ACS, t_e)
        attn_rdy = [(ACS, ACS.n)]
        wo_r = self.l1wo_d.rearrange("(kc k) m -> k kc m", k=128)
        res_r = Rot([B.ST[0], B.ST[1]])
        out_r = Rot([B.ST[2], B.ST[3]])
        for g in range(2):
            wa_half = B.WA[:, (g % 2) * 4096:(g % 2) * 4096 + 4096]
            _, ww = self.warot.slot(g)
            t_wl = P.dma("sync", wa_half,
                         wo_r[:, :, g * 512:(g + 1) * 512], SLD, waits=ww)
            for mj in range(4):
                m = g * 4 + mj
                res_t, wr = res_r.slot(m)
                t_res = P.dma("sync", res_t[:], B.RES[m], SLD,
                              waits=wr + [self.x1_done])
                out_t, wo2 = out_r.slot(m)

                def lhsT_fn(_m, k, wa_half=wa_half, t_wl=t_wl, mj=mj):
                    return (wa_half[:, k * 512 + mj * 128: k * 512 + mj * 128 + 128],
                            [(SLD, t_wl)])

                def rhs_fn(k, n):
                    return (B.XA[k][:, n * 512:(n + 1) * 512], attn_rdy)

                def evac(_m, n, ps, t_mm, m=m, res_t=res_t, out_t=out_t,
                         t_res=t_res, wo2=wo2):
                    t_e = P.emit(
                        "vector", lambda e, n=n, ps=ps:
                        e.scalar_tensor_tensor(
                            out_t[:, n * 512:(n + 1) * 512], ps, 1.0,
                            res_t[:, n * 512:(n + 1) * 512],
                            op0=ALU.mult, op1=ALU.add),
                        waits=[(MM, t_mm), (SLD, t_res)] + wo2, inc=EV)
                    return (EV, t_e)

                self.gemm(1, 4, NCH, lhsT_fn, rhs_fn, evac, banks=(2, 3, 4, 5))
                t_ev = EV.n
                res_r.release(m, EV, t_ev)
                t_st = P.dma("gpsimd", B.RES[m], out_t[:], GST,
                             waits=[(EV, t_ev)])
                out_r.release(m, GST, t_st)
            self.warot.release(g, MM, MM.n)
        self.xa_guard = (MM, MM.n)
        self.res_attn1_done = (GST, GST.n)

    def phase_T(self):
        B, P = self, self.P
        SLD, GST, MM, EV, ACS = B.SLD, B.GST, B.MM, B.EV, B.ACS
        if "RESS" in self.dbg:
            RESS = self.dram("RESS", [NCH, 128, T])
            for m in range(NCH):
                P.dma("gpsimd", RESS[m], B.RES[m], GST,
                      waits=[self.res_attn1_done])
        t_w1 = []
        for k in range(NCH):
            t_w1.append(P.dma("sync", B.XA[k][:], self.l1w1t_d
                              .rearrange("(kc k) m -> kc k m", k=128)[k],
                              SLD, waits=[self.xa_guard]))
        if "XAD" in self.dbg:
            self.XAD = self.dram("XAD", [NCH, 128, 2048])
            for k in range(NCH):
                P.dma("gpsimd", self.XAD[k], B.XA[k][:], GST,
                      waits=[(SLD, t_w1[k])])
        xt_r = Rot([B.TW[0], B.TW[1]])
        lg_t, dec_t, dmap_t, scr_t = B.ST[0], B.ST[1], B.ST[2], B.ST[3]
        b1bc = B.ST[4]
        self.t_b1row = P.dma("sync", b1bc[:],
                             self.l1b1_d.squeeze(0).partition_broadcast(128),
                             SLD, waits=[(B.ACS, B.ACS.n), (B.EV, B.EV.n)])
        stg_r = Rot([B.STG[0], B.STG[1]])
        lg_guard = None
        mk_done = None
        for tt in range(NTT):
            xt, wx = xt_r.slot(tt)
            t_x = P.dma("sync", xt[:],
                        B.XN1[:, :, tt * 128:(tt + 1) * 128]
                        .rearrange("c p q -> p c q"), SLD,
                        waits=wx + [self.xn1d_done])
            if "XTD" in self.dbg:
                if not hasattr(self, "XTD"):
                    self.XTD = self.dram("XTD", [NTT, 128, 1024])
                t_xd = P.dma("gpsimd", self.XTD[tt], xt[:], GST,
                             waits=[(SLD, t_x)])
                xt_r.release(tt, GST, t_xd)
            t_lg = None
            for n in range(4):
                t_mm = None
                for k in range(NCH):
                    t_mm = P.emit(
                        "tensor", lambda e, n=n, k=k, xt=xt:
                        e.matmul(B.PS[n][:, 0:512],
                                 xt[:, k * 128:(k + 1) * 128],
                                 B.XA[k][:, n * 512:(n + 1) * 512],
                                 start=(k == 0), stop=(k == NCH - 1)),
                        waits=[(SLD, t_x), (SLD, t_w1[k])]
                        + (self.bankw(n) if k == 0 else []),
                        inc=(MM if k == NCH - 1 else None))
                t_bv = P.emit("vector", lambda e, n=n: e.scalar_tensor_tensor(
                    lg_t[:, n * 512:(n + 1) * 512], B.PS[n][:, 0:512], 1.0,
                    b1bc[:, n * 512:(n + 1) * 512],
                    op0=ALU.mult, op1=ALU.add),
                    waits=[(MM, t_mm), (SLD, self.t_b1row)]
                    + ([lg_guard] if (lg_guard and n == 0) else []),
                    inc=EV)
                t_lg = P.emit("scalar", lambda e, n=n: e.activation(
                    lg_t[:, n * 512:(n + 1) * 512],
                    lg_t[:, n * 512:(n + 1) * 512], AF.Silu),
                    waits=[(EV, t_bv)], inc=ACS)
                self.bank_free[n] = (EV, t_bv)
                if n == 3:
                    xt_r.release(tt, MM, t_mm)
            t_dump = None
            if "LGD" in self.dbg:
                if not hasattr(self, "LGD"):
                    self.LGD = self.dram("LGD", [NTT, 128, 2048])
                t_dump = P.dma("gpsimd", self.LGD[tt], lg_t[:], GST,
                               waits=[(ACS, t_lg)])
            P.wait("vector", [(ACS, t_lg)])
            P.emit("vector", lambda e: e.tensor_scalar(
                dec_t[:], lg_t[:], 0.0, None, op0=ALU.is_gt))
            P.emit("vector", lambda e: e.tensor_scalar(
                scr_t[:], dec_t[:], -1.0, 1.0, op0=ALU.mult, op1=ALU.add))
            P.emit("vector", lambda e: e.memset(dmap_t[:], 0.0))
            dmap3 = dmap_t[:, 0:2040].rearrange("p (t n) -> p t n", n=255)
            dec3 = dec_t[:, 0:2040].rearrange("p (t n) -> p t n", n=255)
            decc3 = scr_t[:, 0:2040].rearrange("p (t n) -> p t n", n=255)
            for tr in range(8):
                P.emit("vector", lambda e, tr=tr: e.memset(
                    dmap_t[:, tr * 255:tr * 255 + 1], 1.0))
            B.spacers()
            for d in range(7):
                lo, hi = 2 ** d - 1, 2 ** (d + 1) - 1
                mcur = dmap3[:, :, lo:hi]
                lvl = dmap3[:, :, hi:hi + 2 ** (d + 1)] \
                    .rearrange("p t (n two) -> p t n two", two=2)
                P.emit("vector", lambda e, lvl=lvl, mcur=mcur, lo=lo, hi=hi:
                       e.tensor_tensor(lvl[:, :, :, 1], mcur,
                                       dec3[:, :, lo:hi], op=ALU.mult))
                P.emit("vector", lambda e, lvl=lvl, mcur=mcur, lo=lo, hi=hi:
                       e.tensor_tensor(lvl[:, :, :, 0], mcur,
                                       decc3[:, :, lo:hi], op=ALU.mult))
                B.spacers(6)
            t_mk = P.emit("vector", lambda e: e.tensor_tensor(
                lg_t[:], lg_t[:], dmap_t[:], op=ALU.mult),
                waits=([(GST, t_dump)] if t_dump else []), inc=EV)
            if "DMAPD" in self.dbg:
                if not hasattr(self, "DMAPD"):
                    self.DMAPD = self.dram("DMAPD", [NTT, 128, 2048])
                t_dump2 = P.dma("gpsimd", self.DMAPD[tt], dmap_t[:], GST,
                                waits=[(EV, t_mk)])
                P.wait("vector", [(GST, t_dump2)])
            for g4 in range(4):
                bi = 4 + (g4 % 2)
                t_mm2 = None
                for j in range(4):
                    kc = g4 * 4 + j
                    t_mm2 = P.emit(
                        "tensor", lambda e, kc=kc, j=j, bi=bi:
                        e.transpose(B.PS[bi][:, j * 128:(j + 1) * 128],
                                    lg_t[:, kc * 128:(kc + 1) * 128],
                                    B.ident[:]),
                        waits=[(EV, t_mk)] + (self.bankw(bi) if j == 0
                                              else []),
                        inc=(MM if j == 3 else None))
                stg, ws = stg_r.slot(tt * 4 + g4)
                t_e = P.emit("scalar", lambda e, stg=stg, bi=bi:
                             e.copy(stg[:, 0:512], B.PS[bi][:]),
                             waits=[(MM, t_mm2)] + ws, inc=ACS)
                self.bank_free[bi] = (ACS, t_e)
                t_st = P.dma("gpsimd",
                             B.MSKT[tt, g4 * 4:g4 * 4 + 4]
                             .rearrange("a p q -> p a q"),
                             stg[:, 0:512].rearrange("p (a q) -> p a q", q=128),
                             GST, waits=[(ACS, t_e)])
                stg_r.release(tt * 4 + g4, GST, t_st)
                mk_done = t_st
            lg_guard = (MM, MM.n)
        part1_mm = (MM, MM.n)
        t_w2 = []
        for k in range(NCH * 2):
            t_w2.append(P.dma(
                "sync", B.XA[k // 2][:, (k % 2) * 1024:(k % 2) * 1024 + 1024],
                self.l1w2t_d.rearrange("(kc k) m -> kc k m", k=128)[k],
                SLD, waits=[part1_mm, (GST, mk_done)]))
        msk_r = Rot([B.ST[0], B.ST[1], B.ST[2], B.ST[3]])
        res_r = Rot([B.TW[0], B.TW[1]])
        stg2_r = Rot([B.STG[0][:, 0:512], B.STG[0][:, 512:1024],
                      B.STG[1][:, 0:512], B.STG[1][:, 512:1024]])
        si = 0
        for tg in range(4):
            t_mks = []
            for kq in range(4):
                mk_t, wm = msk_r.slot(kq)
                for kk in range(4):
                    k = kq * 4 + kk
                    t_mks.append(P.dma(
                        "sync",
                        mk_t[:, kk * 512:(kk + 1) * 512]
                        .rearrange("p (a q) -> p a q", q=128),
                        B.MSKT[tg * 4:(tg + 1) * 4, k]
                        .rearrange("a p q -> p a q"), B.VST,
                        waits=(wm if kk == 0 else []) + [(GST, mk_done)]))
            for m in range(NCH):
                bi = 6 + (m % 2)
                t_mm = None
                for k in range(16):
                    mk_t, _ = msk_r.slot(k // 4)
                    t_mm = P.emit(
                        "tensor",
                        lambda e, k=k, m=m, bi=bi, mk_t=mk_t,
                        s=(k == 0), t=(k == 15):
                        e.matmul(B.PS[bi][:],
                                 B.XA[k // 2][:, (k % 2) * 1024 + m * 128:
                                              (k % 2) * 1024 + m * 128 + 128],
                                 mk_t[:, (k % 4) * 512:(k % 4) * 512 + 512],
                                 start=s, stop=t),
                        waits=[(B.VST, t_mks[-1]), (SLD, t_w2[-1])]
                        + (self.bankw(bi) if k == 0 else []),
                        inc=(MM if k == 15 else None))
                stg, ws = stg2_r.slot(si)
                t_res = P.dma("sync", stg,
                              B.RES[m][:, tg * 512:(tg + 1) * 512], SLD,
                              waits=ws + [self.res_attn1_done,
                                          (GST, mk_done)])
                t_a = P.emit("vector", lambda e, bi=bi, stg=stg:
                             e.scalar_tensor_tensor(
                                 stg, B.PS[bi][:, 0:512], 1.0, stg,
                                 op0=ALU.mult, op1=ALU.add),
                             waits=[(MM, t_mm), (SLD, t_res)], inc=EV)
                self.bank_free[bi] = (EV, t_a)
                t_st = P.dma("gpsimd", B.RES[m][:, tg * 512:(tg + 1) * 512],
                             stg, GST, waits=[(EV, t_a)])
                stg2_r.release(si, GST, t_st)
                si += 1
            for kq in range(4):
                msk_r.release(kq, MM, MM.n)
        self.xa_guard = (MM, MM.n)
        self.final_done = (GST, GST.n)

    def phase_U(self):
        B, P = self, self.P
        SLD, GST, MM, EV, ACS = B.SLD, B.GST, B.MM, B.EV, B.ACS
        t_lds = []
        for j in range(NCH):
            t_lds.append(P.dma("sync", B.XA[j][:], B.RES[j], SLD,
                               waits=[self.xa_guard, self.final_done]))
        stg_r = Rot([B.STG[0], B.STG[1]])
        for tt in range(NTT):
            cs = slice(tt * 128, tt * 128 + 128)
            t_mm = None
            for j in range(NCH):
                bi = j // 4
                ps = B.PS[bi][:, (j % 4) * 128:(j % 4) * 128 + 128]
                t_mm = P.emit("tensor", lambda e, j=j, ps=ps, cs=cs:
                              e.transpose(ps, B.XA[j][:, cs], B.ident[:]),
                              waits=[(SLD, t_lds[j])]
                              + (self.bankw(bi) if j % 4 == 0 else []),
                              inc=MM)
            stg, ws = stg_r.slot(tt)
            t_e = None
            for bi in range(2):
                t_e = P.emit("scalar", lambda e, bi=bi, stg=stg:
                             e.copy(stg[:, bi * 512:bi * 512 + 512],
                                    B.PS[bi][:]),
                             waits=[(MM, t_mm)] + ws, inc=ACS)
                self.bank_free[bi] = (ACS, t_e)
            t_st = P.dma("gpsimd", self.out_d[tt], stg[:], GST,
                         waits=[(ACS, t_e)])
            stg_r.release(tt, GST, t_st)
        P.wait("gpsimd", [(GST, GST.n)])

    def finalize_debug(self):
        B, P = self, self.P
        P.wait("gpsimd", [(B.GST, B.GST.n), (B.SLD, B.SLD.n)])
        t = P.dma("gpsimd", B.out_d[0], B.TW[0][:], B.GST)
        P.wait("gpsimd", [(B.GST, t)])
        self.assemble()

    def assemble(self):
        nc, P = self.nc, self.P
        with nc.Block() as block:
            @block.tensor
            def _(e):
                for f in P.stream["tensor"]:
                    f(e)

            @block.vector
            def _(e):
                for f in P.stream["vector"]:
                    f(e)

            @block.scalar
            def _(e):
                for f in P.stream["scalar"]:
                    f(e)

            @block.gpsimd
            def _(e):
                for f in P.stream["gpsimd"]:
                    f(e)

            @block.sync
            def _(e):
                for f in P.stream["sync"]:
                    f(e)
        self.es.close()
        return nc


def make_inmaps(inputs):
    inp = {k: np.ascontiguousarray(np.asarray(v, dtype=np.float32))
           for k, v in inputs.items()}
    shared = {}
    shared["ident"] = np.eye(128, dtype=np.float32)
    shared["l0a_w_in"] = inp["L0_attn_W_in"]
    shared["l0a_gates_w"] = np.stack([inp["L0_attn_Wa"], inp["L0_attn_Wb"],
                                      inp["L0_attn_Wc"], inp["L0_attn_Wd"]])
    shared["l0a_w_out"] = inp["L0_attn_W_out"]
    shared["l0a_b_in"] = np.ascontiguousarray(
        inp["L0_attn_b_in"].reshape(16, 128).T)

    def vec(v):
        return np.ascontiguousarray(np.asarray(v, np.float32).reshape(NCH, 128).T)
    sv = np.zeros((16, 128, NCH), np.float32)
    sv[0] = vec(inp["L0_attn_ln_g"])
    sv[1] = vec(inp["L0_attn_ln_b"])
    sv[2] = vec(inp["L0_ffn_ln1_g"])
    sv[3] = vec(inp["L0_ffn_ln1_b"])
    sv[4] = vec(inp["L0_attn_a0"] + inp["L0_attn_ba"])
    sv[5] = vec(inp["L0_attn_b0"] + inp["L0_attn_bb"])
    sv[6] = vec(inp["L0_attn_c0"] + inp["L0_attn_bc"])
    sv[7] = vec(inp["L0_attn_d0"] + inp["L0_attn_bd"])
    sv[8] = vec(inp["L0_attn_b_out"])
    sv[9] = vec(inp["L0_attn_conv_b"])
    sv[10] = vec(inp["L0_ffn_ln2_g"])
    sv[11] = vec(inp["L0_ffn_ln2_b"])
    shared["l0a_smallv"] = sv
    shared["l0a_convk"] = np.ascontiguousarray(
        inp["L0_attn_conv_k"].reshape(5, NCH, 128).transpose(2, 1, 0))
    shared["l0f_wq"] = inp["L0_ffn_wq"]
    shared["l0f_wk"] = inp["L0_ffn_wk"]
    shared["l0f_wv"] = inp["L0_ffn_wv"]
    shared["l0f_wo"] = inp["L0_ffn_wo"]
    shared["l0f_wp"] = inp["L0_ffn_wp"]
    shared["l0f_selw"] = inp["L0_ffn_sel_w"]
    shared["l0f_keys"] = inp["L0_ffn_keys"]
    shared["l0f_values"] = inp["L0_ffn_values"]
    shared["l1_wq"] = np.ascontiguousarray(inp["L1_attn_w_aq"].reshape(D, D))
    shared["l1_wk"] = np.ascontiguousarray(inp["L1_attn_w_ak"].reshape(D, D))
    shared["l1_wv"] = np.ascontiguousarray(inp["L1_attn_w_av"].reshape(D, D))
    shared["l1_wo"] = np.ascontiguousarray(inp["L1_attn_w_ao"].reshape(D, D))
    w1t = np.zeros((D, 2048), np.float32)
    w1t[:, 0:2040] = inp["L1_ffn_W_in"].T
    shared["l1_w1t"] = w1t
    b1 = np.zeros((1, 2048), np.float32)
    b1[0, 0:2040] = inp["L1_ffn_b_in"]
    shared["l1_b1"] = b1
    w2t = np.zeros((2048, D), np.float32)
    w2t[0:2040, :] = inp["L1_ffn_W_out"].T
    shared["l1_w2t"] = w2t
    rr = np.arange(128)[:, None]
    cc = np.arange(896)[None, :]
    shared["l1_mask"] = np.where(cc > rr + 384, np.float32(-1e30),
                                 np.float32(0.0)).astype(np.float32)
    shared["ones_r"] = np.ones((1, 128), np.float32)

    in_maps = []
    for b in range(8):
        m = dict(shared)
        m["x_tm"] = np.ascontiguousarray(
            inp["L0_attn_x"][b].reshape(NTT, 128, D))
        in_maps.append(m)
    return in_maps


import os


def build_full(dbg=(), upto=None):
    upto = upto or os.environ.get("BK_UPTO", "Z")
    B = Builder(dbg=dbg, paranoid=bool(int(os.environ.get("BK_PARANOID", "0"))))
    seq = [("A", B.phase_A), ("C", B.phase_C), ("E", B.phase_DE),
           ("F", B.phase_F), ("G", B.phase_G), ("H", B.phase_H),
           ("I", B.phase_I), ("J", B.phase_J), ("K", B.phase_K),
           ("L", B.phase_L), ("M", B.phase_M), ("N", B.phase_N),
           ("O", B.phase_O), ("P", B.phase_PQ), ("R", B.phase_R),
           ("S", B.phase_S), ("T", B.phase_T), ("U", B.phase_U)]
    done_all = True
    for name, fn in seq:
        fn()
        if name == upto:
            done_all = (name == "U")
            break
    if done_all:
        B.assemble()
    else:
        B.finalize_debug()
    return B.nc


def kernel(**inputs):
    in_maps = make_inmaps(inputs)
    nc = build_full()
    res = run_bass_kernel_spmd(nc, in_maps, core_ids=list(range(8)))
    out = np.stack([np.asarray(r["out"]).reshape(T, D) for r in res.results])
    return out


def kernel_debug(inputs, dbg, cores=(0,), upto=None):
    in_maps = make_inmaps(inputs)
    nc = build_full(dbg=dbg, upto=upto)
    res = run_bass_kernel_spmd(nc, [in_maps[c] for c in cores],
                               core_ids=list(cores))
    return [{k: np.asarray(r[k]) for k in r} for r in res.results]
